# version 16
# baseline (speedup 1.0000x reference)
import sys
sys.path.insert(0, "/opt/trn_rl_repo")
import numpy as np
import concourse.bass as bass
import concourse.mybir as mybir
from concourse.bass_utils import run_bass_kernel_spmd
from contextlib import ExitStack

F32 = mybir.dt.float32
BF16 = mybir.dt.bfloat16
I32 = mybir.dt.int32
AF = mybir.ActivationFunctionType
ALU = mybir.AluOpType

HID = 2048
KCH = 16
NH, HD, NG, NST = 64, 64, 8, 128
INTER = NH * HD
GN = NG * NST
CONVD = INTER + 2 * GN
NCC = CONVD // 128
EPS = 1e-5
T = 2048
NA = T // 512
NMLP = 16384
TB = T // 512
NGF = NMLP // 512

class Sched:
    DMA_Q = {"qsync", "qgps", "qscal"}
    NSLOT = 8
    def __init__(self):
        engs = ["pe", "dve", "act", "pool"]
        qs = ["qsync", "qgps", "qscal"]
        self.items = {k: [] for k in engs + qs}
        self.cnt = {k: 0 for k in engs}
        self.qslot = {q: 0 for q in qs}
        self.qtarget = {}
        self.events = {}
        self.autoctr = 0
        self.pending = {}
        self.semnames = list(engs) + [f"{q}{i}" for q in qs for i in range(self.NSLOT)]
    def add(self, eng, fn, waits=(), signal=None, n_sig=1):
        waits = tuple(w for w in waits if w) + self.pending.pop(eng, ())
        if eng not in self.DMA_Q and signal is None:
            signal = f"__a{self.autoctr}"
            self.autoctr += 1
        if eng in self.DMA_Q:
            if signal:
                slot = self.qslot[eng]
                self.qslot[eng] = (slot + 1) % self.NSLOT
                sn = f"{eng}{slot}"
                fence = self.qtarget.get(sn, 0)
                tgt = fence + 16 * n_sig
                self.qtarget[sn] = tgt
                self.events[signal] = (sn, tgt)
                self.items[eng].append((fn, waits, sn, fence, 16 * n_sig))
            else:
                self.items[eng].append((fn, waits, None, 0, 0))
        else:
            step = n_sig if signal else 0
            self.cnt[eng] += step
            if signal:
                assert signal not in self.events, signal
                self.events[signal] = (eng, self.cnt[eng])
            self.items[eng].append((fn, waits, None, 0, step))
    def emit(self, eng_name, eng_obj, sems):
        is_dma = eng_name in self.DMA_Q
        chain = (not is_dma) and eng_name != "pe"
        cnt_sofar = 0
        for item in self.items[eng_name]:
            fn, waits, sn, fence, step = item
            best = {}
            for w in waits:
                weng, wval = self.events[w]
                if weng == eng_name and not is_dma:
                    if eng_name == "pe":
                        continue
                best[weng] = max(best.get(weng, 0), wval)
            if chain and cnt_sofar > 0:
                best[eng_name] = max(best.get(eng_name, 0), cnt_sofar)
            if is_dma and sn is not None and fence > 0:
                best[sn] = max(best.get(sn, 0), fence)
            cnt_sofar += step
            for weng, wval in best.items():
                eng_obj.wait_ge(sems[weng], wval)
            r = fn(eng_obj)
            if is_dma:
                insts = r if isinstance(r, list) else [r]
                if sn is not None:
                    assert step == 16 * len(insts), (step, len(insts))
                    for i in insts:
                        i.then_inc(sems[sn], 16)
            elif step:
                r.then_inc(sems[eng_name], step)

    def barrier(self, name):
        evs = []
        for eng in ["pe", "dve", "act", "pool"]:
            if self.cnt[eng] > 0:
                self.events[name + "_" + eng] = (eng, self.cnt[eng])
                evs.append(name + "_" + eng)
        for sn, tgt in list(self.qtarget.items()):
            self.events[name + "_" + sn] = (sn, tgt)
            evs.append(name + "_" + sn)
        for k in self.items:
            self.pending[k] = self.pending.get(k, ()) + tuple(evs)

class R:
    def __init__(self, w=None):
        self.w = w
        self.rs = []
    def rd(self, ev):
        waits = [self.w] if self.w else []
        self.rs.append(ev)
        return waits
    def wr(self, ev):
        waits = ([self.w] if self.w else []) + self.rs
        self.w = ev
        self.rs = []
        return waits

def pi128():
    return np.concatenate([np.arange(0, 128, 2), np.arange(1, 128, 2)])

def pi_full(n):
    p = pi128()
    return np.concatenate([c * 128 + p for c in range(n // 128)])

def build():
    nc = bass.Bass("TRN2", target_bir_lowering=False)
    xT_in = nc.dram_tensor("xT_in", [HID, T + 3], F32, kind="ExternalInput").ap()
    xprev_in = nc.dram_tensor("xprev_in", [HID, T + 3], F32, kind="ExternalInput").ap()
    flag_in = nc.dram_tensor("flag_in", [128, 1], F32, kind="ExternalInput").ap()
    wz_in = nc.dram_tensor("wz_in", [HID, INTER], F32, kind="ExternalInput").ap()
    wx_in = nc.dram_tensor("wx_in", [HID, INTER], F32, kind="ExternalInput").ap()
    wb_in = nc.dram_tensor("wb_in", [HID, GN], F32, kind="ExternalInput").ap()
    wc_in = nc.dram_tensor("wc_in", [HID, GN], F32, kind="ExternalInput").ap()
    wdt_in = nc.dram_tensor("wdt_in", [HID, NH], F32, kind="ExternalInput").ap()
    wout_in = nc.dram_tensor("wout_in", [INTER, HID], F32, kind="ExternalInput").ap()
    convw_in = nc.dram_tensor("convw_in", [128, NCC * 4], F32, kind="ExternalInput").ap()
    convb_in = nc.dram_tensor("convb_in", [128, NCC], F32, kind="ExternalInput").ap()
    dtb_in = nc.dram_tensor("dtb_in", [NH, 1], F32, kind="ExternalInput").ap()
    alog_in = nc.dram_tensor("alog_in", [NH, 1], F32, kind="ExternalInput").ap()
    drep_in = nc.dram_tensor("drep_in", [128, 32], F32, kind="ExternalInput").ap()
    nw_in = nc.dram_tensor("nw_in", [128, 32], F32, kind="ExternalInput").ap()
    ident_in = nc.dram_tensor("ident_in", [128, 128], F32, kind="ExternalInput").ap()
    mask_in = nc.dram_tensor("mask_in", [128, 128], BF16, kind="ExternalInput").ap()
    unpi_in = nc.dram_tensor("unpi_in", [128, 128], F32, kind="ExternalInput").ap()
    rep_in = nc.dram_tensor("rep_in", [4, 128], BF16, kind="ExternalInput").ap()
    shc_in = nc.dram_tensor("shc_in", [128, 1], I32, kind="ExternalInput").ap()
    pg_in = nc.dram_tensor("pg_in", [HID // 2, NMLP], I32, kind="ExternalInput").ap()
    pu_in = nc.dram_tensor("pu_in", [HID // 2, NMLP], I32, kind="ExternalInput").ap()
    pd_in = nc.dram_tensor("pd_in", [NMLP // 2, HID], I32, kind="ExternalInput").ap()
    sg_in = nc.dram_tensor("sg_in", [HID // 32, NMLP], F32, kind="ExternalInput").ap()
    su_in = nc.dram_tensor("su_in", [HID // 32, NMLP], F32, kind="ExternalInput").ap()
    sd_in = nc.dram_tensor("sd_in", [NMLP // 32, HID], F32, kind="ExternalInput").ap()
    out_dram = nc.dram_tensor("out", [T, HID], F32, kind="ExternalOutput").ap()

    selc_in = nc.dram_tensor("selc_in", [128, 8], F32, kind="ExternalInput").ap()
    diag_dram = nc.dram_tensor("diag_dram", [NCC, 128, 4, 128], BF16, kind="Internal").ap()
    wallb_dram = nc.dram_tensor("wallb_dram", [81, 128, KCH, 128], BF16, kind="Internal").ap()
    y_dram = nc.dram_tensor("y_dram", [16, 128, 32, 128], BF16, kind="Internal").ap()
    gate_dram = nc.dram_tensor("gate_dram", [4, 128, 32, 512], BF16, kind="Internal").ap()
    mext_dram = nc.dram_tensor("mext_dram", [16, 8, 128, 8, 128], BF16, kind="Internal").ap()
    cexp_dram = nc.dram_tensor("cexp_dram", [16, 64], F32, kind="Internal").ap()
    cc_in = nc.dram_tensor("cc_in", [128, 4096], F32, kind="Internal").ap()
    cc_out = nc.dram_tensor("cc_out", [1024, 4096], F32, kind="Internal", addr_space="Local").ap()
    woutb_dram = nc.dram_tensor("woutb_dram", [KCH, 128, 32, 128], BF16, kind="Internal").ap()
    shuf_dram = nc.dram_tensor("shuf_dram", [NH, 128], F32, kind="Internal").ap()
    attn_dram = nc.dram_tensor("attn_dram", [HID, T], F32, kind="Internal").ap()
    xn_dram = nc.dram_tensor("xn_dram", [HID, T], BF16, kind="Internal").ap()
    acc_dram = nc.dram_tensor("acc_dram", [HID, T], F32, kind="Internal").ap()
    attn_out = attn_dram
    xn_in = xn_dram
    xnr_d = xn_dram.rearrange("(k p) t -> p k t", p=128)
    acc_r = acc_dram.rearrange("(k p) t -> p k t", p=128)

    es = ExitStack()
    ARENA_B = 212824
    arena = es.enter_context(nc.sbuf_tensor("arena", [128, ARENA_B // 2], BF16))

    class Ar:
        def __init__(self):
            self.off = 0
            self.base = 0
        def carve(self, P, freeshape, dt):
            esz = 2 if dt == BF16 else 4
            nb = int(np.prod(freeshape)) * esz
            a = (self.off + 3) // 4 * 4
            self.off = a + nb
            assert self.off <= ARENA_B, ("arena overflow", self.off)
            v = arena[0:P, a // 2:(a + nb) // 2]
            if dt != BF16:
                v = v.bitcast(dt)
            if len(freeshape) == 2:
                v = v.rearrange("p (a b) -> p a b", b=freeshape[1])
            elif len(freeshape) == 3:
                v = v.rearrange("p (a b c) -> p a b c", b=freeshape[1], c=freeshape[2])
            else:
                assert len(freeshape) == 1
            return v
    ar = Ar()
    def sbuf(P, freeshape, dt):
        return ar.carve(P, freeshape, dt)

    ones1 = sbuf(1, [128], F32)
    onescol = sbuf(128, [1], F32)
    onescolb = sbuf(128, [1], BF16)
    ones64 = sbuf(64, [128], F32)
    epscol = sbuf(128, [1], F32)
    flagc = sbuf(128, [1], F32)
    identf = sbuf(128, [128], F32)
    identb = sbuf(128, [128], BF16)
    maskt = sbuf(128, [128], BF16)
    convw = sbuf(128, [NCC * 4], F32)
    convb = sbuf(128, [NCC], F32)
    dtbc = sbuf(NH, [1], F32)
    acol = sbuf(NH, [1], F32)
    drep = sbuf(128, [32], F32)
    nwc = sbuf(128, [32], F32)
    selc = sbuf(128, [8], F32)
    ar.base = ar.off

    S = sbuf(128, [NG, 512], F32)
    Sb = sbuf(128, [512], BF16)
    mk_xc = ar.off
    xc = sbuf(128, [2, 515], F32)
    xnc = sbuf(128, [KCH, 515], BF16)
    xbcp = sbuf(128, [NCC, 515], BF16)
    tailb = sbuf(128, [NCC, 3], BF16)
    convout = xbcp[:, :, 3:515]
    wtile = sbuf(128, [2, KCH, 128], BF16)
    wotile = sbuf(128, [1, 32, 128], BF16)
    dgt = sbuf(128, [2, 4, 128], BF16)
    dstg = wtile[:, 0, 0:4, :]
    xsq = sbuf(128, [2, 512], F32)
    statr = sbuf(1, [5, 512], F32)
    statg = sbuf(1, [2, 128], F32)
    dtp = sbuf(64, [4, 512], F32)
    lc1p = sbuf(1, [1024], F32)
    qtok = sbuf(128, [64], F32)
    wfm = sbuf(64, [128], F32)
    wtok = sbuf(128, [64], F32)
    crow = sbuf(1, [64], F32)
    cexp = sbuf(1, [64], F32)
    qm = sbuf(128, [8, 128], F32)
    ebuf = sbuf(128, [8, 128], BF16)
    eext = sbuf(128, [8, 128], BF16)
    mprime = sbuf(128, [8, 128], BF16)
    mext = sbuf(128, [8, 128], BF16)
    xTt = sbuf(128, [32, 128], BF16)
    bTt = sbuf(128, [8, 128], BF16)
    y_sb = sbuf(128, [32, 128], BF16)
    sgb = sbuf(128, [32, 128], BF16)
    gnb = sbuf(128, [32, 512], BF16)
    normb = gnb
    gate = gnb
    scr = sbuf(128, [512], F32)
    astg = scr

    PST = es.enter_context(nc.psum_tensor("PST", [128, 4096], F32))
    pb = [PST[:, 512 * i:512 * (i + 1)] for i in range(8)]
    def pbf(i):
        return pb[i].bitcast(BF16)

    S3 = S.rearrange("p g x -> p (g x)")
    xT3 = xTt.rearrange("p a b -> p (a b)")
    xTh = xTt.rearrange("p a b -> p (a b)").rearrange("p (h d) -> p h d", d=64)

    prepass = False
    Sq = Sched()

    rWT = [R(), R()]
    rWO = [R(), R()]
    rDG = [R(), R()]
    rPB = [R() for _ in range(8)]
    rXBC = R()
    rCO = rXBC
    rXT = R(); rBT = R(); rQT = R(); rWFM = R(); rWTOK = R(); rCROW = R(); rCEXP = R()
    rQM = R(); rT2M = R(); rE = R(); rEE = R(); rMP = R(); rME = R()
    rYSB = R(); rSGB = R(); rNORM = R(); rS = R(); rSB = R(); rXN = R()
    rLC1 = R(); rSHUF = R(); rSTAT = R(); rDTP = R(); rSCR = R(); rASTG = R(); rXSQ = [R(), R()]
    rXC = [R(), R()]; rTAIL = R(); rGNB = R()

    def c_loads(e):
        return [e.dma_start(out=convw[:], in_=convw_in),
                e.dma_start(out=convb[:], in_=convb_in),
                e.dma_start(out=dtbc[:], in_=dtb_in),
                e.dma_start(out=acol[:], in_=alog_in),
                e.dma_start(out=drep[:], in_=drep_in),
                e.dma_start(out=nwc[:], in_=nw_in),
                e.dma_start(out=identf[:], in_=ident_in),
                e.dma_start(out=maskt[:], in_=mask_in),
                e.dma_start(out=flagc[:], in_=flag_in),
                e.dma_start(out=selc[:], in_=selc_in)]
    Sq.add("qsync", c_loads, signal="consts", n_sig=10)
    CON = ("consts",)
    Sq.add("dve", lambda e: e.memset(ones1[:], 1.0), signal="ones1")
    Sq.add("dve", lambda e: e.memset(onescol[:], 1.0), signal="onescol")
    Sq.add("dve", lambda e: e.memset(onescolb[:], 1.0), signal="onescolb")
    Sq.add("dve", lambda e: e.memset(ones64[:], 1.0), signal="ones64")
    Sq.add("dve", lambda e: e.memset(epscol[:], EPS), signal="epscol")
    Sq.add("dve", lambda e: e.tensor_copy(identb[:], identf[:]), waits=CON, signal="identb")
    Sq.add("act", lambda e: e.activation(out=acol[:], in_=acol[:], func=AF.Exp),
           waits=CON, signal="aexp")
    Sq.add("dve", lambda e: e.tensor_scalar_mul(acol[:], acol[:], -1.0),
           waits=("aexp",), signal="acol")
    Sq.add("dve", lambda e: e.memset(S3, 0.0), signal="sinit")
    rS.w = "sinit"

    for oc in range(KCH):
        def ldw(e, oc=oc):
            return [e.dma_start(out=wotile[:, 0, :, :],
                                in_=wout_in[:, oc*128:(oc+1)*128].rearrange("(k p) c -> p k c", p=128))]
        Sq.add("qgps", ldw, waits=tuple(rWO[0].wr(f"wfld{oc}")) + CON, signal=f"wfld{oc}", n_sig=1)
        def fold(e):
            return e.tensor_tensor(wotile[:, 0, :, :], wotile[:, 0, :, :],
                                   nwc[:].unsqueeze(2).broadcast_to([128, 32, 128]), ALU.mult)
        Sq.add("dve", fold, waits=tuple(rWO[0].wr(f"wfoldc{oc}")) + CON,
               signal=f"wfoldc{oc}")
        Sq.add("qgps", lambda e, oc=oc: [e.dma_start(out=woutb_dram[oc], in_=wotile[:, 0, :, :])],
               waits=tuple(rWO[0].rd(f"wfold{oc}")), signal=f"wfold{oc}", n_sig=1)
    WOUT_READY = (f"wfold{KCH-1}",)
    rDSTG = R()
    for cc in range(NCC):
        def bld(e, cc=cc):
            last = None
            for k in range(4):
                last = e.tensor_scalar(dstg[:, k, :], identb[:], convw[:, cc*4+k:cc*4+k+1],
                                       None, ALU.mult)
            return last
        Sq.add("dve", bld, waits=("identb",) + CON + tuple(rWT[0].wr(f"dgb{cc}")),
               signal=f"dgb{cc}")
        Sq.add("qgps", lambda e, cc=cc: [e.dma_start(out=diag_dram[cc], in_=dstg)],
               waits=tuple(rWT[0].rd(f"dgst{cc}")), signal=f"dgst{cc}", n_sig=1)
    DIAG_READY = (f"dgst{NCC-1}",)

    def wsrc(oc):
        if oc < 32: return wx_in, oc * 128, 128
        if oc < 40: return wb_in, (oc - 32) * 128, 128
        if oc < 48: return wc_in, (oc - 40) * 128, 128
        if oc < 80: return wz_in, (oc - 48) * 128, 128
        return wdt_in, 0, 64

    rFT = [R(), R()]
    for oc in range(81):
        fsrc, fco, fwidth = wsrc(oc)
        fbuf = oc % 2
        def fld(e, fsrc=fsrc, fco=fco, fwidth=fwidth, fbuf=fbuf):
            return [e.dma_start(out=y_sb[:, 16*fbuf:16*fbuf+16, 0:fwidth],
                                in_=fsrc[:, fco:fco+fwidth].rearrange("(k p) c -> p k c", p=128))]
        Sq.add("qgps", fld, waits=tuple(rFT[fbuf].wr(f"wfl{oc}")), signal=f"wfl{oc}", n_sig=1)
        def fst(e, oc=oc, fwidth=fwidth, fbuf=fbuf):
            return [e.dma_start(out=wallb_dram[oc][:, :, 0:fwidth],
                                in_=y_sb[:, 16*fbuf:16*fbuf+16, 0:fwidth])]
        Sq.add("qsync", fst, waits=(f"wfl{oc}",), signal=f"wfo{oc}", n_sig=1)
        rFT[fbuf].rd(f"wfo{oc}")
    rYSB.w = "wfo80"

    passes = ([("p", xprev_in, True)] if prepass else []) + [("m", xT_in, False)]
    steps = [(pfx, src, so, ia) for (pfx, src, so) in passes for ia in range(NA)]
    for (PFX, xsrc, SO, ia) in steps:
        A = PFX + f"a{ia}"
        ofs = 3 + 512 * ia
        NCC_P = 40 if SO else NCC
        if PFX == "m" and ia == 0 and prepass:
            Sq.add("dve", lambda e: e.tensor_scalar(S3, S3, flagc[:], None, ALU.mult),
                   waits=tuple(rS.wr("flagS")) + CON, signal="flagS")

        widths = ([(0, 3)] if ia == 0 else []) + [(3, 515)]
        for (c0, c1) in widths:
            W = c1 - c0
            tw = A + f"w{c0}"
            for k in range(KCH):
                si = k % 2
                def xld(e, k=k, si=si, c0=c0, c1=c1, ia=ia, xsrc=xsrc):
                    lo = 512 * ia + c0
                    return [e.dma_start(out=xc[:, si, 0:c1-c0],
                                        in_=xsrc[128*k:128*(k+1), lo:lo+(c1-c0)])]
                Sq.add("qsync", xld, waits=tuple(rXC[si].wr(tw + f"xl{k}")) + CON,
                       signal=tw + f"xl{k}", n_sig=1)
                Sq.add("act", lambda e, si=si, W=W: e.activation(
                        out=xsq[:, si, 0:W], in_=xc[:, si, 0:W], func=AF.Square),
                       waits=(tw + f"xl{k}",) + tuple(rXSQ[si].wr(tw + f"sq{k}")),
                       signal=tw + f"sq{k}")
                rXC[si].rd(tw + f"sq{k}")
                Sq.add("pe", lambda e, si=si, k=k, W=W: e.matmul(
                        pb[0][0:1, 0:W], onescol[:], xc[:, si, 0:W],
                        start=(k == 0), stop=(k == KCH - 1)),
                       waits=(tw + f"xl{k}", "onescol") +
                             (tuple(rPB[0].wr(tw + "smm")) if k == 0 else ()),
                       signal=(tw + "smm") if k == KCH - 1 else tw + f"sm{k}")
                rXC[si].rd(tw + ("smm" if k == KCH - 1 else f"sm{k}"))
                Sq.add("pe", lambda e, si=si, k=k, W=W: e.matmul(
                        pb[1][0:1, 0:W], onescol[:], xsq[:, si, 0:W],
                        start=(k == 0), stop=(k == KCH - 1)),
                       waits=(tw + f"sq{k}",) + (tuple(rPB[1].wr(tw + "qmm")) if k == 0 else ()),
                       signal=(tw + "qmm") if k == KCH - 1 else tw + f"qm{k}")
                rXSQ[si].rd(tw + ("qmm" if k == KCH - 1 else f"qm{k}"))
            st = statr[0:1]
            Sq.add("dve", lambda e, st=st, W=W: e.tensor_scalar_mul(
                    st[:, 0, 0:W], pb[0][0:1, 0:W], 1.0 / HID),
                   waits=(tw + "smm",) + tuple(rSTAT.wr(tw + "st1")), signal=tw + "st1")
            rPB[0].rd(tw + "st1")
            Sq.add("dve", lambda e, st=st, W=W: e.tensor_scalar_mul(
                    st[:, 1, 0:W], pb[1][0:1, 0:W], 1.0 / HID),
                   waits=(tw + "qmm",), signal=tw + "st2")
            rPB[1].rd(tw + "st2")
            Sq.add("dve", lambda e, st=st, W=W: e.tensor_tensor(
                    st[:, 2, 0:W], st[:, 0, 0:W], st[:, 0, 0:W], ALU.mult))
            Sq.add("dve", lambda e, st=st, W=W: e.tensor_tensor(
                    st[:, 2, 0:W], st[:, 1, 0:W], st[:, 2, 0:W], ALU.subtract),
                   signal=tw + "var")
            Sq.add("act", lambda e, st=st, W=W: e.activation(
                    out=st[:, 3, 0:W], in_=st[:, 2, 0:W], func=AF.Sqrt, bias=epscol[0:1]),
                   waits=(tw + "var", "epscol"), signal=tw + "sqr")
            Sq.add("dve", lambda e, st=st, W=W: e.reciprocal(st[:, 3, 0:W], st[:, 3, 0:W]),
                   waits=(tw + "sqr",))
            Sq.add("dve", lambda e, st=st, W=W: e.tensor_tensor(
                    st[:, 4, 0:W], st[:, 0, 0:W], st[:, 3, 0:W], ALU.mult),
                   signal=tw + "mr")
            rSTAT.rd(tw + "mr")
            Sq.add("pe", lambda e, st=st, W=W: e.matmul(
                    pb[2][:, 0:W], ones1[:], st[:, 3, 0:W], start=True, stop=True,
                    skip_group_check=True),
                   waits=(tw + "mr", "ones1") + tuple(rPB[2].wr(tw + "bca")),
                   signal=tw + "bca")
            rSTAT.rd(tw + "bca")
            Sq.add("pe", lambda e, st=st, W=W: e.matmul(
                    pb[3][:, 0:W], ones1[:], st[:, 4, 0:W], start=True, stop=True,
                    skip_group_check=True),
                   waits=(tw + "mr",) + tuple(rPB[3].wr(tw + "bcb")),
                   signal=tw + "bcb")
            rSTAT.rd(tw + "bcb")
            for k in range(KCH):
                si = k % 2
                def xld2(e, k=k, si=si, c0=c0, c1=c1, ia=ia, xsrc=xsrc):
                    lo = 512 * ia + c0
                    return [e.dma_start(out=xc[:, si, 0:c1-c0],
                                        in_=xsrc[128*k:128*(k+1), lo:lo+(c1-c0)])]
                Sq.add("qsync", xld2, waits=tuple(rXC[si].wr(tw + f"x2l{k}")),
                       signal=tw + f"x2l{k}", n_sig=1)
                Sq.add("dve", lambda e, si=si, c0=c0, c1=c1, W=W: e.tensor_tensor(
                        xc[:, si, 0:W], xc[:, si, 0:W],
                        pb[2][:, 0:W], ALU.mult),
                       waits=(tw + f"x2l{k}", tw + "bca") + tuple(rXC[si].wr(tw + f"xs{k}")),
                       signal=tw + f"xs{k}")
                rPB[2].rd(tw + f"xs{k}")
                Sq.add("dve", lambda e, si=si, k=k, c0=c0, c1=c1, W=W: e.tensor_tensor(
                        xnc[:, k, c0:c1], xc[:, si, 0:W],
                        pb[3][:, 0:W], ALU.subtract),
                       waits=(tw + "bcb",) + (tuple(rXN.wr(tw + "xn0")) if k == 0 else ()),
                       signal=(tw + "xn") if k == KCH - 1 else tw + f"xn{k}")
                rXC[si].rd(tw + ("xn" if k == KCH - 1 else f"xn{k}"))
                rPB[3].rd(tw + ("xn" if k == KCH - 1 else f"xn{k}"))
                XN = (A + "w3xn",) + ((A + "w0xn",) if ia == 0 else ())

        if not SO:
            Sq.add("qgps", lambda e, ia=ia: [e.dma_start(
                    out=xnr_d[:, :, ia*512:(ia+1)*512], in_=xnc[:, :, 3:515])],
                   waits=XN, signal=A + "xnst", n_sig=1)
            rXN.rd(A + "xnst")
        if ia > 0:
            Sq.add("dve", lambda e, NCC_P=NCC_P: e.tensor_copy(
                    xbcp[:, 0:NCC_P, 0:3], tailb[:, 0:NCC_P, :]),
                   waits=tuple(rXBC.wr(A + "tail")) + tuple(rTAIL.rd(A + "tail")),
                   signal=A + "tail")
            halo_done = (A + "tail",)

        def proj(oc, tag, n_cols, out_fn, evac_engine, evac_extra_waits=()):
            src_, co, width = wsrc(oc)
            buf = proj.bufctr % 2
            proj.bufctr += 1
            pi = 4 + (proj.psctr % 2)
            proj.psctr += 1
            tg = A + tag
            def ldw(e, oc=oc, width=width, buf=buf):
                return [e.dma_start(out=wtile[:, buf, 0:KCH//2, 0:width],
                                    in_=wallb_dram[oc][:, 0:KCH//2, 0:width])]
            def ldw2(e, oc=oc, width=width, buf=buf):
                return [e.dma_start(out=wtile[:, buf, KCH//2:KCH, 0:width],
                                    in_=wallb_dram[oc][:, KCH//2:KCH, 0:width])]
            Sq.add("qgps", ldw, waits=(f"wfo{oc}",) + tuple(rWT[buf].wr(tg + "ld")),
                   signal=tg + "ld", n_sig=1)
            Sq.add("qsync", ldw2, waits=(), signal=tg + "ldb", n_sig=1)
            def mms(e, width=width, buf=buf, pi=pi, n_cols=n_cols):
                last = None
                for k in range(KCH):
                    last = e.matmul(pb[pi][0:width, 0:n_cols], wtile[:, buf, k, 0:width],
                                    xnc[:, k, 515-n_cols:515] if n_cols == 512 else xnc[:, k, 0:3],
                                    start=(k == 0), stop=(k == KCH - 1))
                return last
            Sq.add("pe", mms, waits=(tg + "ld", tg + "ldb") + XN + tuple(rPB[pi].wr(tg + "mm")),
                   signal=tg + "mm")
            rWT[buf].rd(tg + "mm")
            rXN.rd(tg + "mm")
            ew = (tg + "mm",) + evac_extra_waits + tuple(rPB[pi].rd(tg + "ev"))
            Sq.add(evac_engine, lambda e, pi=pi: out_fn(e, pb[pi]), waits=ew, signal=tg + "ev")
            return tg
        proj.bufctr = 0
        proj.psctr = 0
        if ia > 0:
            rPB[4].w = rPB[4].w

        for oc in range(NCC_P):
            def ev(e, ps, oc=oc):
                return e.activation(out=xbcp[:, oc, 3:515], in_=ps[:], func=AF.Copy)
            tg = proj(oc, f"px{oc}", 512, ev, "act",
                      evac_extra_waits=tuple(rXBC.wr(A + f"px{oc}ev")) +
                                       ((A + "tail",) if ia > 0 else ()))
        if ia == 0:
            for oc in range(NCC_P):
                def ev(e, ps, oc=oc):
                    return e.activation(out=xbcp[:, oc, 0:3], in_=ps[:, 0:3], func=AF.Copy)
                proj(oc, f"ph{oc}", 3, ev, "act")
            halo_done = (A + f"ph{NCC_P-1}ev",)
        HALO = halo_done

        Sq.add("dve", lambda e, NCC_P=NCC_P: e.tensor_copy(
                tailb[:, 0:NCC_P, :], xbcp[:, 0:NCC_P, 512:515]),
               waits=(A + f"px{NCC_P-1}ev",) + HALO + tuple(rTAIL.wr(A + "tsv")) +
                     tuple(rXBC.rd(A + "tsv")),
               signal=A + "tsv")
        for cc in range(NCC_P):
            tg = A + f"cv{cc}"
            buf = cc % 2
            Sq.add("qsync", lambda e, cc=cc, buf=buf: [e.dma_start(out=dgt[:, buf, :, :], in_=diag_dram[cc])],
                   waits=DIAG_READY + tuple(rDG[buf].wr(tg + "ld")), signal=tg + "ld", n_sig=1)
            def cmm(e, cc=cc, buf=buf):
                last = None
                for k in range(4):
                    last = e.matmul(pb[6][:], dgt[:, buf, k, :], xbcp[:, cc, k:k+512],
                                    start=(k == 0), stop=(k == 3))
                return last
            Sq.add("pe", cmm, waits=(tg + "ld", A + f"px{cc}ev") + HALO +
                             tuple(rPB[6].wr(tg + "mm")), signal=tg + "mm")
            rDG[buf].rd(tg + "mm")
            rXBC.rd(tg + "mm")
            Sq.add("act", lambda e, cc=cc: e.activation(
                    out=convout[:, cc, :], in_=pb[6][:], func=AF.Silu, bias=convb[:, cc:cc+1]),
                   waits=(tg + "mm", A + "tsv") + tuple(rPB[6].rd(tg + "ev")) +
                         tuple(rCO.wr(tg + "ev")),
                   signal=tg + "ev")
        CONVDONE = (A + f"cv{NCC_P-1}ev",)

        if not SO:
            for oc in range(48, 80):
                def ev(e, ps, oc=oc):
                    return e.activation(out=gate[:, oc - 48, :], in_=ps[:], func=AF.Copy)
                proj(oc, f"pz{oc}", 512, ev, "act",
                     evac_extra_waits=tuple(rGNB.wr(A + f"pz{oc}ev")))
            GATE_RDY = (A + "pz79ev",)
            Sq.add("qgps", lambda e, ia=ia: [e.dma_start(out=gate_dram[ia], in_=gnb[:])],
                   waits=GATE_RDY, signal=A + "gst", n_sig=1)
            rGNB.rd(A + "gst")
        def evdt(e, ps):
            return e.tensor_scalar(dtp[:, 0, :], ps[0:64, :], dtbc[:], None, ALU.add)
        proj(80, "pdt", 512, evdt, "dve",
             evac_extra_waits=tuple(rDTP.wr(A + "pdtev")) + CON)

        Sq.add("act", lambda e: e.activation(out=dtp[:, 1, :], in_=dtp[:, 0, :], func=AF.Abs),
               waits=(A + "pdtev",), signal=A + "dtab")
        Sq.add("act", lambda e: e.activation(out=dtp[:, 2, :], in_=dtp[:, 1, :], func=AF.Exp, scale=-1.0),
               waits=(A + "dtab",), signal=A + "dte")
        Sq.add("act", lambda e: e.activation(out=dtp[:, 2, :], in_=dtp[:, 2, :], func=AF.Ln, bias=1.0),
               signal=A + "dtl")
        Sq.add("dve", lambda e: e.tensor_scalar(dtp[:, 1, :], dtp[:, 0, :], 0.0, None, ALU.max),
               waits=(A + "dte",), signal=A + "dtr")
        Sq.add("dve", lambda e: e.tensor_tensor(dtp[:, 3, :], dtp[:, 1, :], dtp[:, 2, :], ALU.add),
               waits=(A + "dtl",), signal=A + "dtv")
        Sq.add("act", lambda e: e.activation(out=dtp[:, 0, :], in_=dtp[:, 3, :], func=AF.Ln),
               waits=(A + "dtv",), signal=A + "dtln")
        Sq.add("dve", lambda e: e.tensor_scalar(dtp[:, 1, :], dtp[:, 3, :], acol[:], None, ALU.mult),
               waits=(A + "dtv", "acol"), signal=A + "dta")
        for il in range(4):
            Sq.add("dve", lambda e, il=il: e.tensor_tensor_scan(
                    dtp[:, 2, il*128:(il+1)*128], ones64[:], dtp[:, 1, il*128:(il+1)*128],
                    0.0, ALU.mult, ALU.add),
                   waits=(A + "dta", "ones64"), signal=A + f"lcs{il}")
        Sq.add("dve", lambda e: e.tensor_tensor(dtp[:, 3, :], dtp[:, 0, :], dtp[:, 2, :], ALU.subtract),
               waits=(A + "dtln", A + "lcs3"), signal=A + "q")
        rDTP.rd(A + "q")

        for il in range(4):
            L = A + f"l{il}"
            lc0 = il * 128
            for w in range(10):
                is_b = w >= 8
                def tmm(e, w=w, lc0=lc0, is_b=is_b):
                    last = None
                    for j in range(4):
                        cc = (w * 4 + j) if not is_b else (32 + (w - 8) * 4 + j)
                        last = e.transpose(pbf(7)[:, j*128:(j+1)*128],
                                           convout[:, cc, lc0:lc0+128], identb[:])
                    return last
                Sq.add("pe", tmm, waits=CONVDONE + ("identb",) + tuple(rPB[7].wr(L + f"tp{w}")),
                       signal=L + f"tp{w}")
                rCO.rd(L + f"tp{w}")
                if not is_b:
                    Sq.add("dve", lambda e, w=w: e.tensor_copy(xTt[:, w*4:(w+1)*4, :],
                            pbf(7)[:, 0:512].rearrange("p (a b) -> p a b", b=128)),
                           waits=(L + f"tp{w}",) + (tuple(rXT.wr(L + "tpe0")) if w == 0 else ()) +
                                 tuple(rPB[7].rd(L + f"tpe{w}")),
                           signal=L + f"tpe{w}")
                else:
                    Sq.add("dve", lambda e, w=w: e.tensor_copy(bTt[:, (w-8)*4:(w-7)*4, :],
                            pbf(7)[:, 0:512].rearrange("p (a b) -> p a b", b=128)),
                           waits=(L + f"tp{w}",) + (tuple(rBT.wr(L + "tpe8")) if w == 8 else ()) +
                                 tuple(rPB[7].rd(L + f"tpe{w}")),
                           signal=L + f"tpe{w}")
            if not SO:
                Sq.add("pe", lambda e, lc0=lc0: e.transpose(pb[7][:, 0:64],
                        dtp[:, 3, lc0:lc0+128], identf[0:64, 0:64]),
                       waits=(A + "q",) + tuple(rPB[7].wr(L + "qtm")), signal=L + "qtm")
                rDTP.rd(L + "qtm")
                Sq.add("dve", lambda e: e.tensor_copy(qtok[:], pb[7][:, 0:64]),
                       waits=(L + "qtm",) + tuple(rQT.wr(L + "qte")) + tuple(rPB[7].rd(L + "qte")),
                       signal=L + "qte")
                Sq.add("qgps", lambda e, lc0=lc0: [e.dma_start(out=shuf_dram, in_=dtp[:, 2, lc0:lc0+128])],
                       waits=(A + "q",) + tuple(rSHUF.wr(L + "shst")), signal=L + "shst", n_sig=1)
                rDTP.rd(L + "shst")
            Sq.add("pe", lambda e, lc0=lc0: e.transpose(pb[7][0:1, 0:64],
                    dtp[:, 2, lc0+127:lc0+128], identf[0:64, 0:64]),
                   waits=(A + "q",) + tuple(rPB[7].wr(L + "crm")), signal=L + "crm")
            rDTP.rd(L + "crm")
            Sq.add("dve", lambda e: e.tensor_copy(crow[:], pb[7][0:1, 0:64]),
                   waits=(L + "crm",) + tuple(rCROW.wr(L + "cre")) + tuple(rPB[7].rd(L + "cre")),
                   signal=L + "cre")
            Sq.add("act", lambda e: e.activation(out=cexp[:], in_=crow[:], func=AF.Exp),
                   waits=(L + "cre",) + tuple(rCEXP.wr(L + "cex")), signal=L + "cex")
            rCROW.rd(L + "cex")
            if not SO:
                Sq.add("qgps", lambda e, l4=4*ia+il: [e.dma_start(
                        out=cexp_dram[l4:l4+1, :], in_=cexp[:])],
                       waits=(L + "cex",), signal=L + "cxst", n_sig=1)
                rCEXP.rd(L + "cxst")
            Sq.add("pe", lambda e: e.matmul(pb[7][:, 0:64], ones1[:], cexp[:],
                    start=True, stop=True, skip_group_check=True),
                   waits=(L + "cex", "ones1") + tuple(rPB[7].wr(L + "cbc")), signal=L + "cbc")
            rCEXP.rd(L + "cbc")
            Sq.add("act", lambda e, lc0=lc0: e.activation(out=wfm[:], in_=dtp[:, 3, lc0:lc0+128],
                    func=AF.Exp, bias=dtp[:, 2, lc0+127:lc0+128]),
                   waits=(A + "q",) + tuple(rWFM.wr(L + "wfm")), signal=L + "wfm")
            rDTP.rd(L + "wfm")
            Sq.add("pe", lambda e: e.transpose(pb[6][:, 0:64], wfm[:], identf[0:64, 0:64]),
                   waits=(L + "wfm",) + tuple(rPB[6].wr(L + "wtm")), signal=L + "wtm")
            rWFM.rd(L + "wtm")
            Sq.add("dve", lambda e: e.tensor_copy(wtok[:], pb[6][:, 0:64]),
                   waits=(L + "wtm",) + tuple(rWTOK.wr(L + "wte")) + tuple(rPB[6].rd(L + "wte")),
                   signal=L + "wte")

            for g in (range(0) if SO else range(NG)):
                G = L + f"g{g}"
                Sq.add("qgps", lambda e, g=g: [e.dma_start(out=lc1p[:],
                        in_=shuf_dram[8*g:8*g+8, :].rearrange("h t -> (h t)").unsqueeze(0))],
                       waits=(L + "shst",) + tuple(rLC1.wr(G + "shq")), signal=G + "shq", n_sig=1)
                def t1mm(e):
                    e.matmul(pb[0][:], ones1[:], lc1p[:, 0:512],
                             start=True, stop=True, skip_group_check=True)
                    return e.matmul(pb[1][:], ones1[:], lc1p[:, 512:1024],
                                    start=True, stop=True, skip_group_check=True)
                Sq.add("pe", t1mm, waits=(G + "shq", "ones1") + tuple(rPB[0].wr(G + "t1")) +
                               tuple(rPB[1].wr(G + "t1")), signal=G + "t1")
                rLC1.rd(G + "t1")
                Sq.add("dve", lambda e, g=g: e.tensor_tensor(
                        qm[:], qtok[:, 8*g:8*g+8].unsqueeze(2).broadcast_to([128, 8, 128]),
                        maskt[:].unsqueeze(1).broadcast_to([128, 8, 128]), ALU.add),
                       waits=(L + "qte",) + CON + tuple(rQM.wr(G + "qm")), signal=G + "qm")
                rQT.rd(G + "qm")
                def t2madd(e):
                    e.tensor_tensor(qm[:, 0:4, :], qm[:, 0:4, :],
                                    pb[0][:].rearrange("p (a b) -> p a b", b=128), ALU.add)
                    return e.tensor_tensor(qm[:, 4:8, :], qm[:, 4:8, :],
                                           pb[1][:].rearrange("p (a b) -> p a b", b=128), ALU.add)
                Sq.add("dve", t2madd, waits=(G + "qm", G + "t1"), signal=G + "t2m")
                Sq.add("act", lambda e: e.activation(out=ebuf[:], in_=qm[:], func=AF.Exp),
                       waits=(G + "t2m",) + tuple(rE.wr(G + "e")), signal=G + "e")
                rQM.rd(G + "e")
                def eexta(e):
                    e.activation(out=eext[:, 0:4, :], in_=pb[0][:], func=AF.Exp)
                    return e.activation(out=eext[:, 4:8, :], in_=pb[1][:], func=AF.Exp)
                Sq.add("act", eexta, waits=(G + "t1",) + tuple(rEE.wr(G + "ee")), signal=G + "ee")
                rPB[0].rd(G + "ee"); rPB[1].rd(G + "ee")
                rPB[0].rd(G + "t2m"); rPB[1].rd(G + "t2m")
                Sq.add("pe", lambda e, g=g, lc0=lc0: e.matmul(
                        pb[6][:, 0:128], convout[:, 32+g, lc0:lc0+128],
                        convout[:, 40+g, lc0:lc0+128], start=True, stop=True,
                        skip_group_check=True),
                       waits=CONVDONE + tuple(rPB[6].wr(G + "gp")), signal=G + "gp")
                rCO.rd(G + "gp")
                Sq.add("dve", lambda e: e.tensor_tensor(
                        mprime[:], ebuf[:], pb[6][:, 0:128].unsqueeze(1).broadcast_to([128, 8, 128]),
                        ALU.mult),
                       waits=(G + "e", G + "gp") + tuple(rMP.wr(G + "mp")), signal=G + "mp")
                rE.rd(G + "mp"); rPB[6].rd(G + "mp")
                Sq.add("dve", lambda e, g=g, lc0=lc0: e.tensor_tensor(
                        mext[:], eext[:],
                        convout[:, 40+g, lc0:lc0+128].unsqueeze(1).broadcast_to([128, 8, 128]),
                        ALU.mult),
                       waits=(G + "ee",) + CONVDONE + tuple(rME.wr(G + "me")), signal=G + "me")
                rEE.rd(G + "me"); rCO.rd(G + "me")
                Sq.add("qgps", lambda e, g=g, l4=4*ia+il: [e.dma_start(
                        out=mext_dram[l4, g], in_=mext[:])],
                       waits=(G + "me",), signal=G + "mst", n_sig=1)
                rME.rd(G + "mst")
                Sq.add("dve", lambda e, g=g: e.tensor_copy(Sb[:], S[:, g, :]),
                       waits=tuple(rS.rd(G + "sb")) + tuple(rSB.wr(G + "sb")), signal=G + "sb")
                def ymm(e, g=g):
                    last = None
                    for cc4 in range(4):
                        h0, h1 = 2 * cc4, 2 * cc4 + 1
                        cc = 4 * g + cc4
                        e.matmul(pb[2][0:64, cc4*128:(cc4+1)*128], xTt[:, cc, 0:64],
                                 mprime[:, h0, :], start=True, stop=False, skip_group_check=True)
                        e.matmul(pb[2][0:64, cc4*128:(cc4+1)*128], Sb[:, h0*64:(h0+1)*64],
                                 mext[:, h0, :], start=False, stop=True, skip_group_check=True)
                        e.matmul(pb[2][64:128, cc4*128:(cc4+1)*128], xTt[:, cc, 64:128],
                                 mprime[:, h1, :], start=True, stop=False,
                                 tile_position=(0, 64), skip_group_check=True)
                        last = e.matmul(pb[2][64:128, cc4*128:(cc4+1)*128], Sb[:, h1*64:(h1+1)*64],
                                        mext[:, h1, :], start=False, stop=True,
                                        tile_position=(0, 64), skip_group_check=True)
                    return last
                Sq.add("pe", ymm, waits=(G + "mp", G + "me", L + "tpe7", G + "sb") +
                               tuple(rPB[2].wr(G + "ymm")), signal=G + "ymm")
                rMP.rd(G + "ymm"); rME.rd(G + "ymm"); rXT.rd(G + "ymm"); rSB.rd(G + "ymm")
                def yev(e, g=g, lc0=lc0):
                    last = None
                    for cc4 in range(4):
                        cc = 4 * g + cc4
                        last = e.scalar_tensor_tensor(
                            y_sb[:, cc, :], convout[:, cc, lc0:lc0+128], drep[:, cc:cc+1],
                            pb[2][:, cc4*128:(cc4+1)*128], ALU.mult, ALU.add)
                    return last
                Sq.add("dve", yev, waits=(G + "ymm",) + CON +
                               (tuple(rYSB.wr(G + "yev")) if g == 0 else ()),
                       signal=G + "yev")
                rPB[2].rd(G + "yev"); rCO.rd(G + "yev")
            Sq.add("dve", lambda e: e.tensor_tensor(
                    xTh, xTh, wtok[:].unsqueeze(2).broadcast_to([128, 64, 64]), ALU.mult),
                   waits=(L + "wte",) + tuple(rXT.wr(L + "xscl")), signal=L + "xscl")
            rWTOK.rd(L + "xscl")
            for g in range(NG):
                def dsmm(e, g=g):
                    return e.matmul(pb[3][:], bTt[:, g, :], xT3[:, 512*g:512*(g+1)],
                                    start=True, stop=True, skip_group_check=True)
                Sq.add("pe", dsmm, waits=(L + "xscl", L + "tpe9") + tuple(rPB[3].wr(L + f"ds{g}")),
                       signal=L + f"ds{g}")
                rBT.rd(L + f"ds{g}"); rXT.rd(L + f"ds{g}")
                def supd(e, g=g):
                    e.tensor_tensor(scr[:].rearrange("p (a b) -> p a b", b=64),
                                    S[:, g, :].rearrange("p (a b) -> p a b", b=64),
                                    pb[7][:, 8*g:8*g+8].unsqueeze(2).broadcast_to([128, 8, 64]),
                                    ALU.mult)
                    return e.tensor_tensor(S[:, g, :], scr[:], pb[3][:], ALU.add)
                Sq.add("dve", supd, waits=(L + f"ds{g}", L + "cbc") +
                               (tuple(rS.wr(L + "supd0")) if g == 0 else ()) +
                               (tuple(rSCR.wr(L + "supd0")) if g == 0 else ()),
                       signal=L + f"supd{g}")
                rPB[3].rd(L + f"supd{g}"); rPB[7].rd(L + f"supd{g}")
            SUPD = (L + f"supd{NG-1}",)
            rS.w = L + f"supd{NG-1}"
            if not SO:
                Sq.add("qgps", lambda e, l4=4*ia+il: [e.dma_start(
                        out=y_dram[l4], in_=y_sb[:])],
                       waits=(L + f"g{NG-1}yev",), signal=L + "yst", n_sig=1)
                rYSB.rd(L + "yst")


    Sq.add("qgps", lambda e: [e.dma_start(out=cc_in, in_=S3)],
           waits=tuple(rS.rd("ccst")), signal="ccst", n_sig=1)
    Sq.semnames.append("ccv")
    def ccfn(e):
        e.collective_compute(
            "AllGather", ALU.bypass, ins=[cc_in], outs=[cc_out],
            replica_groups=[[0, 1, 2, 3, 4, 5, 6, 7]]).then_inc(sems["ccv"], 1)
        return []
    Sq.add("qgps", ccfn, waits=("ccst",), signal=None)
    Sq.events["ccg"] = ("ccv", 1)
    ar.off = mk_xc
    gtile = sbuf(128, [2, 4096], F32)
    wot2 = sbuf(128, [2, 32, 128], BF16)
    rGT = [R(), R()]
    for k in range(8):
        buf = k % 2
        Sq.add("qsync", lambda e, k=k, buf=buf: [e.dma_start(
                out=gtile[:, buf, :], in_=cc_out[128*k:128*(k+1), :])],
               waits=("ccg",) + tuple(rGT[buf].wr(f"gld{k}")), signal=f"gld{k}", n_sig=1)
        if k == 0:
            Sq.add("dve", lambda e: e.tensor_scalar(S3, gtile[:, 0, :], selc[:, 0:1],
                                                    None, ALU.mult),
                   waits=("gld0",) + CON + tuple(rS.wr("pac0")), signal="pac0")
            rGT[0].rd("pac0")
        else:
            Sq.add("dve", lambda e, k=k, buf=buf: e.tensor_scalar(
                    gtile[:, buf, :], gtile[:, buf, :], selc[:, k:k+1], None, ALU.mult),
                   waits=(f"gld{k}",), signal=f"gsc{k}")
            Sq.add("dve", lambda e, buf=buf: e.tensor_tensor(S3, S3, gtile[:, buf, :],
                                                             ALU.add),
                   signal=f"pac{k}")
            rGT[buf].rd(f"pac{k}")
    rS.w = "pac7"

    rY2 = R(); rG2 = R(); rSB2 = R(); rCX2 = R(); rSG2 = R(); rST2 = R()
    rMB = [R(), R()]
    rPB2 = [R() for _ in range(8)]
    rWO2 = [R(), R()]; rSCR2 = R()
    mbuf = [mext, mprime]
    for l in range(16):
        N = f"n{l}"
        M1 = f"ma{l // 4}l{l % 4}"
        lc0 = (l % 4) * 128
        Sq.add("qsync", lambda e, l=l: [e.dma_start(out=y_sb[:], in_=y_dram[l])],
               waits=(M1 + "yst",) + tuple(rY2.wr(N + "yld")), signal=N + "yld", n_sig=1)
        Sq.add("qsync", lambda e, l=l: [e.dma_start(
                out=xTt[:], in_=gate_dram[l // 4][:, :, (l % 4)*128:(l % 4)*128+128])],
               waits=(f"ma{l // 4}gst",) + tuple(rG2.wr(N + "gld")), signal=N + "gld", n_sig=1)
        Sq.add("qsync", lambda e, l=l: [e.dma_start(out=cexp[:], in_=cexp_dram[l:l+1, :])],
               waits=(M1 + "cxst",) + tuple(rCX2.wr(N + "cxl")), signal=N + "cxl", n_sig=1)
        cbank = l % 2
        Sq.add("pe", lambda e, cbank=cbank: e.matmul(pb[cbank][:, 0:64], ones1[:], cexp[:],
                start=True, stop=True, skip_group_check=True),
               waits=(N + "cxl", "ones1") + tuple(rPB2[cbank].wr(N + "cbc")), signal=N + "cbc")
        rCX2.rd(N + "cbc")
        for g in range(NG):
            G = N + f"g{g}"
            mb = mbuf[g % 2]
            Sq.add("qsync", lambda e, l=l, g=g, mb=mb: [e.dma_start(
                    out=mb[:], in_=mext_dram[l, g])],
                   waits=(M1 + f"g{g}mst",) + tuple(rMB[g % 2].wr(G + "ml")),
                   signal=G + "ml", n_sig=1)
            Sq.add("dve", lambda e, g=g: e.tensor_copy(Sb[:], S[:, g, :]),
                   waits=tuple(rS.rd(G + "pb")) + tuple(rSB2.wr(G + "pb")),
                   signal=G + "pb")
            dpi = 2 + (l * NG + g) % 2
            def dymm(e, mb=mb, dpi=dpi):
                last = None
                for cc4 in range(4):
                    h0, h1 = 2 * cc4, 2 * cc4 + 1
                    e.matmul(pb[dpi][0:64, cc4*128:(cc4+1)*128], Sb[:, h0*64:(h0+1)*64],
                             mb[:, h0, :], start=True, stop=True, skip_group_check=True)
                    last = e.matmul(pb[dpi][64:128, cc4*128:(cc4+1)*128],
                                    Sb[:, h1*64:(h1+1)*64], mb[:, h1, :],
                                    start=True, stop=True, tile_position=(0, 64),
                                    skip_group_check=True)
                return last
            Sq.add("pe", dymm, waits=(G + "ml", G + "pb") + tuple(rPB2[dpi].wr(G + "dy")),
                   signal=G + "dy")
            rSB2.rd(G + "dy"); rMB[g % 2].rd(G + "dy")
            Sq.add("dve", lambda e, g=g, dpi=dpi: e.tensor_tensor(
                    y_sb[:, 4*g:4*g+4, :], y_sb[:, 4*g:4*g+4, :],
                    pb[dpi][:].rearrange("p (a b) -> p a b", b=128), ALU.add),
                   waits=(G + "dy", N + "yld"), signal=G + "ya")
            rPB2[dpi].rd(G + "ya")
            Sq.add("dve", lambda e, g=g, cbank=cbank: e.tensor_tensor(
                    S[:, g, :].rearrange("p (a b) -> p a b", b=64),
                    S[:, g, :].rearrange("p (a b) -> p a b", b=64),
                    pb[cbank][:, 8*g:8*g+8].unsqueeze(2).broadcast_to([128, 8, 64]),
                    ALU.mult),
                   waits=(N + "cbc",) + tuple(rS.wr(G + "pu")), signal=G + "pu")
            rPB2[cbank].rd(G + "pu")
        Sq.add("act", lambda e: e.activation(out=sgb[:], in_=xTt[:], func=AF.Silu),
               waits=(N + "gld",) + tuple(rSG2.wr(N + "sg")), signal=N + "sg")
        rG2.rd(N + "sg")
        Sq.add("dve", lambda e: e.tensor_tensor(sgb[:], y_sb[:], sgb[:], ALU.mult),
               waits=(N + "sg", N + f"g{NG-1}ya"), signal=N + "gmul")
        Sq.add("act", lambda e: e.activation(out=y_sb[:], in_=sgb[:], func=AF.Square),
               waits=(N + "gmul",), signal=N + "gsq")
        def gsmm2(e):
            last = None
            for cc in range(32):
                last = e.matmul(pb[6][0:1, 0:128], onescolb[:], y_sb[:, cc, :],
                                start=(cc == 0), stop=(cc == 31))
            return last
        Sq.add("pe", gsmm2, waits=(N + "gsq", "onescolb") + tuple(rPB2[6].wr(N + "gsmm")),
               signal=N + "gsmm")
        rY2.w = N + "gsmm"
        Sq.add("dve", lambda e: e.tensor_scalar_mul(statg[0:1, 0, 0:128],
                pb[6][0:1, 0:128], 1.0 / INTER),
               waits=(N + "gsmm",) + tuple(rST2.wr(N + "gvar")), signal=N + "gvar")
        rPB2[6].rd(N + "gvar")
        Sq.add("act", lambda e: e.activation(out=statg[0:1, 1, 0:128],
                in_=statg[0:1, 0, 0:128], func=AF.Sqrt, bias=epscol[0:1]),
               waits=(N + "gvar", "epscol"), signal=N + "gsqr")
        Sq.add("dve", lambda e: e.reciprocal(statg[0:1, 1, 0:128], statg[0:1, 1, 0:128]),
               waits=(N + "gsqr",), signal=N + "grs")
        Sq.add("pe", lambda e: e.matmul(pb[6][:, 0:128], ones1[:], statg[0:1, 1, 0:128],
                start=True, stop=True, skip_group_check=True),
               waits=(N + "grs",) + tuple(rPB2[6].wr(N + "gbc")), signal=N + "gbc")
        rST2.rd(N + "gbc")
        Sq.add("dve", lambda e, lc0=lc0: e.tensor_tensor(
                gnb[:, :, lc0:lc0+128], sgb[:],
                pb[6][:, 0:128].unsqueeze(1).broadcast_to([128, 32, 128]), ALU.mult),
               waits=(N + "gbc",) + (("ma3gst",) if l == 0 else ()) +
                     tuple(rGNB.wr(N + "norm")),
               signal=N + "norm")
        rSG2.rd(N + "norm"); rPB2[6].rd(N + "norm")
        if l % 4 == 3:
            ia2 = l // 4
            for oc in range(KCH):
                tg = f"n{l}op{oc}"
                pi = 4 + (oc % 2)
                wb = oc % 2
                Sq.add("qsync",
                       lambda e, oc=oc, wb=wb: [e.dma_start(out=wot2[:, wb, 0:16, :],
                        in_=woutb_dram[oc][:, 0:16, :])],
                       waits=WOUT_READY + tuple(rWO2[wb].wr(tg + "ld")),
                       signal=tg + "ld", n_sig=1)
                Sq.add("qgps",
                       lambda e, oc=oc, wb=wb: [e.dma_start(out=wot2[:, wb, 16:32, :],
                        in_=woutb_dram[oc][:, 16:32, :])],
                       waits=(), signal=tg + "ldb", n_sig=1)
                def omm(e, pi=pi, wb=wb):
                    last = None
                    for kc in range(32):
                        last = e.matmul(pb[pi][:], wot2[:, wb, kc, :], gnb[:, kc, :],
                                        start=(kc == 0), stop=(kc == 31))
                    return last
                Sq.add("pe", omm, waits=(tg + "ld", tg + "ldb", N + "norm") +
                               tuple(rPB2[pi].wr(tg + "mm")), signal=tg + "mm")
                rWO2[wb].rd(tg + "mm")
                rGNB.rd(tg + "mm")
                Sq.add("dve", lambda e, pi=pi: e.tensor_copy(astg[:], pb[pi][:]),
                       waits=(tg + "mm",) + tuple(rSCR2.wr(tg + "ev")), signal=tg + "ev")
                rPB2[pi].rd(tg + "ev")
                Sq.add("qgps", lambda e, oc=oc, ia2=ia2: [e.dma_start(
                        out=attn_out.rearrange("(c p) t -> p c t", p=128)[:, oc, 512*ia2:512*(ia2+1)],
                        in_=astg[:])],
                       waits=(tg + "ev",), signal=tg + "st", n_sig=1)
                rSCR2.rd(tg + "st")

    ar.off = ar.base
    rep = sbuf(4, [128], BF16)
    shc = sbuf(128, [1], I32)
    xn = sbuf(128, [KCH, T], BF16)
    bytR = sbuf(128, [2, 2048], I32)
    kiF = sbuf(128, [2048], I32)
    qaF = sbuf(128, [2048], BF16)
    qbF = sbuf(128, [2048], BF16)
    scstB = sbuf(4, [2, 1024], BF16)
    wgB = sbuf(128, [2, KCH, 256], BF16)
    wuB = sbuf(128, [2, KCH, 256], BF16)
    interB = sbuf(128, [2, 4, T], BF16)
    wdR = sbuf(128, [2, 4, HID], BF16)
    dsF = sbuf(128, [4, 512], F32)
    qa3 = qaF.rearrange("p (a f) -> p a f", f=256)

    bankG = [PST[:, 0:512], PST[:, 512:1024]]
    bankU = [PST[:, 1024:1536], PST[:, 1536:2048]]
    bankD = [PST[:, 2048:2560], PST[:, 2560:3072]]
    bankS = [PST[:, 3072:3584], PST[:, 3584:4096]]

    NSL = NMLP // 256
    NGRP = NSL // 2

    Sq.barrier("MF")
    Sq.add("qsync", lambda e: e.dma_start(out=rep[:], in_=rep_in), signal="rep")
    Sq.add("qsync", lambda e: e.dma_start(out=shc[:], in_=shc_in), signal="shcl")
    Sq.add("qgps", lambda e: e.dma_start(out=xn[:], in_=xn_in.rearrange("(k p) t -> p k t", p=128)),
           signal="xn")

    pgr = pg_in.rearrange("(kc b) n -> b kc n", b=64)
    pur = pu_in.rearrange("(kc b) n -> b kc n", b=64)
    sgr = sg_in.rearrange("(kc b) n -> b kc n", b=4)
    sur = su_in.rearrange("(kc b) n -> b kc n", b=4)
    sdr = sd_in.rearrange("(nc b) h -> b nc h", b=4)

    CN = {"dg": 0, "sx": 0, "gm": 0, "um": 0, "im": 0, "ge": 0, "dm": 0, "cl": 0}
    dec_done = {}
    dwd_ev = {}
    dmz_ev = {}
    pe_sexp = []

    def emit_ext_val(ldev, first_wts=()):
        n = CN["dg"] - 1
        Sq.add("dve", lambda e: e.tensor_tensor(
                bytR[:, n % 2, :], bytR[:, n % 2, :],
                shc[:].broadcast_to([128, 2048]), ALU.logical_shift_right),
               waits=(ldev, "shcl") + tuple(first_wts))
        Sq.add("dve", lambda e: e.tensor_scalar(
                kiF[:], bytR[:, n % 2, :], 7, None, ALU.bitwise_and))
        Sq.add("dve", lambda e: e.tensor_scalar(qaF[:], kiF[:], -2.0, 2.0, ALU.add, ALU.mult))
        Sq.add("dve", lambda e: e.tensor_scalar(qbF[:], kiF[:], -4.0, 4.0, ALU.add, ALU.mult))
        Sq.add("dve", lambda e: e.tensor_tensor(qaF[:], qaF[:], qbF[:], ALU.max))
        Sq.add("dve", lambda e: e.tensor_tensor(qaF[:], qaF[:], kiF[:], ALU.max))
        Sq.add("dve", lambda e: e.tensor_scalar(
                kiF[:], bytR[:, n % 2, :], 8, None, ALU.bitwise_and),
               signal=f"fv{n}")
        Sq.add("dve", lambda e: e.scalar_tensor_tensor(qaF[:], kiF[:], -4.0, qaF[:],
                                                       ALU.add, ALU.mult))

    pe_sld = []

    def sexp_item(ld_fn, mm_waits_extra=()):
        m = CN["sx"]; CN["sx"] += 1
        swts = (f"sx{m-2}",) if m >= 2 else ()
        pe_sld.append((ld_fn(m % 2), swts, f"sl{m}"))
        pwts = [f"sl{m}"]
        if m == 0:
            pwts.append("rep")
        if m >= 1:
            pwts.append(f"fw{m-1}")
        def mm(e, m=m):
            e.matmul(bankS[0], rep[:], scstB[:, m % 2, 0:512],
                     start=True, stop=True, skip_group_check=True)
            return e.matmul(bankS[1], rep[:], scstB[:, m % 2, 512:1024],
                            start=True, stop=True, skip_group_check=True)
        pe_sexp.append((mm, tuple(pwts) + tuple(mm_waits_extra), f"sx{m}"))
        return m

    def emit_decode_gu(gs, mi, inject=None):
        psrc, ssrc, wt = (pgr, sgr, wgB) if mi == 0 else (pur, sur, wuB)
        buf = gs % 2
        for d in range(2):
            n = CN["dg"]; CN["dg"] += 1
            wts = (f"fv{n-2}",) if n >= 2 else ()
            bslot3 = bytR[:, n % 2, :].rearrange("p (kc f) -> p kc f", f=256)
            def ld(e, psrc=psrc, d=d, gs=gs, bslot3=bslot3):
                return [e.dma_start(out=bslot3[0:64],
                                    in_=psrc[0:64, 8*d:8*d+8, 256*gs:256*(gs+1)]),
                        e.dma_start(out=bslot3[64:128],
                                    in_=psrc[0:64, 8*d:8*d+8, 256*gs:256*(gs+1)])]
            Sq.add("qsync", ld, waits=wts, signal=f"fl{n}", n_sig=2)
            emit_ext_val(f"fl{n}")
            for gh in range(2):
                kc0 = 8 * d + 4 * gh
                def sld(e, ssrc=ssrc, kc0=kc0, gs=gs, slot=None):
                    return [e.dma_start(
                        out=scstB[:, slot, :].rearrange("p (a f) -> p a f", f=256),
                        in_=ssrc[0:4, kc0:kc0+4, 256*gs:256*(gs+1)])]
                m = sexp_item(lambda slot, sld=sld: (lambda e: sld(e, slot=slot)))
                wwts = [f"sx{m}"]
                if d == 0 and gh == 0 and gs >= 2:
                    wwts.append(f"lgm{gs-2}" if mi == 0 else f"lum{gs-2}")
                sig = None
                if d == 1 and gh == 1:
                    sig = f"dec{gs}m{mi}"
                    dec_done[(gs, mi)] = sig
                def wm(e, wt=wt, buf=buf, kc0=kc0, gh=gh, m=m):
                    return e.tensor_tensor(
                        wt[:, buf, kc0:kc0+4, :],
                        qa3[:, 4*gh:4*gh+4, :],
                        PST[:, 3072:4096].rearrange("p (a f) -> p a f", f=256), ALU.mult)
                Sq.add("dve", wm, waits=tuple(wwts), signal=sig or (f"fw{m}"))
                if sig:
                    Sq.events[f"fw{m}"] = Sq.events[sig]
                if inject and (d, gh) in inject:
                    inject[(d, gh)]()

    def emit_decode_down(s, jn, inject=None):
        G = s // 2
        gslot = G % 2
        if True:
            nch = 2 * s + jn
            j = 2 * (s % 2) + jn
            n = CN["dg"]; CN["dg"] += 1
            wts = (f"fv{n-2}",) if n >= 2 else ()
            def ld(e, nch=nch, n=n):
                return [e.dma_start(out=bytR[0:64, n % 2, :], in_=pd_in[64*nch:64*nch+64, :]),
                        e.dma_start(out=bytR[64:128, n % 2, :], in_=pd_in[64*nch:64*nch+64, :])]
            Sq.add("qsync", ld, waits=wts, signal=f"fl{n}", n_sig=2)
            fw = ()
            if jn == 0 and s % 2 == 0 and G >= 2:
                fw = (dmz_ev[G - 2],)
            emit_ext_val(f"fl{n}", first_wts=fw)
            for rh in range(2):
                def sld(e, nch=nch, rh=rh, slot=None):
                    return [e.dma_start(
                        out=scstB[:, slot, :].rearrange("p (o f) -> p o f", o=1),
                        in_=sdr[0:4, nch:nch+1, 1024*rh:1024*(rh+1)])]
                m = sexp_item(lambda slot, sld=sld: (lambda e: sld(e, slot=slot)))
                sig = None
                if jn == 1 and rh == 1:
                    sig = f"dwd{s}"
                    dwd_ev[s] = sig
                def wm(e, gslot=gslot, j=j, rh=rh, m=m):
                    return e.tensor_tensor(
                        wdR[:, gslot, j, 1024*rh:1024*(rh+1)],
                        qaF[:, 1024*rh:1024*(rh+1)], PST[:, 3072:4096], ALU.mult)
                Sq.add("dve", wm, waits=(f"sx{m}",), signal=sig or (f"fw{m}"))
                if sig:
                    Sq.events[f"fw{m}"] = Sq.events[sig]
                if inject and rh in inject:
                    inject[rh]()

    def emit_imul(s, i):
        idx = CN["im"]; CN["im"] += 1
        assert idx == 8 * s + i
        j2, tb = i // 4, i % 4
        gslot = (s // 2) % 2
        jg = 2 * (s % 2) + j2
        def im(e, gslot=gslot, jg=jg, tb=tb, idx=idx):
            return e.tensor_tensor(interB[:, gslot, jg, tb*512:(tb+1)*512],
                                   interB[:, gslot, jg, tb*512:(tb+1)*512],
                                   bankU[idx % 2], ALU.mult)
        Sq.add("dve", im, waits=(f"ge{idx}", f"um{idx}"), signal=f"im{idx}")

    def pop_sexp(k):
        for _ in range(k):
            if pe_sexp:
                while pe_sld and (CN["sx"] - len(pe_sexp) + 2 >
                                  CN["sx"] - len(pe_sld)):
                    lfn, lwts, lsig = pe_sld.pop(0)
                    Sq.add("qgps", lfn, waits=lwts, signal=lsig, n_sig=1)
                fn, wts, sig = pe_sexp.pop(0)
                Sq.add("pe", fn, waits=wts, signal=sig)

    def emit_dmm_cell(G, q, tb):
        gslot = G % 2
        for hj in range(4):
            hc = 4 * q + hj
            cl = CN["cl"]; CN["cl"] += 1
            half = cl % 4
            dmidx = CN["dm"]; CN["dm"] += 1
            def dmm(e, hc=hc, tb=tb, dmidx=dmidx, gslot=gslot):
                last = None
                for j in range(4):
                    last = e.matmul(bankD[dmidx % 2],
                                    wdR[:, gslot, j, hc*128:(hc+1)*128],
                                    interB[:, gslot, j, tb*512:(tb+1)*512],
                                    start=(j == 0), stop=(j == 3))
                return last
            wts = [f"dv{dmidx-2}"] if dmidx >= 2 else []
            if hj == 0:
                wts.append(f"im{8*(2*G+1)+4+tb}")
                if q == 0 and tb == 0:
                    wts.append(dwd_ev[2 * G + 1])
            Sq.add("pe", dmm, waits=tuple(wts), signal=f"dm{dmidx}")
            if hc == 15 and tb == 3:
                dmz_ev[G] = f"dm{dmidx}"
            awts = [f"dm{dmidx}"]
            if cl >= 4:
                awts.append(f"sq{cl-4}")
            Sq.add("act", lambda e, half=half, dmidx=dmidx: e.activation(
                    out=dsF[:, half, :], in_=bankD[dmidx % 2], func=AF.Copy),
                   waits=tuple(awts), signal=f"dv{dmidx}")
            def store(e, G=G, hc=hc, tb=tb, half=half):
                if G > 0:
                    return [e.dma_start(out=acc_r[:, hc:hc+1, tb*512:(tb+1)*512],
                                        in_=dsF[:, half, :].unsqueeze(1),
                                        accum_op=ALU.add)]
                return [e.dma_start(out=acc_r[:, hc:hc+1, tb*512:(tb+1)*512],
                                    in_=dsF[:, half, :].unsqueeze(1))]
            Sq.add("qgps", store, waits=(f"dv{dmidx}",), signal=f"sq{cl}", n_sig=1)

    emit_decode_gu(0, 0)
    emit_decode_gu(0, 1)
    pop_sexp(len(pe_sexp))

    for s in range(NSL):
        G = s // 2
        emit_decode_down(s, 0)
        emit_decode_down(s, 1, inject={
            0: lambda: emit_imul(s, 0)})
        if s + 1 < NSL:
            emit_decode_gu(s + 1, 0, inject={
                (0, 0): lambda: (emit_imul(s, 1), emit_imul(s, 2)),
                (1, 1): lambda: emit_imul(s, 3)})
            emit_decode_gu(s + 1, 1, inject={
                (0, 0): lambda: emit_imul(s, 4),
                (0, 1): lambda: emit_imul(s, 5),
                (1, 0): lambda: emit_imul(s, 6)})
            emit_imul(s, 7)
        else:
            for i in range(1, 8):
                emit_imul(s, i)
        Gd = G - 1 if s >= 2 else None
        qbase = 2 * (s % 2)
        for i in range(8):
            j2, tb = i // 4, i % 4
            gmidx = CN["gm"]; CN["gm"] += 1
            assert gmidx == 8 * s + i
            def gmm(e, s=s, j2=j2, tb=tb, gmidx=gmidx):
                last = None
                for k in range(KCH):
                    last = e.matmul(bankG[gmidx % 2], wgB[:, s % 2, k, j2*128:(j2+1)*128],
                                    xn[:, k, tb*512:(tb+1)*512],
                                    start=(k == 0), stop=(k == KCH - 1))
                return last
            wts = [f"ge{gmidx-2}"] if gmidx >= 2 else []
            if i == 0:
                wts.append(dec_done[(s, 0)])
                if s == 0:
                    wts.append("xn")
            Sq.add("pe", gmm, waits=tuple(wts), signal=f"gm{gmidx}")
            if i == 7:
                Sq.events[f"lgm{s}"] = Sq.events[f"gm{gmidx}"]
            awts = [f"gm{gmidx}"]
            if i == 0 and s % 2 == 0 and G >= 2:
                awts.append(dmz_ev[G - 2])
            gslot = G % 2
            jg = 2 * (s % 2) + j2
            Sq.add("act", lambda e, gslot=gslot, jg=jg, tb=tb, gmidx=gmidx: e.activation(
                    out=interB[:, gslot, jg, tb*512:(tb+1)*512], in_=bankG[gmidx % 2],
                    func=AF.Gelu_apprx_tanh),
                   waits=tuple(awts), signal=f"ge{gmidx}")
            umidx = CN["um"]; CN["um"] += 1
            assert umidx == gmidx
            def umm(e, s=s, j2=j2, tb=tb, umidx=umidx):
                last = None
                for k in range(KCH):
                    last = e.matmul(bankU[umidx % 2], wuB[:, s % 2, k, j2*128:(j2+1)*128],
                                    xn[:, k, tb*512:(tb+1)*512],
                                    start=(k == 0), stop=(k == KCH - 1))
                return last
            wts = [f"im{umidx-2}"] if umidx >= 2 else []
            if i == 0:
                wts.append(dec_done[(s, 1)])
            Sq.add("pe", umm, waits=tuple(wts), signal=f"um{umidx}")
            if i == 7:
                Sq.events[f"lum{s}"] = Sq.events[f"um{umidx}"]
            pop_sexp(2 if i % 2 == 0 else 1)
            if Gd is not None:
                emit_dmm_cell(Gd, qbase + i // 4, tb)
        pop_sexp(len(pe_sexp))

    for q in range(4):
        for tb in range(4):
            emit_dmm_cell(NGRP - 1, q, tb)

    Sq.barrier("FR")
    ar.off = ar.base
    unpif = sbuf(128, [128], F32)
    xtile = sbuf(128, [KCH, 512], F32)
    atile = sbuf(128, [KCH, 512], F32)
    ftile = sbuf(128, [KCH, 512], F32)
    ostage = sbuf(128, [KCH, 128], F32)
    rXTI = R(); rATI = R(); rFTI = R(); rOST = R()
    rPBr = [R() for i in range(4)]
    attn_r = attn_dram.rearrange("(k p) t -> p k t", p=128)
    xT_r = xT_in.rearrange("(k p) t -> p k t", p=128)

    Sq.add("qsync", lambda e: [e.dma_start(out=unpif, in_=unpi_in)], signal="unpi", n_sig=1)
    for ia in range(NA):
        RA = f"r{ia}"
        def rld(e, ia=ia):
            return [e.dma_start(out=xtile, in_=xT_r[:, :, 3+512*ia:3+512*(ia+1)]),
                    e.dma_start(out=atile, in_=attn_r[:, :, 512*ia:512*(ia+1)]),
                    e.dma_start(out=ftile, in_=acc_r[:, :, 512*ia:512*(ia+1)])]
        Sq.add("qsync", rld,
               waits=tuple(rXTI.wr(RA + "ld")) + tuple(rATI.wr(RA + "ld")) +
                     tuple(rFTI.wr(RA + "ld")),
               signal=RA + "ld", n_sig=3)
        Sq.add("dve", lambda e: e.tensor_tensor(xtile, xtile, atile, ALU.add),
               waits=(RA + "ld",) + tuple(rXTI.wr(RA + "s1")), signal=RA + "s1")
        rATI.rd(RA + "s1")
        Sq.add("dve", lambda e: e.tensor_tensor(xtile, xtile, ftile, ALU.add),
               waits=(RA + "s1",) + tuple(rXTI.wr(RA + "s2")), signal=RA + "s2")
        rFTI.rd(RA + "s2")
        for tt in range(4):
            RT = RA + f"t{tt}"
            for kg in range(4):
                def tmm(e, tt=tt, kg=kg):
                    last = None
                    for j in range(4):
                        kc = 4 * kg + j
                        last = e.transpose(pb[kg][:, j*128:(j+1)*128],
                                           xtile[:, kc, tt*128:(tt+1)*128], unpif)
                    return last
                Sq.add("pe", tmm, waits=(RA + "s2", "unpi") + tuple(rPBr[kg].wr(RT + f"tm{kg}")),
                       signal=RT + f"tm{kg}")
                rXTI.rd(RT + f"tm{kg}")
                Sq.add("dve", lambda e, kg=kg: e.tensor_copy(ostage[:, 4*kg:4*kg+4, :],
                        pb[kg].rearrange("p (a b) -> p a b", b=128)),
                       waits=(RT + f"tm{kg}",) +
                             (tuple(rOST.wr(RT + "ev0")) if kg == 0 else ()) +
                             tuple(rPBr[kg].rd(RT + f"ev{kg}")),
                       signal=RT + f"ev{kg}")
            def rst(e, ia=ia, tt=tt):
                return [e.dma_start(out=out_dram[512*ia+128*tt:512*ia+128*(tt+1), :],
                                    in_=ostage.rearrange("p a b -> p (a b)"))]
            Sq.add("qgps", rst, waits=tuple(RT + f"ev{kg}" for kg in range(4)),
                   signal=RT + "st", n_sig=1)
            rOST.rd(RT + "st")

    sems = {}
    for name in Sq.semnames:
        sems[name] = es.enter_context(nc.semaphore("sem_" + name))
    final_evs = [f"r{ia}t{tt}st" for ia in range(NA) for tt in range(4)]
    with es, nc.Block() as block:
        @block.sync
        def _(sync):
            Sq.emit("qsync", sync, sems)
        @block.gpsimd
        def _(gpsimd):
            Sq.emit("qgps", gpsimd, sems)
            agg = {}
            for evn in final_evs:
                sn, v = Sq.events[evn]
                agg[sn] = max(agg.get(sn, 0), v)
            for sn, v in agg.items():
                gpsimd.wait_ge(sems[sn], v)
        @block.vector
        def _(vector):
            Sq.emit("dve", vector, sems)
        @block.scalar
        def _(scalar):
            Sq.emit("act", scalar, sems)
        @block.tensor
        def _(tensor):
            Sq.emit("pe", tensor, sems)
    return nc


def kernel(**inputs):
    x = np.asarray(inputs["L0_attn_hidden_states"], dtype=np.float32)
    Wz = np.asarray(inputs["L0_attn_Wz"], dtype=np.float32)
    Wx = np.asarray(inputs["L0_attn_Wx"], dtype=np.float32)
    WB = np.asarray(inputs["L0_attn_WB"], dtype=np.float32)
    WC = np.asarray(inputs["L0_attn_WC"], dtype=np.float32)
    Wdt = np.asarray(inputs["L0_attn_Wdt"], dtype=np.float32)
    conv_w = np.asarray(inputs["L0_attn_conv_w"], dtype=np.float32)
    conv_b = np.asarray(inputs["L0_attn_conv_b"], dtype=np.float32)
    dt_bias = np.asarray(inputs["L0_attn_dt_bias"], dtype=np.float32)
    A_log = np.asarray(inputs["L0_attn_A_log"], dtype=np.float32)
    D = np.asarray(inputs["L0_attn_D"], dtype=np.float32)
    norm_w = np.asarray(inputs["L0_attn_norm_w"], dtype=np.float32)
    Wout = np.asarray(inputs["L0_attn_Wout"], dtype=np.float32)
    pg = np.asarray(inputs["L0_ffn_gate_W_packed"], dtype=np.int32)
    sg = np.asarray(inputs["L0_ffn_gate_scales"], dtype=np.float32)
    pu = np.asarray(inputs["L0_ffn_up_W_packed"], dtype=np.int32)
    su = np.asarray(inputs["L0_ffn_up_scales"], dtype=np.float32)
    pd = np.asarray(inputs["L0_ffn_down_W_packed"], dtype=np.int32)
    sd = np.asarray(inputs["L0_ffn_down_scales"], dtype=np.float32)

    pig = pi_full(HID)
    pid = pi_full(NMLP)
    wz = np.ascontiguousarray(Wz[pig])
    wx = np.ascontiguousarray(Wx[pig])
    wb = np.ascontiguousarray(WB[pig])
    wc = np.ascontiguousarray(WC[pig])
    wdt = np.ascontiguousarray(Wdt[pig])
    wout = np.ascontiguousarray(Wout[:, pig])
    convw_h = np.ascontiguousarray(conv_w.reshape(NCC, 128, 4).transpose(1, 0, 2).reshape(128, NCC * 4))
    convb_h = np.ascontiguousarray(conv_b.reshape(NCC, 128).T)
    dtb_h = np.ascontiguousarray(dt_bias[:, None])
    alog_h = np.ascontiguousarray(A_log[:, None])
    drep_h = np.ascontiguousarray(np.repeat(D, HD).reshape(32, 128).T)
    nw_h = np.ascontiguousarray(norm_w.reshape(32, 128).T)
    ident = np.eye(128, dtype=np.float32)
    import ml_dtypes
    mask = np.where(np.arange(128)[:, None] <= np.arange(128)[None, :], 0, -1e30).astype(ml_dtypes.bfloat16)
    unpi = np.zeros((128, 128), np.float32)
    unpi[np.argsort(pi128()), np.arange(128)] = 1.0
    repm = np.zeros((4, 128), np.float32)
    for p in range(128):
        repm[(p % 64) // 16, p] = -0.125
    repm = repm.astype(ml_dtypes.bfloat16)
    shc_h = np.zeros((128, 1), np.int32)
    shc_h[64:] = 4
    pgT = np.ascontiguousarray(pg.T)[:, pid].copy()
    puT = np.ascontiguousarray(pu.T)[:, pid].copy()
    pdT = np.ascontiguousarray(pd.T)[:, pig].copy()
    sgT = np.ascontiguousarray(sg.T)[:, pid].copy()
    suT = np.ascontiguousarray(su.T)[:, pid].copy()
    sdT = np.ascontiguousarray(sd.T)[:, pig].copy()

    shared = {
        "wz_in": wz, "wx_in": wx, "wb_in": wb, "wc_in": wc, "wdt_in": wdt,
        "wout_in": wout, "convw_in": convw_h, "convb_in": convb_h,
        "dtb_in": dtb_h, "alog_in": alog_h, "drep_in": drep_h, "nw_in": nw_h,
        "ident_in": ident, "mask_in": mask, "unpi_in": unpi, "rep_in": repm,
        "shc_in": shc_h,
        "pg_in": pgT, "pu_in": puT, "pd_in": pdT,
        "sg_in": sgT, "su_in": suT, "sd_in": sdT,
    }
    z3 = np.zeros((3, HID), np.float32)
    zprev = np.zeros((HID, T + 3), np.float32)
    in_maps = []
    for c in range(8):
        b, h = c // 2, c % 2
        xb = x[b]
        if h == 0:
            xT_c = np.ascontiguousarray(np.concatenate([z3, xb[0:T]], 0).T[pig])
            xp_c = zprev
            fl = np.zeros((128, 1), np.float32)
        else:
            xT_c = np.ascontiguousarray(np.concatenate([xb[T-3:T], xb[T:2*T]], 0).T[pig])
            xp_c = np.ascontiguousarray(np.concatenate([z3, xb[0:T]], 0).T[pig])
            fl = np.ones((128, 1), np.float32)
        sel = np.zeros((128, 8), np.float32)
        if h == 1:
            sel[:, (c - 1) % 8] = 1.0
        m = dict(shared)
        m.update({"xT_in": xT_c, "xprev_in": xp_c, "flag_in": fl, "selc_in": sel})
        in_maps.append(m)

    nc = build()
    res = run_bass_kernel_spmd(nc, in_maps, core_ids=list(range(8)))
    inv = np.argsort(pig)
    out = np.empty((4, 2 * T, HID), np.float32)
    for c in range(8):
        b, h = c // 2, c % 2
        out[b, h * T:(h + 1) * T, :] = res.results[c]["out"]
    return out


# version 17
# speedup vs baseline: 1.0028x; 1.0028x over previous
import sys
sys.path.insert(0, "/opt/trn_rl_repo")
import numpy as np
import concourse.bass as bass
import concourse.mybir as mybir
from concourse.bass_utils import run_bass_kernel_spmd
from contextlib import ExitStack

F32 = mybir.dt.float32
BF16 = mybir.dt.bfloat16
I32 = mybir.dt.int32
AF = mybir.ActivationFunctionType
ALU = mybir.AluOpType

HID = 2048
KCH = 16
NH, HD, NG, NST = 64, 64, 8, 128
INTER = NH * HD
GN = NG * NST
CONVD = INTER + 2 * GN
NCC = CONVD // 128
EPS = 1e-5
T = 2048
NA = T // 512
NMLP = 16384
TB = T // 512
NGF = NMLP // 512

class Sched:
    DMA_Q = {"qsync", "qgps", "qscal"}
    NSLOT = 8
    def __init__(self):
        engs = ["pe", "dve", "act", "pool"]
        qs = ["qsync", "qgps", "qscal"]
        self.items = {k: [] for k in engs + qs}
        self.cnt = {k: 0 for k in engs}
        self.qslot = {q: 0 for q in qs}
        self.qtarget = {}
        self.events = {}
        self.autoctr = 0
        self.pending = {}
        self.semnames = list(engs) + [f"{q}{i}" for q in qs for i in range(self.NSLOT)]
    def add(self, eng, fn, waits=(), signal=None, n_sig=1):
        waits = tuple(w for w in waits if w) + self.pending.pop(eng, ())
        if eng not in self.DMA_Q and signal is None:
            signal = f"__a{self.autoctr}"
            self.autoctr += 1
        if eng in self.DMA_Q:
            if signal:
                slot = self.qslot[eng]
                self.qslot[eng] = (slot + 1) % self.NSLOT
                sn = f"{eng}{slot}"
                fence = self.qtarget.get(sn, 0)
                tgt = fence + 16 * n_sig
                self.qtarget[sn] = tgt
                self.events[signal] = (sn, tgt)
                self.items[eng].append((fn, waits, sn, fence, 16 * n_sig))
            else:
                self.items[eng].append((fn, waits, None, 0, 0))
        else:
            step = n_sig if signal else 0
            self.cnt[eng] += step
            if signal:
                assert signal not in self.events, signal
                self.events[signal] = (eng, self.cnt[eng])
            self.items[eng].append((fn, waits, None, 0, step))
    def emit(self, eng_name, eng_obj, sems):
        is_dma = eng_name in self.DMA_Q
        chain = (not is_dma) and eng_name != "pe"
        cnt_sofar = 0
        for item in self.items[eng_name]:
            fn, waits, sn, fence, step = item
            best = {}
            for w in waits:
                weng, wval = self.events[w]
                if weng == eng_name and not is_dma:
                    if eng_name == "pe":
                        continue
                best[weng] = max(best.get(weng, 0), wval)
            if chain and cnt_sofar > 0:
                best[eng_name] = max(best.get(eng_name, 0), cnt_sofar)
            if is_dma and sn is not None and fence > 0:
                best[sn] = max(best.get(sn, 0), fence)
            cnt_sofar += step
            for weng, wval in best.items():
                eng_obj.wait_ge(sems[weng], wval)
            r = fn(eng_obj)
            if is_dma:
                insts = r if isinstance(r, list) else [r]
                if sn is not None:
                    assert step == 16 * len(insts), (step, len(insts))
                    for i in insts:
                        i.then_inc(sems[sn], 16)
            elif step:
                r.then_inc(sems[eng_name], step)

    def barrier(self, name):
        evs = []
        for eng in ["pe", "dve", "act", "pool"]:
            if self.cnt[eng] > 0:
                self.events[name + "_" + eng] = (eng, self.cnt[eng])
                evs.append(name + "_" + eng)
        for sn, tgt in list(self.qtarget.items()):
            self.events[name + "_" + sn] = (sn, tgt)
            evs.append(name + "_" + sn)
        for k in self.items:
            self.pending[k] = self.pending.get(k, ()) + tuple(evs)

class R:
    def __init__(self, w=None):
        self.w = w
        self.rs = []
    def rd(self, ev):
        waits = [self.w] if self.w else []
        self.rs.append(ev)
        return waits
    def wr(self, ev):
        waits = ([self.w] if self.w else []) + self.rs
        self.w = ev
        self.rs = []
        return waits

def pi128():
    return np.concatenate([np.arange(0, 128, 2), np.arange(1, 128, 2)])

def pi_full(n):
    p = pi128()
    return np.concatenate([c * 128 + p for c in range(n // 128)])

def build():
    nc = bass.Bass("TRN2", target_bir_lowering=False)
    xT_in = nc.dram_tensor("xT_in", [HID, T + 3], F32, kind="ExternalInput").ap()
    xprev_in = nc.dram_tensor("xprev_in", [HID, T + 3], F32, kind="ExternalInput").ap()
    flag_in = nc.dram_tensor("flag_in", [128, 1], F32, kind="ExternalInput").ap()
    wz_in = nc.dram_tensor("wz_in", [HID, INTER], F32, kind="ExternalInput").ap()
    wx_in = nc.dram_tensor("wx_in", [HID, INTER], F32, kind="ExternalInput").ap()
    wb_in = nc.dram_tensor("wb_in", [HID, GN], F32, kind="ExternalInput").ap()
    wc_in = nc.dram_tensor("wc_in", [HID, GN], F32, kind="ExternalInput").ap()
    wdt_in = nc.dram_tensor("wdt_in", [HID, NH], F32, kind="ExternalInput").ap()
    wout_in = nc.dram_tensor("wout_in", [INTER, HID], F32, kind="ExternalInput").ap()
    convw_in = nc.dram_tensor("convw_in", [128, NCC * 4], F32, kind="ExternalInput").ap()
    convb_in = nc.dram_tensor("convb_in", [128, NCC], F32, kind="ExternalInput").ap()
    dtb_in = nc.dram_tensor("dtb_in", [NH, 1], F32, kind="ExternalInput").ap()
    alog_in = nc.dram_tensor("alog_in", [NH, 1], F32, kind="ExternalInput").ap()
    drep_in = nc.dram_tensor("drep_in", [128, 32], F32, kind="ExternalInput").ap()
    nw_in = nc.dram_tensor("nw_in", [128, 32], F32, kind="ExternalInput").ap()
    ident_in = nc.dram_tensor("ident_in", [128, 128], F32, kind="ExternalInput").ap()
    mask_in = nc.dram_tensor("mask_in", [128, 128], BF16, kind="ExternalInput").ap()
    unpi_in = nc.dram_tensor("unpi_in", [128, 128], F32, kind="ExternalInput").ap()
    rep_in = nc.dram_tensor("rep_in", [4, 128], BF16, kind="ExternalInput").ap()
    shc_in = nc.dram_tensor("shc_in", [128, 1], I32, kind="ExternalInput").ap()
    pg_in = nc.dram_tensor("pg_in", [HID // 2, NMLP], I32, kind="ExternalInput").ap()
    pu_in = nc.dram_tensor("pu_in", [HID // 2, NMLP], I32, kind="ExternalInput").ap()
    pd_in = nc.dram_tensor("pd_in", [NMLP // 2, HID], I32, kind="ExternalInput").ap()
    sg_in = nc.dram_tensor("sg_in", [HID // 32, NMLP], F32, kind="ExternalInput").ap()
    su_in = nc.dram_tensor("su_in", [HID // 32, NMLP], F32, kind="ExternalInput").ap()
    sd_in = nc.dram_tensor("sd_in", [NMLP // 32, HID], F32, kind="ExternalInput").ap()
    out_dram = nc.dram_tensor("out", [T, HID], F32, kind="ExternalOutput").ap()

    selc_in = nc.dram_tensor("selc_in", [128, 8], F32, kind="ExternalInput").ap()
    diag_dram = nc.dram_tensor("diag_dram", [NCC, 128, 4, 128], BF16, kind="Internal").ap()
    wallb_dram = nc.dram_tensor("wallb_dram", [81, 128, KCH, 128], BF16, kind="Internal").ap()
    y_dram = nc.dram_tensor("y_dram", [16, 128, 32, 128], BF16, kind="Internal").ap()
    gate_dram = nc.dram_tensor("gate_dram", [4, 128, 32, 512], BF16, kind="Internal").ap()
    mext_dram = nc.dram_tensor("mext_dram", [16, 8, 128, 8, 128], BF16, kind="Internal").ap()
    cexp_dram = nc.dram_tensor("cexp_dram", [16, 64], F32, kind="Internal").ap()
    cc_in = nc.dram_tensor("cc_in", [128, 4096], F32, kind="Internal").ap()
    cc_out = nc.dram_tensor("cc_out", [1024, 4096], F32, kind="Internal", addr_space="Local").ap()
    woutb_dram = nc.dram_tensor("woutb_dram", [KCH, 128, 32, 128], BF16, kind="Internal").ap()
    shuf_dram = nc.dram_tensor("shuf_dram", [NH, 128], F32, kind="Internal").ap()
    attn_dram = nc.dram_tensor("attn_dram", [HID, T], F32, kind="Internal").ap()
    xn_dram = nc.dram_tensor("xn_dram", [HID, T], BF16, kind="Internal").ap()
    acc_dram = nc.dram_tensor("acc_dram", [HID, T], F32, kind="Internal").ap()
    attn_out = attn_dram
    xn_in = xn_dram
    xnr_d = xn_dram.rearrange("(k p) t -> p k t", p=128)
    acc_r = acc_dram.rearrange("(k p) t -> p k t", p=128)

    es = ExitStack()
    ARENA_B = 212824
    arena = es.enter_context(nc.sbuf_tensor("arena", [128, ARENA_B // 2], BF16))

    class Ar:
        def __init__(self):
            self.off = 0
            self.base = 0
        def carve(self, P, freeshape, dt):
            esz = 2 if dt == BF16 else 4
            nb = int(np.prod(freeshape)) * esz
            a = (self.off + 3) // 4 * 4
            self.off = a + nb
            assert self.off <= ARENA_B, ("arena overflow", self.off)
            v = arena[0:P, a // 2:(a + nb) // 2]
            if dt != BF16:
                v = v.bitcast(dt)
            if len(freeshape) == 2:
                v = v.rearrange("p (a b) -> p a b", b=freeshape[1])
            elif len(freeshape) == 3:
                v = v.rearrange("p (a b c) -> p a b c", b=freeshape[1], c=freeshape[2])
            else:
                assert len(freeshape) == 1
            return v
    ar = Ar()
    def sbuf(P, freeshape, dt):
        return ar.carve(P, freeshape, dt)

    ones1 = sbuf(1, [128], F32)
    onescol = sbuf(128, [1], F32)
    onescolb = sbuf(128, [1], BF16)
    ones64 = sbuf(64, [128], F32)
    epscol = sbuf(128, [1], F32)
    flagc = sbuf(128, [1], F32)
    identf = sbuf(128, [128], F32)
    identb = sbuf(128, [128], BF16)
    maskt = sbuf(128, [128], BF16)
    convw = sbuf(128, [NCC * 4], F32)
    convb = sbuf(128, [NCC], F32)
    dtbc = sbuf(NH, [1], F32)
    acol = sbuf(NH, [1], F32)
    drep = sbuf(128, [32], F32)
    nwc = sbuf(128, [32], F32)
    selc = sbuf(128, [8], F32)
    ar.base = ar.off

    S = sbuf(128, [NG, 512], F32)
    Sb = sbuf(128, [512], BF16)
    mk_xc = ar.off
    xc = sbuf(128, [2, 515], F32)
    xnc = sbuf(128, [KCH, 515], BF16)
    xbcp = sbuf(128, [NCC, 515], BF16)
    tailb = sbuf(128, [NCC, 3], BF16)
    convout = xbcp[:, :, 3:515]
    wtile = sbuf(128, [2, KCH, 128], BF16)
    wotile = sbuf(128, [1, 32, 128], BF16)
    dgt = sbuf(128, [2, 4, 128], BF16)
    dstg = wtile[:, 0, 0:4, :]
    xsq = sbuf(128, [2, 512], F32)
    statr = sbuf(1, [5, 512], F32)
    statg = sbuf(1, [2, 128], F32)
    dtp = sbuf(64, [4, 512], F32)
    lc1p = sbuf(1, [1024], F32)
    qtok = sbuf(128, [64], F32)
    wfm = sbuf(64, [128], F32)
    wtok = sbuf(128, [64], F32)
    crow = sbuf(1, [64], F32)
    cexp = sbuf(1, [64], F32)
    qm = sbuf(128, [8, 128], F32)
    ebuf = sbuf(128, [8, 128], BF16)
    eext = sbuf(128, [8, 128], BF16)
    mprime = sbuf(128, [8, 128], BF16)
    mext = sbuf(128, [8, 128], BF16)
    xTt = sbuf(128, [32, 128], BF16)
    bTt = sbuf(128, [8, 128], BF16)
    y_sb = sbuf(128, [32, 128], BF16)
    sgb = sbuf(128, [32, 128], BF16)
    gnb = sbuf(128, [32, 512], BF16)
    normb = gnb
    gate = gnb
    scr = sbuf(128, [512], F32)
    astg = scr

    PST = es.enter_context(nc.psum_tensor("PST", [128, 4096], F32))
    pb = [PST[:, 512 * i:512 * (i + 1)] for i in range(8)]
    def pbf(i):
        return pb[i].bitcast(BF16)

    S3 = S.rearrange("p g x -> p (g x)")
    xT3 = xTt.rearrange("p a b -> p (a b)")
    xTh = xTt.rearrange("p a b -> p (a b)").rearrange("p (h d) -> p h d", d=64)

    prepass = False
    Sq = Sched()

    rWT = [R(), R()]
    rWO = [R(), R()]
    rDG = [R(), R()]
    rPB = [R() for _ in range(8)]
    rXBC = R()
    rCO = rXBC
    rXT = R(); rBT = R(); rQT = R(); rWFM = R(); rWTOK = R(); rCROW = R(); rCEXP = R()
    rQM = R(); rT2M = R(); rE = R(); rEE = R(); rMP = R(); rME = R()
    rYSB = R(); rSGB = R(); rNORM = R(); rS = R(); rSB = R(); rXN = R()
    rLC1 = R(); rSHUF = R(); rSTAT = R(); rDTP = R(); rSCR = R(); rASTG = R(); rXSQ = [R(), R()]
    rXC = [R(), R()]; rTAIL = R(); rGNB = R()

    def c_loads(e):
        return [e.dma_start(out=convw[:], in_=convw_in),
                e.dma_start(out=convb[:], in_=convb_in),
                e.dma_start(out=dtbc[:], in_=dtb_in),
                e.dma_start(out=acol[:], in_=alog_in),
                e.dma_start(out=drep[:], in_=drep_in),
                e.dma_start(out=nwc[:], in_=nw_in),
                e.dma_start(out=identf[:], in_=ident_in),
                e.dma_start(out=maskt[:], in_=mask_in),
                e.dma_start(out=flagc[:], in_=flag_in),
                e.dma_start(out=selc[:], in_=selc_in)]
    Sq.add("qsync", c_loads, signal="consts", n_sig=10)
    CON = ("consts",)
    Sq.add("dve", lambda e: e.memset(ones1[:], 1.0), signal="ones1")
    Sq.add("dve", lambda e: e.memset(onescol[:], 1.0), signal="onescol")
    Sq.add("dve", lambda e: e.memset(onescolb[:], 1.0), signal="onescolb")
    Sq.add("dve", lambda e: e.memset(ones64[:], 1.0), signal="ones64")
    Sq.add("dve", lambda e: e.memset(epscol[:], EPS), signal="epscol")
    Sq.add("dve", lambda e: e.tensor_copy(identb[:], identf[:]), waits=CON, signal="identb")
    Sq.add("act", lambda e: e.activation(out=acol[:], in_=acol[:], func=AF.Exp),
           waits=CON, signal="aexp")
    Sq.add("dve", lambda e: e.tensor_scalar_mul(acol[:], acol[:], -1.0),
           waits=("aexp",), signal="acol")
    Sq.add("dve", lambda e: e.memset(S3, 0.0), signal="sinit")
    rS.w = "sinit"

    for oc in range(KCH):
        def ldw(e, oc=oc):
            return [e.dma_start(out=wotile[:, 0, :, :],
                                in_=wout_in[:, oc*128:(oc+1)*128].rearrange("(k p) c -> p k c", p=128))]
        Sq.add("qgps", ldw, waits=tuple(rWO[0].wr(f"wfld{oc}")) + CON, signal=f"wfld{oc}", n_sig=1)
        def fold(e):
            return e.tensor_tensor(wotile[:, 0, :, :], wotile[:, 0, :, :],
                                   nwc[:].unsqueeze(2).broadcast_to([128, 32, 128]), ALU.mult)
        Sq.add("dve", fold, waits=tuple(rWO[0].wr(f"wfoldc{oc}")) + CON,
               signal=f"wfoldc{oc}")
        Sq.add("qgps", lambda e, oc=oc: [e.dma_start(out=woutb_dram[oc], in_=wotile[:, 0, :, :])],
               waits=tuple(rWO[0].rd(f"wfold{oc}")), signal=f"wfold{oc}", n_sig=1)
    WOUT_READY = (f"wfold{KCH-1}",)
    rDSTG = R()
    for cc in range(NCC):
        def bld(e, cc=cc):
            last = None
            for k in range(4):
                last = e.tensor_scalar(dstg[:, k, :], identb[:], convw[:, cc*4+k:cc*4+k+1],
                                       None, ALU.mult)
            return last
        Sq.add("dve", bld, waits=("identb",) + CON + tuple(rWT[0].wr(f"dgb{cc}")),
               signal=f"dgb{cc}")
        Sq.add("qgps", lambda e, cc=cc: [e.dma_start(out=diag_dram[cc], in_=dstg)],
               waits=tuple(rWT[0].rd(f"dgst{cc}")), signal=f"dgst{cc}", n_sig=1)
    DIAG_READY = (f"dgst{NCC-1}",)

    def wsrc(oc):
        if oc < 32: return wx_in, oc * 128, 128
        if oc < 40: return wb_in, (oc - 32) * 128, 128
        if oc < 48: return wc_in, (oc - 40) * 128, 128
        if oc < 80: return wz_in, (oc - 48) * 128, 128
        return wdt_in, 0, 64

    rFT = [R(), R()]
    for oc in range(81):
        fsrc, fco, fwidth = wsrc(oc)
        fbuf = oc % 2
        def fld(e, fsrc=fsrc, fco=fco, fwidth=fwidth, fbuf=fbuf):
            return [e.dma_start(out=y_sb[:, 16*fbuf:16*fbuf+16, 0:fwidth],
                                in_=fsrc[:, fco:fco+fwidth].rearrange("(k p) c -> p k c", p=128))]
        Sq.add("qgps", fld, waits=tuple(rFT[fbuf].wr(f"wfl{oc}")), signal=f"wfl{oc}", n_sig=1)
        def fst(e, oc=oc, fwidth=fwidth, fbuf=fbuf):
            return [e.dma_start(out=wallb_dram[oc][:, :, 0:fwidth],
                                in_=y_sb[:, 16*fbuf:16*fbuf+16, 0:fwidth])]
        Sq.add("qsync", fst, waits=(f"wfl{oc}",), signal=f"wfo{oc}", n_sig=1)
        rFT[fbuf].rd(f"wfo{oc}")
    rYSB.w = "wfo80"

    passes = ([("p", xprev_in, True)] if prepass else []) + [("m", xT_in, False)]
    steps = [(pfx, src, so, ia) for (pfx, src, so) in passes for ia in range(NA)]
    for (PFX, xsrc, SO, ia) in steps:
        A = PFX + f"a{ia}"
        ofs = 3 + 512 * ia
        NCC_P = 40 if SO else NCC
        if PFX == "m" and ia == 0 and prepass:
            Sq.add("dve", lambda e: e.tensor_scalar(S3, S3, flagc[:], None, ALU.mult),
                   waits=tuple(rS.wr("flagS")) + CON, signal="flagS")

        widths = ([(0, 3)] if ia == 0 else []) + [(3, 515)]
        for (c0, c1) in widths:
            W = c1 - c0
            tw = A + f"w{c0}"
            for k in range(KCH):
                si = k % 2
                def xld(e, k=k, si=si, c0=c0, c1=c1, ia=ia, xsrc=xsrc):
                    lo = 512 * ia + c0
                    return [e.dma_start(out=xc[:, si, 0:c1-c0],
                                        in_=xsrc[128*k:128*(k+1), lo:lo+(c1-c0)])]
                Sq.add("qsync", xld, waits=tuple(rXC[si].wr(tw + f"xl{k}")) + CON,
                       signal=tw + f"xl{k}", n_sig=1)
                Sq.add("act", lambda e, si=si, W=W: e.activation(
                        out=xsq[:, si, 0:W], in_=xc[:, si, 0:W], func=AF.Square),
                       waits=(tw + f"xl{k}",) + tuple(rXSQ[si].wr(tw + f"sq{k}")),
                       signal=tw + f"sq{k}")
                rXC[si].rd(tw + f"sq{k}")
                Sq.add("pe", lambda e, si=si, k=k, W=W: e.matmul(
                        pb[0][0:1, 0:W], onescol[:], xc[:, si, 0:W],
                        start=(k == 0), stop=(k == KCH - 1)),
                       waits=(tw + f"xl{k}", "onescol") +
                             (tuple(rPB[0].wr(tw + "smm")) if k == 0 else ()),
                       signal=(tw + "smm") if k == KCH - 1 else tw + f"sm{k}")
                rXC[si].rd(tw + ("smm" if k == KCH - 1 else f"sm{k}"))
                Sq.add("pe", lambda e, si=si, k=k, W=W: e.matmul(
                        pb[1][0:1, 0:W], onescol[:], xsq[:, si, 0:W],
                        start=(k == 0), stop=(k == KCH - 1)),
                       waits=(tw + f"sq{k}",) + (tuple(rPB[1].wr(tw + "qmm")) if k == 0 else ()),
                       signal=(tw + "qmm") if k == KCH - 1 else tw + f"qm{k}")
                rXSQ[si].rd(tw + ("qmm" if k == KCH - 1 else f"qm{k}"))
            st = statr[0:1]
            Sq.add("dve", lambda e, st=st, W=W: e.tensor_scalar_mul(
                    st[:, 0, 0:W], pb[0][0:1, 0:W], 1.0 / HID),
                   waits=(tw + "smm",) + tuple(rSTAT.wr(tw + "st1")), signal=tw + "st1")
            rPB[0].rd(tw + "st1")
            Sq.add("dve", lambda e, st=st, W=W: e.tensor_scalar_mul(
                    st[:, 1, 0:W], pb[1][0:1, 0:W], 1.0 / HID),
                   waits=(tw + "qmm",), signal=tw + "st2")
            rPB[1].rd(tw + "st2")
            Sq.add("dve", lambda e, st=st, W=W: e.tensor_tensor(
                    st[:, 2, 0:W], st[:, 0, 0:W], st[:, 0, 0:W], ALU.mult))
            Sq.add("dve", lambda e, st=st, W=W: e.tensor_tensor(
                    st[:, 2, 0:W], st[:, 1, 0:W], st[:, 2, 0:W], ALU.subtract),
                   signal=tw + "var")
            Sq.add("act", lambda e, st=st, W=W: e.activation(
                    out=st[:, 3, 0:W], in_=st[:, 2, 0:W], func=AF.Sqrt, bias=epscol[0:1]),
                   waits=(tw + "var", "epscol"), signal=tw + "sqr")
            Sq.add("dve", lambda e, st=st, W=W: e.reciprocal(st[:, 3, 0:W], st[:, 3, 0:W]),
                   waits=(tw + "sqr",))
            Sq.add("dve", lambda e, st=st, W=W: e.tensor_tensor(
                    st[:, 4, 0:W], st[:, 0, 0:W], st[:, 3, 0:W], ALU.mult),
                   signal=tw + "mr")
            rSTAT.rd(tw + "mr")
            Sq.add("pe", lambda e, st=st, W=W: e.matmul(
                    pb[2][:, 0:W], ones1[:], st[:, 3, 0:W], start=True, stop=True,
                    skip_group_check=True),
                   waits=(tw + "mr", "ones1") + tuple(rPB[2].wr(tw + "bca")),
                   signal=tw + "bca")
            rSTAT.rd(tw + "bca")
            Sq.add("pe", lambda e, st=st, W=W: e.matmul(
                    pb[3][:, 0:W], ones1[:], st[:, 4, 0:W], start=True, stop=True,
                    skip_group_check=True),
                   waits=(tw + "mr",) + tuple(rPB[3].wr(tw + "bcb")),
                   signal=tw + "bcb")
            rSTAT.rd(tw + "bcb")
            for k in range(KCH):
                si = k % 2
                def xld2(e, k=k, si=si, c0=c0, c1=c1, ia=ia, xsrc=xsrc):
                    lo = 512 * ia + c0
                    return [e.dma_start(out=xc[:, si, 0:c1-c0],
                                        in_=xsrc[128*k:128*(k+1), lo:lo+(c1-c0)])]
                Sq.add("qsync", xld2, waits=tuple(rXC[si].wr(tw + f"x2l{k}")),
                       signal=tw + f"x2l{k}", n_sig=1)
                Sq.add("dve", lambda e, si=si, c0=c0, c1=c1, W=W: e.tensor_tensor(
                        xc[:, si, 0:W], xc[:, si, 0:W],
                        pb[2][:, 0:W], ALU.mult),
                       waits=(tw + f"x2l{k}", tw + "bca") + tuple(rXC[si].wr(tw + f"xs{k}")),
                       signal=tw + f"xs{k}")
                rPB[2].rd(tw + f"xs{k}")
                Sq.add("dve", lambda e, si=si, k=k, c0=c0, c1=c1, W=W: e.tensor_tensor(
                        xnc[:, k, c0:c1], xc[:, si, 0:W],
                        pb[3][:, 0:W], ALU.subtract),
                       waits=(tw + "bcb",) + (tuple(rXN.wr(tw + "xn0")) if k == 0 else ()),
                       signal=(tw + "xn") if k == KCH - 1 else tw + f"xn{k}")
                rXC[si].rd(tw + ("xn" if k == KCH - 1 else f"xn{k}"))
                rPB[3].rd(tw + ("xn" if k == KCH - 1 else f"xn{k}"))
                XN = (A + "w3xn",) + ((A + "w0xn",) if ia == 0 else ())

        if not SO:
            Sq.add("qgps", lambda e, ia=ia: [e.dma_start(
                    out=xnr_d[:, :, ia*512:(ia+1)*512], in_=xnc[:, :, 3:515])],
                   waits=XN, signal=A + "xnst", n_sig=1)
            rXN.rd(A + "xnst")
        if ia > 0:
            Sq.add("dve", lambda e, NCC_P=NCC_P: e.tensor_copy(
                    xbcp[:, 0:NCC_P, 0:3], tailb[:, 0:NCC_P, :]),
                   waits=tuple(rXBC.wr(A + "tail")) + tuple(rTAIL.rd(A + "tail")),
                   signal=A + "tail")
            halo_done = (A + "tail",)

        def proj(oc, tag, n_cols, out_fn, evac_engine, evac_extra_waits=()):
            src_, co, width = wsrc(oc)
            buf = proj.bufctr % 2
            proj.bufctr += 1
            pi = 4 + (proj.psctr % 2)
            proj.psctr += 1
            tg = A + tag
            def ldw(e, oc=oc, width=width, buf=buf):
                return [e.dma_start(out=wtile[:, buf, :, 0:width],
                                    in_=wallb_dram[oc][:, :, 0:width])]
            Sq.add("qgps", ldw, waits=(f"wfo{oc}",) + tuple(rWT[buf].wr(tg + "ld")),
                   signal=tg + "ld", n_sig=1)
            def mms(e, width=width, buf=buf, pi=pi, n_cols=n_cols):
                last = None
                for k in range(KCH):
                    last = e.matmul(pb[pi][0:width, 0:n_cols], wtile[:, buf, k, 0:width],
                                    xnc[:, k, 515-n_cols:515] if n_cols == 512 else xnc[:, k, 0:3],
                                    start=(k == 0), stop=(k == KCH - 1))
                return last
            Sq.add("pe", mms, waits=(tg + "ld",) + XN + tuple(rPB[pi].wr(tg + "mm")),
                   signal=tg + "mm")
            rWT[buf].rd(tg + "mm")
            rXN.rd(tg + "mm")
            ew = (tg + "mm",) + evac_extra_waits + tuple(rPB[pi].rd(tg + "ev"))
            Sq.add(evac_engine, lambda e, pi=pi: out_fn(e, pb[pi]), waits=ew, signal=tg + "ev")
            return tg
        proj.bufctr = 0
        proj.psctr = 0
        if ia > 0:
            rPB[4].w = rPB[4].w

        for oc in range(NCC_P):
            def ev(e, ps, oc=oc):
                return e.activation(out=xbcp[:, oc, 3:515], in_=ps[:], func=AF.Copy)
            tg = proj(oc, f"px{oc}", 512, ev, "act",
                      evac_extra_waits=tuple(rXBC.wr(A + f"px{oc}ev")) +
                                       ((A + "tail",) if ia > 0 else ()))
        if ia == 0:
            for oc in range(NCC_P):
                def ev(e, ps, oc=oc):
                    return e.activation(out=xbcp[:, oc, 0:3], in_=ps[:, 0:3], func=AF.Copy)
                proj(oc, f"ph{oc}", 3, ev, "act")
            halo_done = (A + f"ph{NCC_P-1}ev",)
        HALO = halo_done

        Sq.add("dve", lambda e, NCC_P=NCC_P: e.tensor_copy(
                tailb[:, 0:NCC_P, :], xbcp[:, 0:NCC_P, 512:515]),
               waits=(A + f"px{NCC_P-1}ev",) + HALO + tuple(rTAIL.wr(A + "tsv")) +
                     tuple(rXBC.rd(A + "tsv")),
               signal=A + "tsv")
        for cc in range(NCC_P):
            tg = A + f"cv{cc}"
            buf = cc % 2
            Sq.add("qsync", lambda e, cc=cc, buf=buf: [e.dma_start(out=dgt[:, buf, :, :], in_=diag_dram[cc])],
                   waits=DIAG_READY + tuple(rDG[buf].wr(tg + "ld")), signal=tg + "ld", n_sig=1)
            def cmm(e, cc=cc, buf=buf):
                last = None
                for k in range(4):
                    last = e.matmul(pb[6][:], dgt[:, buf, k, :], xbcp[:, cc, k:k+512],
                                    start=(k == 0), stop=(k == 3))
                return last
            Sq.add("pe", cmm, waits=(tg + "ld", A + f"px{cc}ev") + HALO +
                             tuple(rPB[6].wr(tg + "mm")), signal=tg + "mm")
            rDG[buf].rd(tg + "mm")
            rXBC.rd(tg + "mm")
            Sq.add("act", lambda e, cc=cc: e.activation(
                    out=convout[:, cc, :], in_=pb[6][:], func=AF.Silu, bias=convb[:, cc:cc+1]),
                   waits=(tg + "mm", A + "tsv") + tuple(rPB[6].rd(tg + "ev")) +
                         tuple(rCO.wr(tg + "ev")),
                   signal=tg + "ev")
        CONVDONE = (A + f"cv{NCC_P-1}ev",)

        if not SO:
            for oc in range(48, 80):
                def ev(e, ps, oc=oc):
                    return e.activation(out=gate[:, oc - 48, :], in_=ps[:], func=AF.Copy)
                proj(oc, f"pz{oc}", 512, ev, "act",
                     evac_extra_waits=tuple(rGNB.wr(A + f"pz{oc}ev")))
            GATE_RDY = (A + "pz79ev",)
            Sq.add("qgps", lambda e, ia=ia: [e.dma_start(out=gate_dram[ia], in_=gnb[:])],
                   waits=GATE_RDY, signal=A + "gst", n_sig=1)
            rGNB.rd(A + "gst")
        def evdt(e, ps):
            return e.tensor_scalar(dtp[:, 0, :], ps[0:64, :], dtbc[:], None, ALU.add)
        proj(80, "pdt", 512, evdt, "dve",
             evac_extra_waits=tuple(rDTP.wr(A + "pdtev")) + CON)

        Sq.add("act", lambda e: e.activation(out=dtp[:, 1, :], in_=dtp[:, 0, :], func=AF.Abs),
               waits=(A + "pdtev",), signal=A + "dtab")
        Sq.add("act", lambda e: e.activation(out=dtp[:, 2, :], in_=dtp[:, 1, :], func=AF.Exp, scale=-1.0),
               waits=(A + "dtab",), signal=A + "dte")
        Sq.add("act", lambda e: e.activation(out=dtp[:, 2, :], in_=dtp[:, 2, :], func=AF.Ln, bias=1.0),
               signal=A + "dtl")
        Sq.add("dve", lambda e: e.tensor_scalar(dtp[:, 1, :], dtp[:, 0, :], 0.0, None, ALU.max),
               waits=(A + "dte",), signal=A + "dtr")
        Sq.add("dve", lambda e: e.tensor_tensor(dtp[:, 3, :], dtp[:, 1, :], dtp[:, 2, :], ALU.add),
               waits=(A + "dtl",), signal=A + "dtv")
        Sq.add("act", lambda e: e.activation(out=dtp[:, 0, :], in_=dtp[:, 3, :], func=AF.Ln),
               waits=(A + "dtv",), signal=A + "dtln")
        Sq.add("dve", lambda e: e.tensor_scalar(dtp[:, 1, :], dtp[:, 3, :], acol[:], None, ALU.mult),
               waits=(A + "dtv", "acol"), signal=A + "dta")
        for il in range(4):
            Sq.add("dve", lambda e, il=il: e.tensor_tensor_scan(
                    dtp[:, 2, il*128:(il+1)*128], ones64[:], dtp[:, 1, il*128:(il+1)*128],
                    0.0, ALU.mult, ALU.add),
                   waits=(A + "dta", "ones64"), signal=A + f"lcs{il}")
        Sq.add("dve", lambda e: e.tensor_tensor(dtp[:, 3, :], dtp[:, 0, :], dtp[:, 2, :], ALU.subtract),
               waits=(A + "dtln", A + "lcs3"), signal=A + "q")
        rDTP.rd(A + "q")

        for il in range(4):
            L = A + f"l{il}"
            lc0 = il * 128
            for w in range(10):
                is_b = w >= 8
                def tmm(e, w=w, lc0=lc0, is_b=is_b):
                    last = None
                    for j in range(4):
                        cc = (w * 4 + j) if not is_b else (32 + (w - 8) * 4 + j)
                        last = e.transpose(pbf(7)[:, j*128:(j+1)*128],
                                           convout[:, cc, lc0:lc0+128], identb[:])
                    return last
                Sq.add("pe", tmm, waits=CONVDONE + ("identb",) + tuple(rPB[7].wr(L + f"tp{w}")),
                       signal=L + f"tp{w}")
                rCO.rd(L + f"tp{w}")
                if not is_b:
                    Sq.add("dve", lambda e, w=w: e.tensor_copy(xTt[:, w*4:(w+1)*4, :],
                            pbf(7)[:, 0:512].rearrange("p (a b) -> p a b", b=128)),
                           waits=(L + f"tp{w}",) + (tuple(rXT.wr(L + "tpe0")) if w == 0 else ()) +
                                 tuple(rPB[7].rd(L + f"tpe{w}")),
                           signal=L + f"tpe{w}")
                else:
                    Sq.add("dve", lambda e, w=w: e.tensor_copy(bTt[:, (w-8)*4:(w-7)*4, :],
                            pbf(7)[:, 0:512].rearrange("p (a b) -> p a b", b=128)),
                           waits=(L + f"tp{w}",) + (tuple(rBT.wr(L + "tpe8")) if w == 8 else ()) +
                                 tuple(rPB[7].rd(L + f"tpe{w}")),
                           signal=L + f"tpe{w}")
            if not SO:
                Sq.add("pe", lambda e, lc0=lc0: e.transpose(pb[7][:, 0:64],
                        dtp[:, 3, lc0:lc0+128], identf[0:64, 0:64]),
                       waits=(A + "q",) + tuple(rPB[7].wr(L + "qtm")), signal=L + "qtm")
                rDTP.rd(L + "qtm")
                Sq.add("dve", lambda e: e.tensor_copy(qtok[:], pb[7][:, 0:64]),
                       waits=(L + "qtm",) + tuple(rQT.wr(L + "qte")) + tuple(rPB[7].rd(L + "qte")),
                       signal=L + "qte")
                Sq.add("qgps", lambda e, lc0=lc0: [e.dma_start(out=shuf_dram, in_=dtp[:, 2, lc0:lc0+128])],
                       waits=(A + "q",) + tuple(rSHUF.wr(L + "shst")), signal=L + "shst", n_sig=1)
                rDTP.rd(L + "shst")
            Sq.add("pe", lambda e, lc0=lc0: e.transpose(pb[7][0:1, 0:64],
                    dtp[:, 2, lc0+127:lc0+128], identf[0:64, 0:64]),
                   waits=(A + "q",) + tuple(rPB[7].wr(L + "crm")), signal=L + "crm")
            rDTP.rd(L + "crm")
            Sq.add("dve", lambda e: e.tensor_copy(crow[:], pb[7][0:1, 0:64]),
                   waits=(L + "crm",) + tuple(rCROW.wr(L + "cre")) + tuple(rPB[7].rd(L + "cre")),
                   signal=L + "cre")
            Sq.add("act", lambda e: e.activation(out=cexp[:], in_=crow[:], func=AF.Exp),
                   waits=(L + "cre",) + tuple(rCEXP.wr(L + "cex")), signal=L + "cex")
            rCROW.rd(L + "cex")
            if not SO:
                Sq.add("qgps", lambda e, l4=4*ia+il: [e.dma_start(
                        out=cexp_dram[l4:l4+1, :], in_=cexp[:])],
                       waits=(L + "cex",), signal=L + "cxst", n_sig=1)
                rCEXP.rd(L + "cxst")
            Sq.add("pe", lambda e: e.matmul(pb[7][:, 0:64], ones1[:], cexp[:],
                    start=True, stop=True, skip_group_check=True),
                   waits=(L + "cex", "ones1") + tuple(rPB[7].wr(L + "cbc")), signal=L + "cbc")
            rCEXP.rd(L + "cbc")
            Sq.add("act", lambda e, lc0=lc0: e.activation(out=wfm[:], in_=dtp[:, 3, lc0:lc0+128],
                    func=AF.Exp, bias=dtp[:, 2, lc0+127:lc0+128]),
                   waits=(A + "q",) + tuple(rWFM.wr(L + "wfm")), signal=L + "wfm")
            rDTP.rd(L + "wfm")
            Sq.add("pe", lambda e: e.transpose(pb[6][:, 0:64], wfm[:], identf[0:64, 0:64]),
                   waits=(L + "wfm",) + tuple(rPB[6].wr(L + "wtm")), signal=L + "wtm")
            rWFM.rd(L + "wtm")
            Sq.add("dve", lambda e: e.tensor_copy(wtok[:], pb[6][:, 0:64]),
                   waits=(L + "wtm",) + tuple(rWTOK.wr(L + "wte")) + tuple(rPB[6].rd(L + "wte")),
                   signal=L + "wte")

            for g in (range(0) if SO else range(NG)):
                G = L + f"g{g}"
                Sq.add("qgps", lambda e, g=g: [e.dma_start(out=lc1p[:],
                        in_=shuf_dram[8*g:8*g+8, :].rearrange("h t -> (h t)").unsqueeze(0))],
                       waits=(L + "shst",) + tuple(rLC1.wr(G + "shq")), signal=G + "shq", n_sig=1)
                def t1mm(e):
                    e.matmul(pb[0][:], ones1[:], lc1p[:, 0:512],
                             start=True, stop=True, skip_group_check=True)
                    return e.matmul(pb[1][:], ones1[:], lc1p[:, 512:1024],
                                    start=True, stop=True, skip_group_check=True)
                Sq.add("pe", t1mm, waits=(G + "shq", "ones1") + tuple(rPB[0].wr(G + "t1")) +
                               tuple(rPB[1].wr(G + "t1")), signal=G + "t1")
                rLC1.rd(G + "t1")
                Sq.add("dve", lambda e, g=g: e.tensor_tensor(
                        qm[:], qtok[:, 8*g:8*g+8].unsqueeze(2).broadcast_to([128, 8, 128]),
                        maskt[:].unsqueeze(1).broadcast_to([128, 8, 128]), ALU.add),
                       waits=(L + "qte",) + CON + tuple(rQM.wr(G + "qm")), signal=G + "qm")
                rQT.rd(G + "qm")
                def t2madd(e):
                    e.tensor_tensor(qm[:, 0:4, :], qm[:, 0:4, :],
                                    pb[0][:].rearrange("p (a b) -> p a b", b=128), ALU.add)
                    return e.tensor_tensor(qm[:, 4:8, :], qm[:, 4:8, :],
                                           pb[1][:].rearrange("p (a b) -> p a b", b=128), ALU.add)
                Sq.add("dve", t2madd, waits=(G + "qm", G + "t1"), signal=G + "t2m")
                Sq.add("act", lambda e: e.activation(out=ebuf[:], in_=qm[:], func=AF.Exp),
                       waits=(G + "t2m",) + tuple(rE.wr(G + "e")), signal=G + "e")
                rQM.rd(G + "e")
                def eexta(e):
                    e.activation(out=eext[:, 0:4, :], in_=pb[0][:], func=AF.Exp)
                    return e.activation(out=eext[:, 4:8, :], in_=pb[1][:], func=AF.Exp)
                Sq.add("act", eexta, waits=(G + "t1",) + tuple(rEE.wr(G + "ee")), signal=G + "ee")
                rPB[0].rd(G + "ee"); rPB[1].rd(G + "ee")
                rPB[0].rd(G + "t2m"); rPB[1].rd(G + "t2m")
                Sq.add("pe", lambda e, g=g, lc0=lc0: e.matmul(
                        pb[6][:, 0:128], convout[:, 32+g, lc0:lc0+128],
                        convout[:, 40+g, lc0:lc0+128], start=True, stop=True,
                        skip_group_check=True),
                       waits=CONVDONE + tuple(rPB[6].wr(G + "gp")), signal=G + "gp")
                rCO.rd(G + "gp")
                Sq.add("dve", lambda e: e.tensor_tensor(
                        mprime[:], ebuf[:], pb[6][:, 0:128].unsqueeze(1).broadcast_to([128, 8, 128]),
                        ALU.mult),
                       waits=(G + "e", G + "gp") + tuple(rMP.wr(G + "mp")), signal=G + "mp")
                rE.rd(G + "mp"); rPB[6].rd(G + "mp")
                Sq.add("dve", lambda e, g=g, lc0=lc0: e.tensor_tensor(
                        mext[:], eext[:],
                        convout[:, 40+g, lc0:lc0+128].unsqueeze(1).broadcast_to([128, 8, 128]),
                        ALU.mult),
                       waits=(G + "ee",) + CONVDONE + tuple(rME.wr(G + "me")), signal=G + "me")
                rEE.rd(G + "me"); rCO.rd(G + "me")
                Sq.add("qgps", lambda e, g=g, l4=4*ia+il: [e.dma_start(
                        out=mext_dram[l4, g], in_=mext[:])],
                       waits=(G + "me",), signal=G + "mst", n_sig=1)
                rME.rd(G + "mst")
                Sq.add("dve", lambda e, g=g: e.tensor_copy(Sb[:], S[:, g, :]),
                       waits=tuple(rS.rd(G + "sb")) + tuple(rSB.wr(G + "sb")), signal=G + "sb")
                def ymm(e, g=g):
                    last = None
                    for cc4 in range(4):
                        h0, h1 = 2 * cc4, 2 * cc4 + 1
                        cc = 4 * g + cc4
                        e.matmul(pb[2][0:64, cc4*128:(cc4+1)*128], xTt[:, cc, 0:64],
                                 mprime[:, h0, :], start=True, stop=False, skip_group_check=True)
                        e.matmul(pb[2][0:64, cc4*128:(cc4+1)*128], Sb[:, h0*64:(h0+1)*64],
                                 mext[:, h0, :], start=False, stop=True, skip_group_check=True)
                        e.matmul(pb[2][64:128, cc4*128:(cc4+1)*128], xTt[:, cc, 64:128],
                                 mprime[:, h1, :], start=True, stop=False,
                                 tile_position=(0, 64), skip_group_check=True)
                        last = e.matmul(pb[2][64:128, cc4*128:(cc4+1)*128], Sb[:, h1*64:(h1+1)*64],
                                        mext[:, h1, :], start=False, stop=True,
                                        tile_position=(0, 64), skip_group_check=True)
                    return last
                Sq.add("pe", ymm, waits=(G + "mp", G + "me", L + "tpe7", G + "sb") +
                               tuple(rPB[2].wr(G + "ymm")), signal=G + "ymm")
                rMP.rd(G + "ymm"); rME.rd(G + "ymm"); rXT.rd(G + "ymm"); rSB.rd(G + "ymm")
                def yev(e, g=g, lc0=lc0):
                    last = None
                    for cc4 in range(4):
                        cc = 4 * g + cc4
                        last = e.scalar_tensor_tensor(
                            y_sb[:, cc, :], convout[:, cc, lc0:lc0+128], drep[:, cc:cc+1],
                            pb[2][:, cc4*128:(cc4+1)*128], ALU.mult, ALU.add)
                    return last
                Sq.add("dve", yev, waits=(G + "ymm",) + CON +
                               (tuple(rYSB.wr(G + "yev")) if g == 0 else ()),
                       signal=G + "yev")
                rPB[2].rd(G + "yev"); rCO.rd(G + "yev")
            Sq.add("dve", lambda e: e.tensor_tensor(
                    xTh, xTh, wtok[:].unsqueeze(2).broadcast_to([128, 64, 64]), ALU.mult),
                   waits=(L + "wte",) + tuple(rXT.wr(L + "xscl")), signal=L + "xscl")
            rWTOK.rd(L + "xscl")
            for g in range(NG):
                def dsmm(e, g=g):
                    return e.matmul(pb[3][:], bTt[:, g, :], xT3[:, 512*g:512*(g+1)],
                                    start=True, stop=True, skip_group_check=True)
                Sq.add("pe", dsmm, waits=(L + "xscl", L + "tpe9") + tuple(rPB[3].wr(L + f"ds{g}")),
                       signal=L + f"ds{g}")
                rBT.rd(L + f"ds{g}"); rXT.rd(L + f"ds{g}")
                def supd(e, g=g):
                    e.tensor_tensor(scr[:].rearrange("p (a b) -> p a b", b=64),
                                    S[:, g, :].rearrange("p (a b) -> p a b", b=64),
                                    pb[7][:, 8*g:8*g+8].unsqueeze(2).broadcast_to([128, 8, 64]),
                                    ALU.mult)
                    return e.tensor_tensor(S[:, g, :], scr[:], pb[3][:], ALU.add)
                Sq.add("dve", supd, waits=(L + f"ds{g}", L + "cbc") +
                               (tuple(rS.wr(L + "supd0")) if g == 0 else ()) +
                               (tuple(rSCR.wr(L + "supd0")) if g == 0 else ()),
                       signal=L + f"supd{g}")
                rPB[3].rd(L + f"supd{g}"); rPB[7].rd(L + f"supd{g}")
            SUPD = (L + f"supd{NG-1}",)
            rS.w = L + f"supd{NG-1}"
            if not SO:
                Sq.add("qgps", lambda e, l4=4*ia+il: [e.dma_start(
                        out=y_dram[l4], in_=y_sb[:])],
                       waits=(L + f"g{NG-1}yev",), signal=L + "yst", n_sig=1)
                rYSB.rd(L + "yst")


    Sq.add("qgps", lambda e: [e.dma_start(out=cc_in, in_=S3)],
           waits=tuple(rS.rd("ccst")), signal="ccst", n_sig=1)
    Sq.semnames.append("ccv")
    def ccfn(e):
        e.collective_compute(
            "AllGather", ALU.bypass, ins=[cc_in], outs=[cc_out],
            replica_groups=[[0, 1, 2, 3, 4, 5, 6, 7]]).then_inc(sems["ccv"], 1)
        return []
    Sq.add("qgps", ccfn, waits=("ccst",), signal=None)
    Sq.events["ccg"] = ("ccv", 1)
    ar.off = mk_xc
    gtile = sbuf(128, [2, 4096], F32)
    wot2 = sbuf(128, [2, 32, 128], BF16)
    rGT = [R(), R()]
    for k in range(8):
        buf = k % 2
        Sq.add("qsync", lambda e, k=k, buf=buf: [e.dma_start(
                out=gtile[:, buf, :], in_=cc_out[128*k:128*(k+1), :])],
               waits=("ccg",) + tuple(rGT[buf].wr(f"gld{k}")), signal=f"gld{k}", n_sig=1)
        if k == 0:
            Sq.add("dve", lambda e: e.tensor_scalar(S3, gtile[:, 0, :], selc[:, 0:1],
                                                    None, ALU.mult),
                   waits=("gld0",) + CON + tuple(rS.wr("pac0")), signal="pac0")
            rGT[0].rd("pac0")
        else:
            Sq.add("dve", lambda e, k=k, buf=buf: e.tensor_scalar(
                    gtile[:, buf, :], gtile[:, buf, :], selc[:, k:k+1], None, ALU.mult),
                   waits=(f"gld{k}",), signal=f"gsc{k}")
            Sq.add("dve", lambda e, buf=buf: e.tensor_tensor(S3, S3, gtile[:, buf, :],
                                                             ALU.add),
                   signal=f"pac{k}")
            rGT[buf].rd(f"pac{k}")
    rS.w = "pac7"

    rY2 = R(); rG2 = R(); rSB2 = R(); rCX2 = R(); rSG2 = R(); rST2 = R()
    rMB = [R(), R()]
    rPB2 = [R() for _ in range(8)]
    rWO2 = [R(), R()]; rSCR2 = R()
    mbuf = [mext, mprime]
    for l in range(16):
        N = f"n{l}"
        M1 = f"ma{l // 4}l{l % 4}"
        lc0 = (l % 4) * 128
        Sq.add("qsync", lambda e, l=l: [e.dma_start(out=y_sb[:], in_=y_dram[l])],
               waits=(M1 + "yst",) + tuple(rY2.wr(N + "yld")), signal=N + "yld", n_sig=1)
        Sq.add("qsync", lambda e, l=l: [e.dma_start(
                out=xTt[:], in_=gate_dram[l // 4][:, :, (l % 4)*128:(l % 4)*128+128])],
               waits=(f"ma{l // 4}gst",) + tuple(rG2.wr(N + "gld")), signal=N + "gld", n_sig=1)
        Sq.add("qsync", lambda e, l=l: [e.dma_start(out=cexp[:], in_=cexp_dram[l:l+1, :])],
               waits=(M1 + "cxst",) + tuple(rCX2.wr(N + "cxl")), signal=N + "cxl", n_sig=1)
        Sq.add("pe", lambda e: e.matmul(pb[7][:, 0:64], ones1[:], cexp[:],
                start=True, stop=True, skip_group_check=True),
               waits=(N + "cxl", "ones1") + tuple(rPB2[7].wr(N + "cbc")), signal=N + "cbc")
        rCX2.rd(N + "cbc")
        for g in range(NG):
            G = N + f"g{g}"
            mb = mbuf[g % 2]
            Sq.add("qsync", lambda e, l=l, g=g, mb=mb: [e.dma_start(
                    out=mb[:], in_=mext_dram[l, g])],
                   waits=(M1 + f"g{g}mst",) + tuple(rMB[g % 2].wr(G + "ml")),
                   signal=G + "ml", n_sig=1)
            Sq.add("dve", lambda e, g=g: e.tensor_copy(Sb[:], S[:, g, :]),
                   waits=tuple(rS.rd(G + "pb")) + tuple(rSB2.wr(G + "pb")),
                   signal=G + "pb")
            dpi = 2 + (l * NG + g) % 2
            def dymm(e, mb=mb, dpi=dpi):
                last = None
                for cc4 in range(4):
                    h0, h1 = 2 * cc4, 2 * cc4 + 1
                    e.matmul(pb[dpi][0:64, cc4*128:(cc4+1)*128], Sb[:, h0*64:(h0+1)*64],
                             mb[:, h0, :], start=True, stop=True, skip_group_check=True)
                    last = e.matmul(pb[dpi][64:128, cc4*128:(cc4+1)*128],
                                    Sb[:, h1*64:(h1+1)*64], mb[:, h1, :],
                                    start=True, stop=True, tile_position=(0, 64),
                                    skip_group_check=True)
                return last
            Sq.add("pe", dymm, waits=(G + "ml", G + "pb") + tuple(rPB2[dpi].wr(G + "dy")),
                   signal=G + "dy")
            rSB2.rd(G + "dy"); rMB[g % 2].rd(G + "dy")
            Sq.add("dve", lambda e, g=g, dpi=dpi: e.tensor_tensor(
                    y_sb[:, 4*g:4*g+4, :], y_sb[:, 4*g:4*g+4, :],
                    pb[dpi][:].rearrange("p (a b) -> p a b", b=128), ALU.add),
                   waits=(G + "dy", N + "yld"), signal=G + "ya")
            rPB2[dpi].rd(G + "ya")
            Sq.add("dve", lambda e, g=g: e.tensor_tensor(
                    S[:, g, :].rearrange("p (a b) -> p a b", b=64),
                    S[:, g, :].rearrange("p (a b) -> p a b", b=64),
                    pb[7][:, 8*g:8*g+8].unsqueeze(2).broadcast_to([128, 8, 64]),
                    ALU.mult),
                   waits=(N + "cbc",) + tuple(rS.wr(G + "pu")), signal=G + "pu")
            rPB2[7].rd(G + "pu")
        Sq.add("act", lambda e: e.activation(out=sgb[:], in_=xTt[:], func=AF.Silu),
               waits=(N + "gld",) + tuple(rSG2.wr(N + "sg")), signal=N + "sg")
        rG2.rd(N + "sg")
        Sq.add("dve", lambda e: e.tensor_tensor(sgb[:], y_sb[:], sgb[:], ALU.mult),
               waits=(N + "sg", N + f"g{NG-1}ya"), signal=N + "gmul")
        Sq.add("act", lambda e: e.activation(out=y_sb[:], in_=sgb[:], func=AF.Square),
               waits=(N + "gmul",), signal=N + "gsq")
        def gsmm2(e):
            last = None
            for cc in range(32):
                last = e.matmul(pb[6][0:1, 0:128], onescolb[:], y_sb[:, cc, :],
                                start=(cc == 0), stop=(cc == 31))
            return last
        Sq.add("pe", gsmm2, waits=(N + "gsq", "onescolb") + tuple(rPB2[6].wr(N + "gsmm")),
               signal=N + "gsmm")
        rY2.w = N + "gsmm"
        Sq.add("dve", lambda e: e.tensor_scalar_mul(statg[0:1, 0, 0:128],
                pb[6][0:1, 0:128], 1.0 / INTER),
               waits=(N + "gsmm",) + tuple(rST2.wr(N + "gvar")), signal=N + "gvar")
        rPB2[6].rd(N + "gvar")
        Sq.add("act", lambda e: e.activation(out=statg[0:1, 1, 0:128],
                in_=statg[0:1, 0, 0:128], func=AF.Sqrt, bias=epscol[0:1]),
               waits=(N + "gvar", "epscol"), signal=N + "gsqr")
        Sq.add("dve", lambda e: e.reciprocal(statg[0:1, 1, 0:128], statg[0:1, 1, 0:128]),
               waits=(N + "gsqr",), signal=N + "grs")
        Sq.add("pe", lambda e: e.matmul(pb[6][:, 0:128], ones1[:], statg[0:1, 1, 0:128],
                start=True, stop=True, skip_group_check=True),
               waits=(N + "grs",) + tuple(rPB2[6].wr(N + "gbc")), signal=N + "gbc")
        rST2.rd(N + "gbc")
        Sq.add("dve", lambda e, lc0=lc0: e.tensor_tensor(
                gnb[:, :, lc0:lc0+128], sgb[:],
                pb[6][:, 0:128].unsqueeze(1).broadcast_to([128, 32, 128]), ALU.mult),
               waits=(N + "gbc",) + (("ma3gst",) if l == 0 else ()) +
                     tuple(rGNB.wr(N + "norm")),
               signal=N + "norm")
        rSG2.rd(N + "norm"); rPB2[6].rd(N + "norm")
        if l % 4 == 3:
            ia2 = l // 4
            for oc in range(KCH):
                tg = f"n{l}op{oc}"
                pi = 4 + (oc % 2)
                wb = oc % 2
                Sq.add("qsync",
                       lambda e, oc=oc, wb=wb: [e.dma_start(out=wot2[:, wb, 0:16, :],
                        in_=woutb_dram[oc][:, 0:16, :])],
                       waits=WOUT_READY + tuple(rWO2[wb].wr(tg + "ld")),
                       signal=tg + "ld", n_sig=1)
                Sq.add("qgps",
                       lambda e, oc=oc, wb=wb: [e.dma_start(out=wot2[:, wb, 16:32, :],
                        in_=woutb_dram[oc][:, 16:32, :])],
                       waits=(), signal=tg + "ldb", n_sig=1)
                def omm(e, pi=pi, wb=wb):
                    last = None
                    for kc in range(32):
                        last = e.matmul(pb[pi][:], wot2[:, wb, kc, :], gnb[:, kc, :],
                                        start=(kc == 0), stop=(kc == 31))
                    return last
                Sq.add("pe", omm, waits=(tg + "ld", tg + "ldb", N + "norm") +
                               tuple(rPB2[pi].wr(tg + "mm")), signal=tg + "mm")
                rWO2[wb].rd(tg + "mm")
                rGNB.rd(tg + "mm")
                Sq.add("dve", lambda e, pi=pi: e.tensor_copy(astg[:], pb[pi][:]),
                       waits=(tg + "mm",) + tuple(rSCR2.wr(tg + "ev")), signal=tg + "ev")
                rPB2[pi].rd(tg + "ev")
                Sq.add("qgps", lambda e, oc=oc, ia2=ia2: [e.dma_start(
                        out=attn_out.rearrange("(c p) t -> p c t", p=128)[:, oc, 512*ia2:512*(ia2+1)],
                        in_=astg[:])],
                       waits=(tg + "ev",), signal=tg + "st", n_sig=1)
                rSCR2.rd(tg + "st")

    ar.off = ar.base
    rep = sbuf(4, [128], BF16)
    shc = sbuf(128, [1], I32)
    xn = sbuf(128, [KCH, T], BF16)
    bytR = sbuf(128, [2, 2048], I32)
    kiF = sbuf(128, [2048], I32)
    qaF = sbuf(128, [2048], BF16)
    qbF = sbuf(128, [2048], BF16)
    scstB = sbuf(4, [2, 1024], BF16)
    wgB = sbuf(128, [2, KCH, 256], BF16)
    wuB = sbuf(128, [2, KCH, 256], BF16)
    interB = sbuf(128, [2, 4, T], BF16)
    wdR = sbuf(128, [2, 4, HID], BF16)
    dsF = sbuf(128, [4, 512], F32)
    qa3 = qaF.rearrange("p (a f) -> p a f", f=256)

    bankG = [PST[:, 0:512], PST[:, 512:1024]]
    bankU = [PST[:, 1024:1536], PST[:, 1536:2048]]
    bankD = [PST[:, 2048:2560], PST[:, 2560:3072]]
    bankS = [PST[:, 3072:3584], PST[:, 3584:4096]]

    NSL = NMLP // 256
    NGRP = NSL // 2

    Sq.barrier("MF")
    Sq.add("qsync", lambda e: e.dma_start(out=rep[:], in_=rep_in), signal="rep")
    Sq.add("qsync", lambda e: e.dma_start(out=shc[:], in_=shc_in), signal="shcl")
    Sq.add("qgps", lambda e: e.dma_start(out=xn[:], in_=xn_in.rearrange("(k p) t -> p k t", p=128)),
           signal="xn")

    pgr = pg_in.rearrange("(kc b) n -> b kc n", b=64)
    pur = pu_in.rearrange("(kc b) n -> b kc n", b=64)
    sgr = sg_in.rearrange("(kc b) n -> b kc n", b=4)
    sur = su_in.rearrange("(kc b) n -> b kc n", b=4)
    sdr = sd_in.rearrange("(nc b) h -> b nc h", b=4)

    CN = {"dg": 0, "sx": 0, "gm": 0, "um": 0, "im": 0, "ge": 0, "dm": 0, "cl": 0}
    dec_done = {}
    dwd_ev = {}
    dmz_ev = {}
    pe_sexp = []

    def emit_ext_val(ldev, first_wts=()):
        n = CN["dg"] - 1
        Sq.add("dve", lambda e: e.tensor_tensor(
                bytR[:, n % 2, :], bytR[:, n % 2, :],
                shc[:].broadcast_to([128, 2048]), ALU.logical_shift_right),
               waits=(ldev, "shcl") + tuple(first_wts))
        Sq.add("dve", lambda e: e.tensor_scalar(
                kiF[:], bytR[:, n % 2, :], 7, None, ALU.bitwise_and))
        Sq.add("dve", lambda e: e.tensor_scalar(qaF[:], kiF[:], -2.0, 2.0, ALU.add, ALU.mult))
        Sq.add("dve", lambda e: e.tensor_scalar(qbF[:], kiF[:], -4.0, 4.0, ALU.add, ALU.mult))
        Sq.add("dve", lambda e: e.tensor_tensor(qaF[:], qaF[:], qbF[:], ALU.max))
        Sq.add("dve", lambda e: e.tensor_tensor(qaF[:], qaF[:], kiF[:], ALU.max))
        Sq.add("dve", lambda e: e.tensor_scalar(
                kiF[:], bytR[:, n % 2, :], 8, None, ALU.bitwise_and),
               signal=f"fv{n}")
        Sq.add("dve", lambda e: e.scalar_tensor_tensor(qaF[:], kiF[:], -4.0, qaF[:],
                                                       ALU.add, ALU.mult))

    pe_sld = []

    def sexp_item(ld_fn, mm_waits_extra=()):
        m = CN["sx"]; CN["sx"] += 1
        swts = (f"sx{m-2}",) if m >= 2 else ()
        pe_sld.append((ld_fn(m % 2), swts, f"sl{m}"))
        pwts = [f"sl{m}"]
        if m == 0:
            pwts.append("rep")
        if m >= 1:
            pwts.append(f"fw{m-1}")
        def mm(e, m=m):
            e.matmul(bankS[0], rep[:], scstB[:, m % 2, 0:512],
                     start=True, stop=True, skip_group_check=True)
            return e.matmul(bankS[1], rep[:], scstB[:, m % 2, 512:1024],
                            start=True, stop=True, skip_group_check=True)
        pe_sexp.append((mm, tuple(pwts) + tuple(mm_waits_extra), f"sx{m}"))
        return m

    def emit_decode_gu(gs, mi, inject=None):
        psrc, ssrc, wt = (pgr, sgr, wgB) if mi == 0 else (pur, sur, wuB)
        buf = gs % 2
        for d in range(2):
            n = CN["dg"]; CN["dg"] += 1
            wts = (f"fv{n-2}",) if n >= 2 else ()
            bslot3 = bytR[:, n % 2, :].rearrange("p (kc f) -> p kc f", f=256)
            def ld(e, psrc=psrc, d=d, gs=gs, bslot3=bslot3):
                return [e.dma_start(out=bslot3[0:64],
                                    in_=psrc[0:64, 8*d:8*d+8, 256*gs:256*(gs+1)]),
                        e.dma_start(out=bslot3[64:128],
                                    in_=psrc[0:64, 8*d:8*d+8, 256*gs:256*(gs+1)])]
            Sq.add("qsync", ld, waits=wts, signal=f"fl{n}", n_sig=2)
            emit_ext_val(f"fl{n}")
            for gh in range(2):
                kc0 = 8 * d + 4 * gh
                def sld(e, ssrc=ssrc, kc0=kc0, gs=gs, slot=None):
                    return [e.dma_start(
                        out=scstB[:, slot, :].rearrange("p (a f) -> p a f", f=256),
                        in_=ssrc[0:4, kc0:kc0+4, 256*gs:256*(gs+1)])]
                m = sexp_item(lambda slot, sld=sld: (lambda e: sld(e, slot=slot)))
                wwts = [f"sx{m}"]
                if d == 0 and gh == 0 and gs >= 2:
                    wwts.append(f"lgm{gs-2}" if mi == 0 else f"lum{gs-2}")
                sig = None
                if d == 1 and gh == 1:
                    sig = f"dec{gs}m{mi}"
                    dec_done[(gs, mi)] = sig
                def wm(e, wt=wt, buf=buf, kc0=kc0, gh=gh, m=m):
                    return e.tensor_tensor(
                        wt[:, buf, kc0:kc0+4, :],
                        qa3[:, 4*gh:4*gh+4, :],
                        PST[:, 3072:4096].rearrange("p (a f) -> p a f", f=256), ALU.mult)
                Sq.add("dve", wm, waits=tuple(wwts), signal=sig or (f"fw{m}"))
                if sig:
                    Sq.events[f"fw{m}"] = Sq.events[sig]
                if inject and (d, gh) in inject:
                    inject[(d, gh)]()

    def emit_decode_down(s, jn, inject=None):
        G = s // 2
        gslot = G % 2
        if True:
            nch = 2 * s + jn
            j = 2 * (s % 2) + jn
            n = CN["dg"]; CN["dg"] += 1
            wts = (f"fv{n-2}",) if n >= 2 else ()
            def ld(e, nch=nch, n=n):
                return [e.dma_start(out=bytR[0:64, n % 2, :], in_=pd_in[64*nch:64*nch+64, :]),
                        e.dma_start(out=bytR[64:128, n % 2, :], in_=pd_in[64*nch:64*nch+64, :])]
            Sq.add("qsync", ld, waits=wts, signal=f"fl{n}", n_sig=2)
            fw = ()
            if jn == 0 and s % 2 == 0 and G >= 2:
                fw = (dmz_ev[G - 2],)
            emit_ext_val(f"fl{n}", first_wts=fw)
            for rh in range(2):
                def sld(e, nch=nch, rh=rh, slot=None):
                    return [e.dma_start(
                        out=scstB[:, slot, :].rearrange("p (o f) -> p o f", o=1),
                        in_=sdr[0:4, nch:nch+1, 1024*rh:1024*(rh+1)])]
                m = sexp_item(lambda slot, sld=sld: (lambda e: sld(e, slot=slot)))
                sig = None
                if jn == 1 and rh == 1:
                    sig = f"dwd{s}"
                    dwd_ev[s] = sig
                def wm(e, gslot=gslot, j=j, rh=rh, m=m):
                    return e.tensor_tensor(
                        wdR[:, gslot, j, 1024*rh:1024*(rh+1)],
                        qaF[:, 1024*rh:1024*(rh+1)], PST[:, 3072:4096], ALU.mult)
                Sq.add("dve", wm, waits=(f"sx{m}",), signal=sig or (f"fw{m}"))
                if sig:
                    Sq.events[f"fw{m}"] = Sq.events[sig]
                if inject and rh in inject:
                    inject[rh]()

    def emit_imul(s, i):
        idx = CN["im"]; CN["im"] += 1
        assert idx == 8 * s + i
        j2, tb = i // 4, i % 4
        gslot = (s // 2) % 2
        jg = 2 * (s % 2) + j2
        def im(e, gslot=gslot, jg=jg, tb=tb, idx=idx):
            return e.tensor_tensor(interB[:, gslot, jg, tb*512:(tb+1)*512],
                                   interB[:, gslot, jg, tb*512:(tb+1)*512],
                                   bankU[idx % 2], ALU.mult)
        Sq.add("dve", im, waits=(f"ge{idx}", f"um{idx}"), signal=f"im{idx}")

    def pop_sexp(k):
        for _ in range(k):
            if pe_sexp:
                while pe_sld and (CN["sx"] - len(pe_sexp) + 2 >
                                  CN["sx"] - len(pe_sld)):
                    lfn, lwts, lsig = pe_sld.pop(0)
                    Sq.add("qgps", lfn, waits=lwts, signal=lsig, n_sig=1)
                fn, wts, sig = pe_sexp.pop(0)
                Sq.add("pe", fn, waits=wts, signal=sig)

    def emit_dmm_cell(G, q, tb):
        gslot = G % 2
        for hj in range(4):
            hc = 4 * q + hj
            cl = CN["cl"]; CN["cl"] += 1
            half = cl % 4
            dmidx = CN["dm"]; CN["dm"] += 1
            def dmm(e, hc=hc, tb=tb, dmidx=dmidx, gslot=gslot):
                last = None
                for j in range(4):
                    last = e.matmul(bankD[dmidx % 2],
                                    wdR[:, gslot, j, hc*128:(hc+1)*128],
                                    interB[:, gslot, j, tb*512:(tb+1)*512],
                                    start=(j == 0), stop=(j == 3))
                return last
            wts = [f"dv{dmidx-2}"] if dmidx >= 2 else []
            if hj == 0:
                wts.append(f"im{8*(2*G+1)+4+tb}")
                if q == 0 and tb == 0:
                    wts.append(dwd_ev[2 * G + 1])
            Sq.add("pe", dmm, waits=tuple(wts), signal=f"dm{dmidx}")
            if hc == 15 and tb == 3:
                dmz_ev[G] = f"dm{dmidx}"
            awts = [f"dm{dmidx}"]
            if cl >= 4:
                awts.append(f"sq{cl-4}")
            Sq.add("act", lambda e, half=half, dmidx=dmidx: e.activation(
                    out=dsF[:, half, :], in_=bankD[dmidx % 2], func=AF.Copy),
                   waits=tuple(awts), signal=f"dv{dmidx}")
            def store(e, G=G, hc=hc, tb=tb, half=half):
                if G > 0:
                    return [e.dma_start(out=acc_r[:, hc:hc+1, tb*512:(tb+1)*512],
                                        in_=dsF[:, half, :].unsqueeze(1),
                                        accum_op=ALU.add)]
                return [e.dma_start(out=acc_r[:, hc:hc+1, tb*512:(tb+1)*512],
                                    in_=dsF[:, half, :].unsqueeze(1))]
            Sq.add("qgps", store, waits=(f"dv{dmidx}",), signal=f"sq{cl}", n_sig=1)

    emit_decode_gu(0, 0)
    emit_decode_gu(0, 1)
    pop_sexp(len(pe_sexp))

    for s in range(NSL):
        G = s // 2
        emit_decode_down(s, 0)
        emit_decode_down(s, 1, inject={
            0: lambda: emit_imul(s, 0)})
        if s + 1 < NSL:
            emit_decode_gu(s + 1, 0, inject={
                (0, 0): lambda: (emit_imul(s, 1), emit_imul(s, 2)),
                (1, 1): lambda: emit_imul(s, 3)})
            emit_decode_gu(s + 1, 1, inject={
                (0, 0): lambda: emit_imul(s, 4),
                (0, 1): lambda: emit_imul(s, 5),
                (1, 0): lambda: emit_imul(s, 6)})
            emit_imul(s, 7)
        else:
            for i in range(1, 8):
                emit_imul(s, i)
        Gd = G - 1 if s >= 2 else None
        qbase = 2 * (s % 2)
        for i in range(8):
            j2, tb = i // 4, i % 4
            gmidx = CN["gm"]; CN["gm"] += 1
            assert gmidx == 8 * s + i
            def gmm(e, s=s, j2=j2, tb=tb, gmidx=gmidx):
                last = None
                for k in range(KCH):
                    last = e.matmul(bankG[gmidx % 2], wgB[:, s % 2, k, j2*128:(j2+1)*128],
                                    xn[:, k, tb*512:(tb+1)*512],
                                    start=(k == 0), stop=(k == KCH - 1))
                return last
            wts = [f"ge{gmidx-2}"] if gmidx >= 2 else []
            if i == 0:
                wts.append(dec_done[(s, 0)])
                if s == 0:
                    wts.append("xn")
            Sq.add("pe", gmm, waits=tuple(wts), signal=f"gm{gmidx}")
            if i == 7:
                Sq.events[f"lgm{s}"] = Sq.events[f"gm{gmidx}"]
            awts = [f"gm{gmidx}"]
            if i == 0 and s % 2 == 0 and G >= 2:
                awts.append(dmz_ev[G - 2])
            gslot = G % 2
            jg = 2 * (s % 2) + j2
            Sq.add("act", lambda e, gslot=gslot, jg=jg, tb=tb, gmidx=gmidx: e.activation(
                    out=interB[:, gslot, jg, tb*512:(tb+1)*512], in_=bankG[gmidx % 2],
                    func=AF.Gelu_apprx_tanh),
                   waits=tuple(awts), signal=f"ge{gmidx}")
            umidx = CN["um"]; CN["um"] += 1
            assert umidx == gmidx
            def umm(e, s=s, j2=j2, tb=tb, umidx=umidx):
                last = None
                for k in range(KCH):
                    last = e.matmul(bankU[umidx % 2], wuB[:, s % 2, k, j2*128:(j2+1)*128],
                                    xn[:, k, tb*512:(tb+1)*512],
                                    start=(k == 0), stop=(k == KCH - 1))
                return last
            wts = [f"im{umidx-2}"] if umidx >= 2 else []
            if i == 0:
                wts.append(dec_done[(s, 1)])
            Sq.add("pe", umm, waits=tuple(wts), signal=f"um{umidx}")
            if i == 7:
                Sq.events[f"lum{s}"] = Sq.events[f"um{umidx}"]
            pop_sexp(2 if i % 2 == 0 else 1)
            if Gd is not None:
                emit_dmm_cell(Gd, qbase + i // 4, tb)
        pop_sexp(len(pe_sexp))

    for q in range(4):
        for tb in range(4):
            emit_dmm_cell(NGRP - 1, q, tb)

    Sq.barrier("FR")
    ar.off = ar.base
    unpif = sbuf(128, [128], F32)
    xtile = sbuf(128, [KCH, 512], F32)
    atile = sbuf(128, [KCH, 512], F32)
    ftile = sbuf(128, [KCH, 512], F32)
    ostage = sbuf(128, [KCH, 128], F32)
    rXTI = R(); rATI = R(); rFTI = R(); rOST = R()
    rPBr = [R() for i in range(4)]
    attn_r = attn_dram.rearrange("(k p) t -> p k t", p=128)
    xT_r = xT_in.rearrange("(k p) t -> p k t", p=128)

    Sq.add("qsync", lambda e: [e.dma_start(out=unpif, in_=unpi_in)], signal="unpi", n_sig=1)
    for ia in range(NA):
        RA = f"r{ia}"
        def rld(e, ia=ia):
            return [e.dma_start(out=xtile, in_=xT_r[:, :, 3+512*ia:3+512*(ia+1)]),
                    e.dma_start(out=atile, in_=attn_r[:, :, 512*ia:512*(ia+1)]),
                    e.dma_start(out=ftile, in_=acc_r[:, :, 512*ia:512*(ia+1)])]
        Sq.add("qsync", rld,
               waits=tuple(rXTI.wr(RA + "ld")) + tuple(rATI.wr(RA + "ld")) +
                     tuple(rFTI.wr(RA + "ld")),
               signal=RA + "ld", n_sig=3)
        Sq.add("dve", lambda e: e.tensor_tensor(xtile, xtile, atile, ALU.add),
               waits=(RA + "ld",) + tuple(rXTI.wr(RA + "s1")), signal=RA + "s1")
        rATI.rd(RA + "s1")
        Sq.add("dve", lambda e: e.tensor_tensor(xtile, xtile, ftile, ALU.add),
               waits=(RA + "s1",) + tuple(rXTI.wr(RA + "s2")), signal=RA + "s2")
        rFTI.rd(RA + "s2")
        for tt in range(4):
            RT = RA + f"t{tt}"
            for kg in range(4):
                def tmm(e, tt=tt, kg=kg):
                    last = None
                    for j in range(4):
                        kc = 4 * kg + j
                        last = e.transpose(pb[kg][:, j*128:(j+1)*128],
                                           xtile[:, kc, tt*128:(tt+1)*128], unpif)
                    return last
                Sq.add("pe", tmm, waits=(RA + "s2", "unpi") + tuple(rPBr[kg].wr(RT + f"tm{kg}")),
                       signal=RT + f"tm{kg}")
                rXTI.rd(RT + f"tm{kg}")
                Sq.add("dve", lambda e, kg=kg: e.tensor_copy(ostage[:, 4*kg:4*kg+4, :],
                        pb[kg].rearrange("p (a b) -> p a b", b=128)),
                       waits=(RT + f"tm{kg}",) +
                             (tuple(rOST.wr(RT + "ev0")) if kg == 0 else ()) +
                             tuple(rPBr[kg].rd(RT + f"ev{kg}")),
                       signal=RT + f"ev{kg}")
            def rst(e, ia=ia, tt=tt):
                return [e.dma_start(out=out_dram[512*ia+128*tt:512*ia+128*(tt+1), :],
                                    in_=ostage.rearrange("p a b -> p (a b)"))]
            Sq.add("qgps", rst, waits=tuple(RT + f"ev{kg}" for kg in range(4)),
                   signal=RT + "st", n_sig=1)
            rOST.rd(RT + "st")

    sems = {}
    for name in Sq.semnames:
        sems[name] = es.enter_context(nc.semaphore("sem_" + name))
    final_evs = [f"r{ia}t{tt}st" for ia in range(NA) for tt in range(4)]
    with es, nc.Block() as block:
        @block.sync
        def _(sync):
            Sq.emit("qsync", sync, sems)
        @block.gpsimd
        def _(gpsimd):
            Sq.emit("qgps", gpsimd, sems)
            agg = {}
            for evn in final_evs:
                sn, v = Sq.events[evn]
                agg[sn] = max(agg.get(sn, 0), v)
            for sn, v in agg.items():
                gpsimd.wait_ge(sems[sn], v)
        @block.vector
        def _(vector):
            Sq.emit("dve", vector, sems)
        @block.scalar
        def _(scalar):
            Sq.emit("act", scalar, sems)
        @block.tensor
        def _(tensor):
            Sq.emit("pe", tensor, sems)
    return nc


def kernel(**inputs):
    x = np.asarray(inputs["L0_attn_hidden_states"], dtype=np.float32)
    Wz = np.asarray(inputs["L0_attn_Wz"], dtype=np.float32)
    Wx = np.asarray(inputs["L0_attn_Wx"], dtype=np.float32)
    WB = np.asarray(inputs["L0_attn_WB"], dtype=np.float32)
    WC = np.asarray(inputs["L0_attn_WC"], dtype=np.float32)
    Wdt = np.asarray(inputs["L0_attn_Wdt"], dtype=np.float32)
    conv_w = np.asarray(inputs["L0_attn_conv_w"], dtype=np.float32)
    conv_b = np.asarray(inputs["L0_attn_conv_b"], dtype=np.float32)
    dt_bias = np.asarray(inputs["L0_attn_dt_bias"], dtype=np.float32)
    A_log = np.asarray(inputs["L0_attn_A_log"], dtype=np.float32)
    D = np.asarray(inputs["L0_attn_D"], dtype=np.float32)
    norm_w = np.asarray(inputs["L0_attn_norm_w"], dtype=np.float32)
    Wout = np.asarray(inputs["L0_attn_Wout"], dtype=np.float32)
    pg = np.asarray(inputs["L0_ffn_gate_W_packed"], dtype=np.int32)
    sg = np.asarray(inputs["L0_ffn_gate_scales"], dtype=np.float32)
    pu = np.asarray(inputs["L0_ffn_up_W_packed"], dtype=np.int32)
    su = np.asarray(inputs["L0_ffn_up_scales"], dtype=np.float32)
    pd = np.asarray(inputs["L0_ffn_down_W_packed"], dtype=np.int32)
    sd = np.asarray(inputs["L0_ffn_down_scales"], dtype=np.float32)

    pig = pi_full(HID)
    pid = pi_full(NMLP)
    wz = np.ascontiguousarray(Wz[pig])
    wx = np.ascontiguousarray(Wx[pig])
    wb = np.ascontiguousarray(WB[pig])
    wc = np.ascontiguousarray(WC[pig])
    wdt = np.ascontiguousarray(Wdt[pig])
    wout = np.ascontiguousarray(Wout[:, pig])
    convw_h = np.ascontiguousarray(conv_w.reshape(NCC, 128, 4).transpose(1, 0, 2).reshape(128, NCC * 4))
    convb_h = np.ascontiguousarray(conv_b.reshape(NCC, 128).T)
    dtb_h = np.ascontiguousarray(dt_bias[:, None])
    alog_h = np.ascontiguousarray(A_log[:, None])
    drep_h = np.ascontiguousarray(np.repeat(D, HD).reshape(32, 128).T)
    nw_h = np.ascontiguousarray(norm_w.reshape(32, 128).T)
    ident = np.eye(128, dtype=np.float32)
    import ml_dtypes
    mask = np.where(np.arange(128)[:, None] <= np.arange(128)[None, :], 0, -1e30).astype(ml_dtypes.bfloat16)
    unpi = np.zeros((128, 128), np.float32)
    unpi[np.argsort(pi128()), np.arange(128)] = 1.0
    repm = np.zeros((4, 128), np.float32)
    for p in range(128):
        repm[(p % 64) // 16, p] = -0.125
    repm = repm.astype(ml_dtypes.bfloat16)
    shc_h = np.zeros((128, 1), np.int32)
    shc_h[64:] = 4
    pgT = np.ascontiguousarray(pg.T)[:, pid].copy()
    puT = np.ascontiguousarray(pu.T)[:, pid].copy()
    pdT = np.ascontiguousarray(pd.T)[:, pig].copy()
    sgT = np.ascontiguousarray(sg.T)[:, pid].copy()
    suT = np.ascontiguousarray(su.T)[:, pid].copy()
    sdT = np.ascontiguousarray(sd.T)[:, pig].copy()

    shared = {
        "wz_in": wz, "wx_in": wx, "wb_in": wb, "wc_in": wc, "wdt_in": wdt,
        "wout_in": wout, "convw_in": convw_h, "convb_in": convb_h,
        "dtb_in": dtb_h, "alog_in": alog_h, "drep_in": drep_h, "nw_in": nw_h,
        "ident_in": ident, "mask_in": mask, "unpi_in": unpi, "rep_in": repm,
        "shc_in": shc_h,
        "pg_in": pgT, "pu_in": puT, "pd_in": pdT,
        "sg_in": sgT, "su_in": suT, "sd_in": sdT,
    }
    z3 = np.zeros((3, HID), np.float32)
    zprev = np.zeros((HID, T + 3), np.float32)
    in_maps = []
    for c in range(8):
        b, h = c // 2, c % 2
        xb = x[b]
        if h == 0:
            xT_c = np.ascontiguousarray(np.concatenate([z3, xb[0:T]], 0).T[pig])
            xp_c = zprev
            fl = np.zeros((128, 1), np.float32)
        else:
            xT_c = np.ascontiguousarray(np.concatenate([xb[T-3:T], xb[T:2*T]], 0).T[pig])
            xp_c = np.ascontiguousarray(np.concatenate([z3, xb[0:T]], 0).T[pig])
            fl = np.ones((128, 1), np.float32)
        sel = np.zeros((128, 8), np.float32)
        if h == 1:
            sel[:, (c - 1) % 8] = 1.0
        m = dict(shared)
        m.update({"xT_in": xT_c, "xprev_in": xp_c, "flag_in": fl, "selc_in": sel})
        in_maps.append(m)

    nc = build()
    res = run_bass_kernel_spmd(nc, in_maps, core_ids=list(range(8)))
    inv = np.argsort(pig)
    out = np.empty((4, 2 * T, HID), np.float32)
    for c in range(8):
        b, h = c // 2, c % 2
        out[b, h * T:(h + 1) * T, :] = res.results[c]["out"]
    return out


# version 18 (final)
# speedup vs baseline: 1.0198x; 1.0169x over previous
import sys
sys.path.insert(0, "/opt/trn_rl_repo")
import numpy as np
import concourse.bass as bass
import concourse.mybir as mybir
from concourse.bass_utils import run_bass_kernel_spmd
from contextlib import ExitStack

F32 = mybir.dt.float32
BF16 = mybir.dt.bfloat16
I32 = mybir.dt.int32
AF = mybir.ActivationFunctionType
ALU = mybir.AluOpType

HID = 2048
KCH = 16
NH, HD, NG, NST = 64, 64, 8, 128
INTER = NH * HD
GN = NG * NST
CONVD = INTER + 2 * GN
NCC = CONVD // 128
EPS = 1e-5
T = 2048
NA = T // 512
NMLP = 16384
TB = T // 512
NGF = NMLP // 512

class Sched:
    DMA_Q = {"qsync", "qgps", "qscal"}
    NSLOT = 8
    def __init__(self):
        engs = ["pe", "dve", "act", "pool"]
        qs = ["qsync", "qgps", "qscal"]
        self.items = {k: [] for k in engs + qs}
        self.cnt = {k: 0 for k in engs}
        self.qslot = {q: 0 for q in qs}
        self.qtarget = {}
        self.events = {}
        self.autoctr = 0
        self.pending = {}
        self.semnames = list(engs) + [f"{q}{i}" for q in qs for i in range(self.NSLOT)]
    def add(self, eng, fn, waits=(), signal=None, n_sig=1):
        waits = tuple(w for w in waits if w) + self.pending.pop(eng, ())
        if eng not in self.DMA_Q and signal is None:
            signal = f"__a{self.autoctr}"
            self.autoctr += 1
        if eng in self.DMA_Q:
            if signal:
                slot = self.qslot[eng]
                self.qslot[eng] = (slot + 1) % self.NSLOT
                sn = f"{eng}{slot}"
                fence = self.qtarget.get(sn, 0)
                tgt = fence + 16 * n_sig
                self.qtarget[sn] = tgt
                self.events[signal] = (sn, tgt)
                self.items[eng].append((fn, waits, sn, fence, 16 * n_sig))
            else:
                self.items[eng].append((fn, waits, None, 0, 0))
        else:
            step = n_sig if signal else 0
            self.cnt[eng] += step
            if signal:
                assert signal not in self.events, signal
                self.events[signal] = (eng, self.cnt[eng])
            self.items[eng].append((fn, waits, None, 0, step))
    def emit(self, eng_name, eng_obj, sems):
        is_dma = eng_name in self.DMA_Q
        chain = (not is_dma) and eng_name != "pe"
        cnt_sofar = 0
        for item in self.items[eng_name]:
            fn, waits, sn, fence, step = item
            best = {}
            for w in waits:
                weng, wval = self.events[w]
                if weng == eng_name and not is_dma:
                    if eng_name == "pe":
                        continue
                best[weng] = max(best.get(weng, 0), wval)
            if chain and cnt_sofar > 0:
                best[eng_name] = max(best.get(eng_name, 0), cnt_sofar)
            if is_dma and sn is not None and fence > 0:
                best[sn] = max(best.get(sn, 0), fence)
            cnt_sofar += step
            for weng, wval in best.items():
                eng_obj.wait_ge(sems[weng], wval)
            r = fn(eng_obj)
            if is_dma:
                insts = r if isinstance(r, list) else [r]
                if sn is not None:
                    assert step == 16 * len(insts), (step, len(insts))
                    for i in insts:
                        i.then_inc(sems[sn], 16)
            elif step:
                r.then_inc(sems[eng_name], step)

    def barrier(self, name):
        evs = []
        for eng in ["pe", "dve", "act", "pool"]:
            if self.cnt[eng] > 0:
                self.events[name + "_" + eng] = (eng, self.cnt[eng])
                evs.append(name + "_" + eng)
        for sn, tgt in list(self.qtarget.items()):
            self.events[name + "_" + sn] = (sn, tgt)
            evs.append(name + "_" + sn)
        for k in self.items:
            self.pending[k] = self.pending.get(k, ()) + tuple(evs)

class R:
    def __init__(self, w=None):
        self.w = w
        self.rs = []
    def rd(self, ev):
        waits = [self.w] if self.w else []
        self.rs.append(ev)
        return waits
    def wr(self, ev):
        waits = ([self.w] if self.w else []) + self.rs
        self.w = ev
        self.rs = []
        return waits

def pi128():
    return np.concatenate([np.arange(0, 128, 2), np.arange(1, 128, 2)])

def pi_full(n):
    p = pi128()
    return np.concatenate([c * 128 + p for c in range(n // 128)])

def build():
    nc = bass.Bass("TRN2", target_bir_lowering=False)
    xT_in = nc.dram_tensor("xT_in", [HID, T + 3], F32, kind="ExternalInput").ap()
    xprev_in = nc.dram_tensor("xprev_in", [HID, T + 3], F32, kind="ExternalInput").ap()
    flag_in = nc.dram_tensor("flag_in", [128, 1], F32, kind="ExternalInput").ap()
    wz_in = nc.dram_tensor("wz_in", [HID, INTER], F32, kind="ExternalInput").ap()
    wx_in = nc.dram_tensor("wx_in", [HID, INTER], F32, kind="ExternalInput").ap()
    wb_in = nc.dram_tensor("wb_in", [HID, GN], F32, kind="ExternalInput").ap()
    wc_in = nc.dram_tensor("wc_in", [HID, GN], F32, kind="ExternalInput").ap()
    wdt_in = nc.dram_tensor("wdt_in", [HID, NH], F32, kind="ExternalInput").ap()
    wout_in = nc.dram_tensor("wout_in", [INTER, HID], F32, kind="ExternalInput").ap()
    convw_in = nc.dram_tensor("convw_in", [128, NCC * 4], F32, kind="ExternalInput").ap()
    convb_in = nc.dram_tensor("convb_in", [128, NCC], F32, kind="ExternalInput").ap()
    dtb_in = nc.dram_tensor("dtb_in", [NH, 1], F32, kind="ExternalInput").ap()
    alog_in = nc.dram_tensor("alog_in", [NH, 1], F32, kind="ExternalInput").ap()
    drep_in = nc.dram_tensor("drep_in", [128, 32], F32, kind="ExternalInput").ap()
    nw_in = nc.dram_tensor("nw_in", [128, 32], F32, kind="ExternalInput").ap()
    ident_in = nc.dram_tensor("ident_in", [128, 128], F32, kind="ExternalInput").ap()
    mask_in = nc.dram_tensor("mask_in", [128, 128], BF16, kind="ExternalInput").ap()
    unpi_in = nc.dram_tensor("unpi_in", [128, 128], F32, kind="ExternalInput").ap()
    rep_in = nc.dram_tensor("rep_in", [4, 128], BF16, kind="ExternalInput").ap()
    shc_in = nc.dram_tensor("shc_in", [128, 1], I32, kind="ExternalInput").ap()
    pg_in = nc.dram_tensor("pg_in", [HID // 2, NMLP], I32, kind="ExternalInput").ap()
    pu_in = nc.dram_tensor("pu_in", [HID // 2, NMLP], I32, kind="ExternalInput").ap()
    pd_in = nc.dram_tensor("pd_in", [NMLP // 2, HID], I32, kind="ExternalInput").ap()
    sg_in = nc.dram_tensor("sg_in", [HID // 32, NMLP], F32, kind="ExternalInput").ap()
    su_in = nc.dram_tensor("su_in", [HID // 32, NMLP], F32, kind="ExternalInput").ap()
    sd_in = nc.dram_tensor("sd_in", [NMLP // 32, HID], F32, kind="ExternalInput").ap()
    out_dram = nc.dram_tensor("out", [T, HID], F32, kind="ExternalOutput").ap()

    selc_in = nc.dram_tensor("selc_in", [128, 8], F32, kind="ExternalInput").ap()
    diag_dram = nc.dram_tensor("diag_dram", [NCC, 128, 4, 128], BF16, kind="Internal").ap()
    wallb_dram = nc.dram_tensor("wallb_dram", [81, 128, KCH, 128], BF16, kind="Internal").ap()
    y_dram = nc.dram_tensor("y_dram", [16, 128, 32, 128], BF16, kind="Internal").ap()
    gate_dram = nc.dram_tensor("gate_dram", [4, 128, 32, 512], BF16, kind="Internal").ap()
    mext_dram = nc.dram_tensor("mext_dram", [16, 8, 128, 8, 128], BF16, kind="Internal").ap()
    cexp_dram = nc.dram_tensor("cexp_dram", [16, 64], F32, kind="Internal").ap()
    cc_in = nc.dram_tensor("cc_in", [128, 4096], F32, kind="Internal").ap()
    cc_out = nc.dram_tensor("cc_out", [1024, 4096], F32, kind="Internal", addr_space="Local").ap()
    woutb_dram = nc.dram_tensor("woutb_dram", [KCH, 128, 32, 128], BF16, kind="Internal").ap()
    shuf_dram = nc.dram_tensor("shuf_dram", [NH, 128], F32, kind="Internal").ap()
    attn_dram = nc.dram_tensor("attn_dram", [HID, T], F32, kind="Internal").ap()
    xn_dram = nc.dram_tensor("xn_dram", [HID, T], BF16, kind="Internal").ap()
    acc_dram = nc.dram_tensor("acc_dram", [HID, T], F32, kind="Internal").ap()
    attn_out = attn_dram
    xn_in = xn_dram
    xnr_d = xn_dram.rearrange("(k p) t -> p k t", p=128)
    acc_r = acc_dram.rearrange("(k p) t -> p k t", p=128)

    es = ExitStack()
    ARENA_B = 212824
    arena = es.enter_context(nc.sbuf_tensor("arena", [128, ARENA_B // 2], BF16))

    class Ar:
        def __init__(self):
            self.off = 0
            self.base = 0
        def carve(self, P, freeshape, dt):
            esz = 2 if dt == BF16 else 4
            nb = int(np.prod(freeshape)) * esz
            a = (self.off + 3) // 4 * 4
            self.off = a + nb
            assert self.off <= ARENA_B, ("arena overflow", self.off)
            v = arena[0:P, a // 2:(a + nb) // 2]
            if dt != BF16:
                v = v.bitcast(dt)
            if len(freeshape) == 2:
                v = v.rearrange("p (a b) -> p a b", b=freeshape[1])
            elif len(freeshape) == 3:
                v = v.rearrange("p (a b c) -> p a b c", b=freeshape[1], c=freeshape[2])
            else:
                assert len(freeshape) == 1
            return v
    ar = Ar()
    def sbuf(P, freeshape, dt):
        return ar.carve(P, freeshape, dt)

    ones1 = sbuf(1, [128], F32)
    onescol = sbuf(128, [1], F32)
    onescolb = sbuf(128, [1], BF16)
    ones64 = sbuf(64, [128], F32)
    epscol = sbuf(128, [1], F32)
    flagc = sbuf(128, [1], F32)
    identf = sbuf(128, [128], F32)
    identb = sbuf(128, [128], BF16)
    maskt = sbuf(128, [128], BF16)
    convw = sbuf(128, [NCC * 4], F32)
    convb = sbuf(128, [NCC], F32)
    dtbc = sbuf(NH, [1], F32)
    acol = sbuf(NH, [1], F32)
    drep = sbuf(128, [32], F32)
    nwc = sbuf(128, [32], F32)
    selc = sbuf(128, [8], F32)
    ar.base = ar.off

    S = sbuf(128, [NG, 512], F32)
    Sb = sbuf(128, [512], BF16)
    mk_xc = ar.off
    xc = sbuf(128, [2, 515], F32)
    xnc = sbuf(128, [KCH, 515], BF16)
    xbcp = sbuf(128, [NCC, 515], BF16)
    tailb = sbuf(128, [NCC, 3], BF16)
    convout = xbcp[:, :, 3:515]
    wtile = sbuf(128, [2, KCH, 128], BF16)
    wotile = sbuf(128, [1, 32, 128], BF16)
    dgt = sbuf(128, [2, 4, 128], BF16)
    dstg = wtile[:, 0, 0:4, :]
    xsq = sbuf(128, [2, 512], F32)
    statr = sbuf(1, [5, 512], F32)
    statg = sbuf(1, [2, 128], F32)
    dtp = sbuf(64, [4, 512], F32)
    lc1p = sbuf(1, [1024], F32)
    qtok = sbuf(128, [64], F32)
    wfm = sbuf(64, [128], F32)
    wtok = sbuf(128, [64], F32)
    crow = sbuf(1, [64], F32)
    cexp = sbuf(1, [64], F32)
    qm = sbuf(128, [8, 128], F32)
    ebuf = sbuf(128, [8, 128], BF16)
    eext = sbuf(128, [8, 128], BF16)
    mprime = sbuf(128, [8, 128], BF16)
    mext = sbuf(128, [8, 128], BF16)
    xTt = sbuf(128, [32, 128], BF16)
    bTt = sbuf(128, [8, 128], BF16)
    y_sb = sbuf(128, [32, 128], BF16)
    sgb = sbuf(128, [32, 128], BF16)
    gnb = sbuf(128, [32, 512], BF16)
    normb = gnb
    gate = gnb
    scr = sbuf(128, [512], F32)
    astg = scr

    PST = es.enter_context(nc.psum_tensor("PST", [128, 4096], F32))
    pb = [PST[:, 512 * i:512 * (i + 1)] for i in range(8)]
    def pbf(i):
        return pb[i].bitcast(BF16)

    S3 = S.rearrange("p g x -> p (g x)")
    xT3 = xTt.rearrange("p a b -> p (a b)")
    xTh = xTt.rearrange("p a b -> p (a b)").rearrange("p (h d) -> p h d", d=64)

    prepass = False
    Sq = Sched()

    rWT = [R(), R()]
    rWO = [R(), R()]
    rDG = [R(), R()]
    rPB = [R() for _ in range(8)]
    rXBC = R()
    rCO = rXBC
    rXT = R(); rBT = R(); rQT = R(); rWFM = R(); rWTOK = R(); rCROW = R(); rCEXP = R()
    rQM = R(); rT2M = R(); rE = R(); rEE = R(); rMP = R(); rME = R()
    rYSB = R(); rSGB = R(); rNORM = R(); rS = R(); rSB = R(); rXN = R()
    rLC1 = R(); rSHUF = R(); rSTAT = R(); rDTP = R(); rSCR = R(); rASTG = R(); rXSQ = [R(), R()]
    rXC = [R(), R()]; rTAIL = R(); rGNB = R()

    def c_loads(e):
        return [e.dma_start(out=convw[:], in_=convw_in),
                e.dma_start(out=convb[:], in_=convb_in),
                e.dma_start(out=dtbc[:], in_=dtb_in),
                e.dma_start(out=acol[:], in_=alog_in),
                e.dma_start(out=drep[:], in_=drep_in),
                e.dma_start(out=nwc[:], in_=nw_in),
                e.dma_start(out=identf[:], in_=ident_in),
                e.dma_start(out=maskt[:], in_=mask_in),
                e.dma_start(out=flagc[:], in_=flag_in),
                e.dma_start(out=selc[:], in_=selc_in)]
    Sq.add("qsync", c_loads, signal="consts", n_sig=10)
    CON = ("consts",)
    Sq.add("dve", lambda e: e.memset(ones1[:], 1.0), signal="ones1")
    Sq.add("dve", lambda e: e.memset(onescol[:], 1.0), signal="onescol")
    Sq.add("dve", lambda e: e.memset(onescolb[:], 1.0), signal="onescolb")
    Sq.add("dve", lambda e: e.memset(ones64[:], 1.0), signal="ones64")
    Sq.add("dve", lambda e: e.memset(epscol[:], EPS), signal="epscol")
    Sq.add("dve", lambda e: e.tensor_copy(identb[:], identf[:]), waits=CON, signal="identb")
    Sq.add("act", lambda e: e.activation(out=acol[:], in_=acol[:], func=AF.Exp),
           waits=CON, signal="aexp")
    Sq.add("dve", lambda e: e.tensor_scalar_mul(acol[:], acol[:], -1.0),
           waits=("aexp",), signal="acol")
    Sq.add("dve", lambda e: e.memset(S3, 0.0), signal="sinit")
    rS.w = "sinit"

    for oc in range(KCH):
        def ldw(e, oc=oc):
            return [e.dma_start(out=wotile[:, 0, :, :],
                                in_=wout_in[:, oc*128:(oc+1)*128].rearrange("(k p) c -> p k c", p=128))]
        Sq.add("qgps", ldw, waits=tuple(rWO[0].wr(f"wfld{oc}")) + CON, signal=f"wfld{oc}", n_sig=1)
        def fold(e):
            return e.tensor_tensor(wotile[:, 0, :, :], wotile[:, 0, :, :],
                                   nwc[:].unsqueeze(2).broadcast_to([128, 32, 128]), ALU.mult)
        Sq.add("dve", fold, waits=tuple(rWO[0].wr(f"wfoldc{oc}")) + CON,
               signal=f"wfoldc{oc}")
        Sq.add("qgps", lambda e, oc=oc: [e.dma_start(out=woutb_dram[oc], in_=wotile[:, 0, :, :])],
               waits=tuple(rWO[0].rd(f"wfold{oc}")), signal=f"wfold{oc}", n_sig=1)
    WOUT_READY = (f"wfold{KCH-1}",)
    rDSTG = R()
    for cc in range(NCC):
        def bld(e, cc=cc):
            last = None
            for k in range(4):
                last = e.tensor_scalar(dstg[:, k, :], identb[:], convw[:, cc*4+k:cc*4+k+1],
                                       None, ALU.mult)
            return last
        Sq.add("dve", bld, waits=("identb",) + CON + tuple(rWT[0].wr(f"dgb{cc}")),
               signal=f"dgb{cc}")
        Sq.add("qgps", lambda e, cc=cc: [e.dma_start(out=diag_dram[cc], in_=dstg)],
               waits=tuple(rWT[0].rd(f"dgst{cc}")), signal=f"dgst{cc}", n_sig=1)
    DIAG_READY = (f"dgst{NCC-1}",)

    def wsrc(oc):
        if oc < 32: return wx_in, oc * 128, 128
        if oc < 40: return wb_in, (oc - 32) * 128, 128
        if oc < 48: return wc_in, (oc - 40) * 128, 128
        if oc < 80: return wz_in, (oc - 48) * 128, 128
        return wdt_in, 0, 64

    rFT = [R(), R()]
    for oc in range(81):
        fsrc, fco, fwidth = wsrc(oc)
        fbuf = oc % 2
        def fld(e, fsrc=fsrc, fco=fco, fwidth=fwidth, fbuf=fbuf):
            return [e.dma_start(out=y_sb[:, 16*fbuf:16*fbuf+16, 0:fwidth],
                                in_=fsrc[:, fco:fco+fwidth].rearrange("(k p) c -> p k c", p=128))]
        Sq.add("qgps", fld, waits=tuple(rFT[fbuf].wr(f"wfl{oc}")), signal=f"wfl{oc}", n_sig=1)
        def fst(e, oc=oc, fwidth=fwidth, fbuf=fbuf):
            return [e.dma_start(out=wallb_dram[oc][:, :, 0:fwidth],
                                in_=y_sb[:, 16*fbuf:16*fbuf+16, 0:fwidth])]
        Sq.add("qsync", fst, waits=(f"wfl{oc}",), signal=f"wfo{oc}", n_sig=1)
        rFT[fbuf].rd(f"wfo{oc}")
    rYSB.w = "wfo80"

    passes = ([("p", xprev_in, True)] if prepass else []) + [("m", xT_in, False)]
    steps = [(pfx, src, so, ia) for (pfx, src, so) in passes for ia in range(NA)]
    for (PFX, xsrc, SO, ia) in steps:
        A = PFX + f"a{ia}"
        ofs = 3 + 512 * ia
        NCC_P = 40 if SO else NCC
        if PFX == "m" and ia == 0 and prepass:
            Sq.add("dve", lambda e: e.tensor_scalar(S3, S3, flagc[:], None, ALU.mult),
                   waits=tuple(rS.wr("flagS")) + CON, signal="flagS")

        widths = ([(0, 3)] if ia == 0 else []) + [(3, 515)]
        for (c0, c1) in widths:
            W = c1 - c0
            tw = A + f"w{c0}"
            for k in range(KCH):
                si = k % 2
                def xld(e, k=k, si=si, c0=c0, c1=c1, ia=ia, xsrc=xsrc):
                    lo = 512 * ia + c0
                    return [e.dma_start(out=xc[:, si, 0:c1-c0],
                                        in_=xsrc[128*k:128*(k+1), lo:lo+(c1-c0)])]
                Sq.add("qsync", xld, waits=tuple(rXC[si].wr(tw + f"xl{k}")) + CON,
                       signal=tw + f"xl{k}", n_sig=1)
                Sq.add("act", lambda e, si=si, W=W: e.activation(
                        out=xsq[:, si, 0:W], in_=xc[:, si, 0:W], func=AF.Square),
                       waits=(tw + f"xl{k}",) + tuple(rXSQ[si].wr(tw + f"sq{k}")),
                       signal=tw + f"sq{k}")
                rXC[si].rd(tw + f"sq{k}")
                Sq.add("pe", lambda e, si=si, k=k, W=W: e.matmul(
                        pb[0][0:1, 0:W], onescol[:], xc[:, si, 0:W],
                        start=(k == 0), stop=(k == KCH - 1)),
                       waits=(tw + f"xl{k}", "onescol") +
                             (tuple(rPB[0].wr(tw + "smm")) if k == 0 else ()),
                       signal=(tw + "smm") if k == KCH - 1 else tw + f"sm{k}")
                rXC[si].rd(tw + ("smm" if k == KCH - 1 else f"sm{k}"))
                Sq.add("pe", lambda e, si=si, k=k, W=W: e.matmul(
                        pb[1][0:1, 0:W], onescol[:], xsq[:, si, 0:W],
                        start=(k == 0), stop=(k == KCH - 1)),
                       waits=(tw + f"sq{k}",) + (tuple(rPB[1].wr(tw + "qmm")) if k == 0 else ()),
                       signal=(tw + "qmm") if k == KCH - 1 else tw + f"qm{k}")
                rXSQ[si].rd(tw + ("qmm" if k == KCH - 1 else f"qm{k}"))
            st = statr[0:1]
            Sq.add("dve", lambda e, st=st, W=W: e.tensor_scalar_mul(
                    st[:, 0, 0:W], pb[0][0:1, 0:W], 1.0 / HID),
                   waits=(tw + "smm",) + tuple(rSTAT.wr(tw + "st1")), signal=tw + "st1")
            rPB[0].rd(tw + "st1")
            Sq.add("dve", lambda e, st=st, W=W: e.tensor_scalar_mul(
                    st[:, 1, 0:W], pb[1][0:1, 0:W], 1.0 / HID),
                   waits=(tw + "qmm",), signal=tw + "st2")
            rPB[1].rd(tw + "st2")
            Sq.add("dve", lambda e, st=st, W=W: e.tensor_tensor(
                    st[:, 2, 0:W], st[:, 0, 0:W], st[:, 0, 0:W], ALU.mult))
            Sq.add("dve", lambda e, st=st, W=W: e.tensor_tensor(
                    st[:, 2, 0:W], st[:, 1, 0:W], st[:, 2, 0:W], ALU.subtract),
                   signal=tw + "var")
            Sq.add("act", lambda e, st=st, W=W: e.activation(
                    out=st[:, 3, 0:W], in_=st[:, 2, 0:W], func=AF.Sqrt, bias=epscol[0:1]),
                   waits=(tw + "var", "epscol"), signal=tw + "sqr")
            Sq.add("dve", lambda e, st=st, W=W: e.reciprocal(st[:, 3, 0:W], st[:, 3, 0:W]),
                   waits=(tw + "sqr",))
            Sq.add("dve", lambda e, st=st, W=W: e.tensor_tensor(
                    st[:, 4, 0:W], st[:, 0, 0:W], st[:, 3, 0:W], ALU.mult),
                   signal=tw + "mr")
            rSTAT.rd(tw + "mr")
            Sq.add("pe", lambda e, st=st, W=W: e.matmul(
                    pb[2][:, 0:W], ones1[:], st[:, 3, 0:W], start=True, stop=True,
                    skip_group_check=True),
                   waits=(tw + "mr", "ones1") + tuple(rPB[2].wr(tw + "bca")),
                   signal=tw + "bca")
            rSTAT.rd(tw + "bca")
            Sq.add("pe", lambda e, st=st, W=W: e.matmul(
                    pb[3][:, 0:W], ones1[:], st[:, 4, 0:W], start=True, stop=True,
                    skip_group_check=True),
                   waits=(tw + "mr",) + tuple(rPB[3].wr(tw + "bcb")),
                   signal=tw + "bcb")
            rSTAT.rd(tw + "bcb")
            for k in range(KCH):
                si = k % 2
                def xld2(e, k=k, si=si, c0=c0, c1=c1, ia=ia, xsrc=xsrc):
                    lo = 512 * ia + c0
                    return [e.dma_start(out=xc[:, si, 0:c1-c0],
                                        in_=xsrc[128*k:128*(k+1), lo:lo+(c1-c0)])]
                Sq.add("qsync", xld2, waits=tuple(rXC[si].wr(tw + f"x2l{k}")),
                       signal=tw + f"x2l{k}", n_sig=1)
                Sq.add("dve", lambda e, si=si, c0=c0, c1=c1, W=W: e.tensor_tensor(
                        xc[:, si, 0:W], xc[:, si, 0:W],
                        pb[2][:, 0:W], ALU.mult),
                       waits=(tw + f"x2l{k}", tw + "bca") + tuple(rXC[si].wr(tw + f"xs{k}")),
                       signal=tw + f"xs{k}")
                rPB[2].rd(tw + f"xs{k}")
                Sq.add("dve", lambda e, si=si, k=k, c0=c0, c1=c1, W=W: e.tensor_tensor(
                        xnc[:, k, c0:c1], xc[:, si, 0:W],
                        pb[3][:, 0:W], ALU.subtract),
                       waits=(tw + "bcb",) + (tuple(rXN.wr(tw + "xn0")) if k == 0 else ()),
                       signal=(tw + "xn") if k == KCH - 1 else tw + f"xn{k}")
                rXC[si].rd(tw + ("xn" if k == KCH - 1 else f"xn{k}"))
                rPB[3].rd(tw + ("xn" if k == KCH - 1 else f"xn{k}"))
                XN = (A + "w3xn",) + ((A + "w0xn",) if ia == 0 else ())

        if not SO:
            Sq.add("qgps", lambda e, ia=ia: [e.dma_start(
                    out=xnr_d[:, :, ia*512:(ia+1)*512], in_=xnc[:, :, 3:515])],
                   waits=XN, signal=A + "xnst", n_sig=1)
            rXN.rd(A + "xnst")
        if ia > 0:
            Sq.add("dve", lambda e, NCC_P=NCC_P: e.tensor_copy(
                    xbcp[:, 0:NCC_P, 0:3], tailb[:, 0:NCC_P, :]),
                   waits=tuple(rXBC.wr(A + "tail")) + tuple(rTAIL.rd(A + "tail")),
                   signal=A + "tail")
            halo_done = (A + "tail",)

        def proj(oc, tag, n_cols, out_fn, evac_engine, evac_extra_waits=()):
            src_, co, width = wsrc(oc)
            buf = proj.bufctr % 2
            proj.bufctr += 1
            pi = 4 + (proj.psctr % 2)
            proj.psctr += 1
            tg = A + tag
            def ldw(e, oc=oc, width=width, buf=buf):
                return [e.dma_start(out=wtile[:, buf, :, 0:width],
                                    in_=wallb_dram[oc][:, :, 0:width])]
            Sq.add("qgps", ldw, waits=(f"wfo{oc}",) + tuple(rWT[buf].wr(tg + "ld")),
                   signal=tg + "ld", n_sig=1)
            def mms(e, width=width, buf=buf, pi=pi, n_cols=n_cols):
                last = None
                for k in range(KCH):
                    last = e.matmul(pb[pi][0:width, 0:n_cols], wtile[:, buf, k, 0:width],
                                    xnc[:, k, 515-n_cols:515] if n_cols == 512 else xnc[:, k, 0:3],
                                    start=(k == 0), stop=(k == KCH - 1))
                return last
            Sq.add("pe", mms, waits=(tg + "ld",) + XN + tuple(rPB[pi].wr(tg + "mm")),
                   signal=tg + "mm")
            rWT[buf].rd(tg + "mm")
            rXN.rd(tg + "mm")
            ew = (tg + "mm",) + evac_extra_waits + tuple(rPB[pi].rd(tg + "ev"))
            Sq.add(evac_engine, lambda e, pi=pi: out_fn(e, pb[pi]), waits=ew, signal=tg + "ev")
            return tg
        proj.bufctr = 0
        proj.psctr = 0
        if ia > 0:
            rPB[4].w = rPB[4].w

        for oc in range(NCC_P):
            def ev(e, ps, oc=oc):
                return e.activation(out=xbcp[:, oc, 3:515], in_=ps[:], func=AF.Copy)
            tg = proj(oc, f"px{oc}", 512, ev, "act",
                      evac_extra_waits=tuple(rXBC.wr(A + f"px{oc}ev")) +
                                       ((A + "tail",) if ia > 0 else ()))
        if ia == 0:
            for oc in range(NCC_P):
                def ev(e, ps, oc=oc):
                    return e.activation(out=xbcp[:, oc, 0:3], in_=ps[:, 0:3], func=AF.Copy)
                proj(oc, f"ph{oc}", 3, ev, "act")
            halo_done = (A + f"ph{NCC_P-1}ev",)
        HALO = halo_done

        Sq.add("dve", lambda e, NCC_P=NCC_P: e.tensor_copy(
                tailb[:, 0:NCC_P, :], xbcp[:, 0:NCC_P, 512:515]),
               waits=(A + f"px{NCC_P-1}ev",) + HALO + tuple(rTAIL.wr(A + "tsv")) +
                     tuple(rXBC.rd(A + "tsv")),
               signal=A + "tsv")
        for cc in range(NCC_P):
            tg = A + f"cv{cc}"
            buf = cc % 2
            Sq.add("qsync", lambda e, cc=cc, buf=buf: [e.dma_start(out=dgt[:, buf, :, :], in_=diag_dram[cc])],
                   waits=DIAG_READY + tuple(rDG[buf].wr(tg + "ld")), signal=tg + "ld", n_sig=1)
            def cmm(e, cc=cc, buf=buf):
                last = None
                for k in range(4):
                    last = e.matmul(pb[6][:], dgt[:, buf, k, :], xbcp[:, cc, k:k+512],
                                    start=(k == 0), stop=(k == 3))
                return last
            Sq.add("pe", cmm, waits=(tg + "ld", A + f"px{cc}ev") + HALO +
                             tuple(rPB[6].wr(tg + "mm")), signal=tg + "mm")
            rDG[buf].rd(tg + "mm")
            rXBC.rd(tg + "mm")
            Sq.add("act", lambda e, cc=cc: e.activation(
                    out=convout[:, cc, :], in_=pb[6][:], func=AF.Silu, bias=convb[:, cc:cc+1]),
                   waits=(tg + "mm", A + "tsv") + tuple(rPB[6].rd(tg + "ev")) +
                         tuple(rCO.wr(tg + "ev")),
                   signal=tg + "ev")
        CONVDONE = (A + f"cv{NCC_P-1}ev",)

        if not SO:
            for oc in range(48, 80):
                def ev(e, ps, oc=oc):
                    return e.activation(out=gate[:, oc - 48, :], in_=ps[:], func=AF.Copy)
                proj(oc, f"pz{oc}", 512, ev, "act",
                     evac_extra_waits=tuple(rGNB.wr(A + f"pz{oc}ev")))
            GATE_RDY = (A + "pz79ev",)
            Sq.add("qgps", lambda e, ia=ia: [e.dma_start(out=gate_dram[ia], in_=gnb[:])],
                   waits=GATE_RDY, signal=A + "gst", n_sig=1)
            rGNB.rd(A + "gst")
        def evdt(e, ps):
            return e.tensor_scalar(dtp[:, 0, :], ps[0:64, :], dtbc[:], None, ALU.add)
        proj(80, "pdt", 512, evdt, "dve",
             evac_extra_waits=tuple(rDTP.wr(A + "pdtev")) + CON)

        Sq.add("act", lambda e: e.activation(out=dtp[:, 1, :], in_=dtp[:, 0, :], func=AF.Abs),
               waits=(A + "pdtev",), signal=A + "dtab")
        Sq.add("act", lambda e: e.activation(out=dtp[:, 2, :], in_=dtp[:, 1, :], func=AF.Exp, scale=-1.0),
               waits=(A + "dtab",), signal=A + "dte")
        Sq.add("act", lambda e: e.activation(out=dtp[:, 2, :], in_=dtp[:, 2, :], func=AF.Ln, bias=1.0),
               signal=A + "dtl")
        Sq.add("dve", lambda e: e.tensor_scalar(dtp[:, 1, :], dtp[:, 0, :], 0.0, None, ALU.max),
               waits=(A + "dte",), signal=A + "dtr")
        Sq.add("dve", lambda e: e.tensor_tensor(dtp[:, 3, :], dtp[:, 1, :], dtp[:, 2, :], ALU.add),
               waits=(A + "dtl",), signal=A + "dtv")
        Sq.add("act", lambda e: e.activation(out=dtp[:, 0, :], in_=dtp[:, 3, :], func=AF.Ln),
               waits=(A + "dtv",), signal=A + "dtln")
        Sq.add("dve", lambda e: e.tensor_scalar(dtp[:, 1, :], dtp[:, 3, :], acol[:], None, ALU.mult),
               waits=(A + "dtv", "acol"), signal=A + "dta")
        for il in range(4):
            Sq.add("dve", lambda e, il=il: e.tensor_tensor_scan(
                    dtp[:, 2, il*128:(il+1)*128], ones64[:], dtp[:, 1, il*128:(il+1)*128],
                    0.0, ALU.mult, ALU.add),
                   waits=(A + "dta", "ones64"), signal=A + f"lcs{il}")
        Sq.add("dve", lambda e: e.tensor_tensor(dtp[:, 3, :], dtp[:, 0, :], dtp[:, 2, :], ALU.subtract),
               waits=(A + "dtln", A + "lcs3"), signal=A + "q")
        rDTP.rd(A + "q")

        for il in range(4):
            L = A + f"l{il}"
            lc0 = il * 128
            for w in range(10):
                is_b = w >= 8
                def tmm(e, w=w, lc0=lc0, is_b=is_b):
                    last = None
                    for j in range(4):
                        cc = (w * 4 + j) if not is_b else (32 + (w - 8) * 4 + j)
                        last = e.transpose(pbf(7)[:, j*128:(j+1)*128],
                                           convout[:, cc, lc0:lc0+128], identb[:])
                    return last
                Sq.add("pe", tmm, waits=CONVDONE + ("identb",) + tuple(rPB[7].wr(L + f"tp{w}")),
                       signal=L + f"tp{w}")
                rCO.rd(L + f"tp{w}")
                if not is_b:
                    Sq.add("dve", lambda e, w=w: e.tensor_copy(xTt[:, w*4:(w+1)*4, :],
                            pbf(7)[:, 0:512].rearrange("p (a b) -> p a b", b=128)),
                           waits=(L + f"tp{w}",) + (tuple(rXT.wr(L + "tpe0")) if w == 0 else ()) +
                                 tuple(rPB[7].rd(L + f"tpe{w}")),
                           signal=L + f"tpe{w}")
                else:
                    Sq.add("dve", lambda e, w=w: e.tensor_copy(bTt[:, (w-8)*4:(w-7)*4, :],
                            pbf(7)[:, 0:512].rearrange("p (a b) -> p a b", b=128)),
                           waits=(L + f"tp{w}",) + (tuple(rBT.wr(L + "tpe8")) if w == 8 else ()) +
                                 tuple(rPB[7].rd(L + f"tpe{w}")),
                           signal=L + f"tpe{w}")
            if not SO:
                Sq.add("pe", lambda e, lc0=lc0: e.transpose(pb[7][:, 0:64],
                        dtp[:, 3, lc0:lc0+128], identf[0:64, 0:64]),
                       waits=(A + "q",) + tuple(rPB[7].wr(L + "qtm")), signal=L + "qtm")
                rDTP.rd(L + "qtm")
                Sq.add("dve", lambda e: e.tensor_copy(qtok[:], pb[7][:, 0:64]),
                       waits=(L + "qtm",) + tuple(rQT.wr(L + "qte")) + tuple(rPB[7].rd(L + "qte")),
                       signal=L + "qte")
                Sq.add("qgps", lambda e, lc0=lc0: [e.dma_start(out=shuf_dram, in_=dtp[:, 2, lc0:lc0+128])],
                       waits=(A + "q",) + tuple(rSHUF.wr(L + "shst")), signal=L + "shst", n_sig=1)
                rDTP.rd(L + "shst")
            Sq.add("pe", lambda e, lc0=lc0: e.transpose(pb[7][0:1, 0:64],
                    dtp[:, 2, lc0+127:lc0+128], identf[0:64, 0:64]),
                   waits=(A + "q",) + tuple(rPB[7].wr(L + "crm")), signal=L + "crm")
            rDTP.rd(L + "crm")
            Sq.add("dve", lambda e: e.tensor_copy(crow[:], pb[7][0:1, 0:64]),
                   waits=(L + "crm",) + tuple(rCROW.wr(L + "cre")) + tuple(rPB[7].rd(L + "cre")),
                   signal=L + "cre")
            Sq.add("act", lambda e: e.activation(out=cexp[:], in_=crow[:], func=AF.Exp),
                   waits=(L + "cre",) + tuple(rCEXP.wr(L + "cex")), signal=L + "cex")
            rCROW.rd(L + "cex")
            if not SO:
                Sq.add("qgps", lambda e, l4=4*ia+il: [e.dma_start(
                        out=cexp_dram[l4:l4+1, :], in_=cexp[:])],
                       waits=(L + "cex",), signal=L + "cxst", n_sig=1)
                rCEXP.rd(L + "cxst")
            Sq.add("pe", lambda e: e.matmul(pb[7][:, 0:64], ones1[:], cexp[:],
                    start=True, stop=True, skip_group_check=True),
                   waits=(L + "cex", "ones1") + tuple(rPB[7].wr(L + "cbc")), signal=L + "cbc")
            rCEXP.rd(L + "cbc")
            Sq.add("act", lambda e, lc0=lc0: e.activation(out=wfm[:], in_=dtp[:, 3, lc0:lc0+128],
                    func=AF.Exp, bias=dtp[:, 2, lc0+127:lc0+128]),
                   waits=(A + "q",) + tuple(rWFM.wr(L + "wfm")), signal=L + "wfm")
            rDTP.rd(L + "wfm")
            Sq.add("pe", lambda e: e.transpose(pb[6][:, 0:64], wfm[:], identf[0:64, 0:64]),
                   waits=(L + "wfm",) + tuple(rPB[6].wr(L + "wtm")), signal=L + "wtm")
            rWFM.rd(L + "wtm")
            Sq.add("dve", lambda e: e.tensor_copy(wtok[:], pb[6][:, 0:64]),
                   waits=(L + "wtm",) + tuple(rWTOK.wr(L + "wte")) + tuple(rPB[6].rd(L + "wte")),
                   signal=L + "wte")

            for g in (range(0) if SO else range(NG)):
                G = L + f"g{g}"
                Sq.add("qgps", lambda e, g=g: [e.dma_start(out=lc1p[:],
                        in_=shuf_dram[8*g:8*g+8, :].rearrange("h t -> (h t)").unsqueeze(0))],
                       waits=(L + "shst",) + tuple(rLC1.wr(G + "shq")), signal=G + "shq", n_sig=1)
                def t1mm(e):
                    e.matmul(pb[0][:], ones1[:], lc1p[:, 0:512],
                             start=True, stop=True, skip_group_check=True)
                    return e.matmul(pb[1][:], ones1[:], lc1p[:, 512:1024],
                                    start=True, stop=True, skip_group_check=True)
                Sq.add("pe", t1mm, waits=(G + "shq", "ones1") + tuple(rPB[0].wr(G + "t1")) +
                               tuple(rPB[1].wr(G + "t1")), signal=G + "t1")
                rLC1.rd(G + "t1")
                Sq.add("dve", lambda e, g=g: e.tensor_tensor(
                        qm[:], qtok[:, 8*g:8*g+8].unsqueeze(2).broadcast_to([128, 8, 128]),
                        maskt[:].unsqueeze(1).broadcast_to([128, 8, 128]), ALU.add),
                       waits=(L + "qte",) + CON + tuple(rQM.wr(G + "qm")), signal=G + "qm")
                rQT.rd(G + "qm")
                def t2madd(e):
                    e.tensor_tensor(qm[:, 0:4, :], qm[:, 0:4, :],
                                    pb[0][:].rearrange("p (a b) -> p a b", b=128), ALU.add)
                    return e.tensor_tensor(qm[:, 4:8, :], qm[:, 4:8, :],
                                           pb[1][:].rearrange("p (a b) -> p a b", b=128), ALU.add)
                Sq.add("dve", t2madd, waits=(G + "qm", G + "t1"), signal=G + "t2m")
                Sq.add("act", lambda e: e.activation(out=ebuf[:], in_=qm[:], func=AF.Exp),
                       waits=(G + "t2m",) + tuple(rE.wr(G + "e")), signal=G + "e")
                rQM.rd(G + "e")
                def eexta(e):
                    e.activation(out=eext[:, 0:4, :], in_=pb[0][:], func=AF.Exp)
                    return e.activation(out=eext[:, 4:8, :], in_=pb[1][:], func=AF.Exp)
                Sq.add("act", eexta, waits=(G + "t1",) + tuple(rEE.wr(G + "ee")), signal=G + "ee")
                rPB[0].rd(G + "ee"); rPB[1].rd(G + "ee")
                rPB[0].rd(G + "t2m"); rPB[1].rd(G + "t2m")
                Sq.add("pe", lambda e, g=g, lc0=lc0: e.matmul(
                        pb[6][:, 0:128], convout[:, 32+g, lc0:lc0+128],
                        convout[:, 40+g, lc0:lc0+128], start=True, stop=True,
                        skip_group_check=True),
                       waits=CONVDONE + tuple(rPB[6].wr(G + "gp")), signal=G + "gp")
                rCO.rd(G + "gp")
                Sq.add("dve", lambda e: e.tensor_tensor(
                        mprime[:], ebuf[:], pb[6][:, 0:128].unsqueeze(1).broadcast_to([128, 8, 128]),
                        ALU.mult),
                       waits=(G + "e", G + "gp") + tuple(rMP.wr(G + "mp")), signal=G + "mp")
                rE.rd(G + "mp"); rPB[6].rd(G + "mp")
                Sq.add("dve", lambda e, g=g, lc0=lc0: e.tensor_tensor(
                        mext[:], eext[:],
                        convout[:, 40+g, lc0:lc0+128].unsqueeze(1).broadcast_to([128, 8, 128]),
                        ALU.mult),
                       waits=(G + "ee",) + CONVDONE + tuple(rME.wr(G + "me")), signal=G + "me")
                rEE.rd(G + "me"); rCO.rd(G + "me")
                Sq.add("qgps", lambda e, g=g, l4=4*ia+il: [e.dma_start(
                        out=mext_dram[l4, g], in_=mext[:])],
                       waits=(G + "me",), signal=G + "mst", n_sig=1)
                rME.rd(G + "mst")
                Sq.add("dve", lambda e, g=g: e.tensor_copy(Sb[:], S[:, g, :]),
                       waits=tuple(rS.rd(G + "sb")) + tuple(rSB.wr(G + "sb")), signal=G + "sb")
                def ymm(e, g=g):
                    last = None
                    for cc4 in range(4):
                        h0, h1 = 2 * cc4, 2 * cc4 + 1
                        cc = 4 * g + cc4
                        e.matmul(pb[2][0:64, cc4*128:(cc4+1)*128], xTt[:, cc, 0:64],
                                 mprime[:, h0, :], start=True, stop=False, skip_group_check=True)
                        e.matmul(pb[2][0:64, cc4*128:(cc4+1)*128], Sb[:, h0*64:(h0+1)*64],
                                 mext[:, h0, :], start=False, stop=True, skip_group_check=True)
                        e.matmul(pb[2][64:128, cc4*128:(cc4+1)*128], xTt[:, cc, 64:128],
                                 mprime[:, h1, :], start=True, stop=False,
                                 tile_position=(0, 64), skip_group_check=True)
                        last = e.matmul(pb[2][64:128, cc4*128:(cc4+1)*128], Sb[:, h1*64:(h1+1)*64],
                                        mext[:, h1, :], start=False, stop=True,
                                        tile_position=(0, 64), skip_group_check=True)
                    return last
                Sq.add("pe", ymm, waits=(G + "mp", G + "me", L + "tpe7", G + "sb") +
                               tuple(rPB[2].wr(G + "ymm")), signal=G + "ymm")
                rMP.rd(G + "ymm"); rME.rd(G + "ymm"); rXT.rd(G + "ymm"); rSB.rd(G + "ymm")
                def yev(e, g=g, lc0=lc0):
                    last = None
                    for cc4 in range(4):
                        cc = 4 * g + cc4
                        last = e.scalar_tensor_tensor(
                            y_sb[:, cc, :], convout[:, cc, lc0:lc0+128], drep[:, cc:cc+1],
                            pb[2][:, cc4*128:(cc4+1)*128], ALU.mult, ALU.add)
                    return last
                Sq.add("dve", yev, waits=(G + "ymm",) + CON +
                               (tuple(rYSB.wr(G + "yev")) if g == 0 else ()),
                       signal=G + "yev")
                rPB[2].rd(G + "yev"); rCO.rd(G + "yev")
            Sq.add("dve", lambda e: e.tensor_tensor(
                    xTh, xTh, wtok[:].unsqueeze(2).broadcast_to([128, 64, 64]), ALU.mult),
                   waits=(L + "wte",) + tuple(rXT.wr(L + "xscl")), signal=L + "xscl")
            rWTOK.rd(L + "xscl")
            for g in range(NG):
                def dsmm(e, g=g):
                    return e.matmul(pb[3][:], bTt[:, g, :], xT3[:, 512*g:512*(g+1)],
                                    start=True, stop=True, skip_group_check=True)
                Sq.add("pe", dsmm, waits=(L + "xscl", L + "tpe9") + tuple(rPB[3].wr(L + f"ds{g}")),
                       signal=L + f"ds{g}")
                rBT.rd(L + f"ds{g}"); rXT.rd(L + f"ds{g}")
                def supd(e, g=g):
                    e.tensor_tensor(scr[:].rearrange("p (a b) -> p a b", b=64),
                                    S[:, g, :].rearrange("p (a b) -> p a b", b=64),
                                    pb[7][:, 8*g:8*g+8].unsqueeze(2).broadcast_to([128, 8, 64]),
                                    ALU.mult)
                    return e.tensor_tensor(S[:, g, :], scr[:], pb[3][:], ALU.add)
                Sq.add("dve", supd, waits=(L + f"ds{g}", L + "cbc") +
                               (tuple(rS.wr(L + "supd0")) if g == 0 else ()) +
                               (tuple(rSCR.wr(L + "supd0")) if g == 0 else ()),
                       signal=L + f"supd{g}")
                rPB[3].rd(L + f"supd{g}"); rPB[7].rd(L + f"supd{g}")
            SUPD = (L + f"supd{NG-1}",)
            rS.w = L + f"supd{NG-1}"
            if not SO:
                Sq.add("qgps", lambda e, l4=4*ia+il: [e.dma_start(
                        out=y_dram[l4], in_=y_sb[:])],
                       waits=(L + f"g{NG-1}yev",), signal=L + "yst", n_sig=1)
                rYSB.rd(L + "yst")


    Sq.add("qgps", lambda e: [e.dma_start(out=cc_in, in_=S3)],
           waits=tuple(rS.rd("ccst")), signal="ccst", n_sig=1)
    Sq.semnames.append("ccv")
    def ccfn(e):
        e.collective_compute(
            "AllGather", ALU.bypass, ins=[cc_in], outs=[cc_out],
            replica_groups=[[0, 1, 2, 3, 4, 5, 6, 7]]).then_inc(sems["ccv"], 1)
        return []
    Sq.add("qgps", ccfn, waits=("ccst",), signal=None)
    Sq.events["ccg"] = ("ccv", 1)
    ar.off = mk_xc
    gtile = sbuf(128, [2, 4096], F32)
    wot2 = sbuf(128, [2, 32, 128], BF16)
    rGT = [R(), R()]
    for k in range(8):
        buf = k % 2
        Sq.add("qsync", lambda e, k=k, buf=buf: [e.dma_start(
                out=gtile[:, buf, :], in_=cc_out[128*k:128*(k+1), :])],
               waits=("ccg",) + tuple(rGT[buf].wr(f"gld{k}")), signal=f"gld{k}", n_sig=1)
        if k == 0:
            Sq.add("dve", lambda e: e.tensor_scalar(S3, gtile[:, 0, :], selc[:, 0:1],
                                                    None, ALU.mult),
                   waits=("gld0",) + CON + tuple(rS.wr("pac0")), signal="pac0")
            rGT[0].rd("pac0")
        else:
            Sq.add("dve", lambda e, k=k, buf=buf: e.tensor_scalar(
                    gtile[:, buf, :], gtile[:, buf, :], selc[:, k:k+1], None, ALU.mult),
                   waits=(f"gld{k}",), signal=f"gsc{k}")
            Sq.add("dve", lambda e, buf=buf: e.tensor_tensor(S3, S3, gtile[:, buf, :],
                                                             ALU.add),
                   signal=f"pac{k}")
            rGT[buf].rd(f"pac{k}")
    rS.w = "pac7"

    rY2 = R(); rG2 = R(); rSB2 = R(); rCX2 = R(); rSG2 = R(); rST2 = R()
    rMB = [R(), R()]
    rPB2 = [R() for _ in range(8)]
    rWO2 = [R(), R()]; rSCR2 = R()
    mbuf = [mext, mprime]
    for l in range(16):
        N = f"n{l}"
        M1 = f"ma{l // 4}l{l % 4}"
        lc0 = (l % 4) * 128
        m1done = ("ccst",) if l == 0 else ()
        Sq.add("qsync", lambda e, l=l: [e.dma_start(out=y_sb[:], in_=y_dram[l])],
               waits=(M1 + "yst",) + m1done + tuple(rY2.wr(N + "yld")),
               signal=N + "yld", n_sig=1)
        Sq.add("qsync", lambda e, l=l: [e.dma_start(
                out=xTt[:], in_=gate_dram[l // 4][:, :, (l % 4)*128:(l % 4)*128+128])],
               waits=(f"ma{l // 4}gst",) + m1done + tuple(rG2.wr(N + "gld")),
               signal=N + "gld", n_sig=1)
        Sq.add("qsync", lambda e, l=l: [e.dma_start(out=cexp[:], in_=cexp_dram[l:l+1, :])],
               waits=(M1 + "cxst",) + m1done + tuple(rCX2.wr(N + "cxl")),
               signal=N + "cxl", n_sig=1)
        Sq.add("pe", lambda e: e.matmul(pb[7][:, 0:64], ones1[:], cexp[:],
                start=True, stop=True, skip_group_check=True),
               waits=(N + "cxl", "ones1") + tuple(rPB2[7].wr(N + "cbc")), signal=N + "cbc")
        rCX2.rd(N + "cbc")
        for g in range(NG):
            G = N + f"g{g}"
            mb = mbuf[g % 2]
            Sq.add("qsync", lambda e, l=l, g=g, mb=mb: [e.dma_start(
                    out=mb[:], in_=mext_dram[l, g])],
                   waits=(M1 + f"g{g}mst",) + (("ccst",) if l == 0 and g < 2 else ()) +
                         tuple(rMB[g % 2].wr(G + "ml")),
                   signal=G + "ml", n_sig=1)
            Sq.add("dve", lambda e, g=g: e.tensor_copy(Sb[:], S[:, g, :]),
                   waits=tuple(rS.rd(G + "pb")) + tuple(rSB2.wr(G + "pb")),
                   signal=G + "pb")
            dpi = 2 + (l * NG + g) % 2
            def dymm(e, mb=mb, dpi=dpi):
                last = None
                for cc4 in range(4):
                    h0, h1 = 2 * cc4, 2 * cc4 + 1
                    e.matmul(pb[dpi][0:64, cc4*128:(cc4+1)*128], Sb[:, h0*64:(h0+1)*64],
                             mb[:, h0, :], start=True, stop=True, skip_group_check=True)
                    last = e.matmul(pb[dpi][64:128, cc4*128:(cc4+1)*128],
                                    Sb[:, h1*64:(h1+1)*64], mb[:, h1, :],
                                    start=True, stop=True, tile_position=(0, 64),
                                    skip_group_check=True)
                return last
            Sq.add("pe", dymm, waits=(G + "ml", G + "pb") + tuple(rPB2[dpi].wr(G + "dy")),
                   signal=G + "dy")
            rSB2.rd(G + "dy"); rMB[g % 2].rd(G + "dy")
            Sq.add("dve", lambda e, g=g, dpi=dpi: e.tensor_tensor(
                    y_sb[:, 4*g:4*g+4, :], y_sb[:, 4*g:4*g+4, :],
                    pb[dpi][:].rearrange("p (a b) -> p a b", b=128), ALU.add),
                   waits=(G + "dy", N + "yld"), signal=G + "ya")
            rPB2[dpi].rd(G + "ya")
            Sq.add("dve", lambda e, g=g: e.tensor_tensor(
                    S[:, g, :].rearrange("p (a b) -> p a b", b=64),
                    S[:, g, :].rearrange("p (a b) -> p a b", b=64),
                    pb[7][:, 8*g:8*g+8].unsqueeze(2).broadcast_to([128, 8, 64]),
                    ALU.mult),
                   waits=(N + "cbc",) + tuple(rS.wr(G + "pu")), signal=G + "pu")
            rPB2[7].rd(G + "pu")
        Sq.add("act", lambda e: e.activation(out=sgb[:], in_=xTt[:], func=AF.Silu),
               waits=(N + "gld",) + tuple(rSG2.wr(N + "sg")), signal=N + "sg")
        rG2.rd(N + "sg")
        Sq.add("dve", lambda e: e.tensor_tensor(sgb[:], y_sb[:], sgb[:], ALU.mult),
               waits=(N + "sg", N + f"g{NG-1}ya"), signal=N + "gmul")
        Sq.add("act", lambda e: e.activation(out=y_sb[:], in_=sgb[:], func=AF.Square),
               waits=(N + "gmul",), signal=N + "gsq")
        def gsmm2(e):
            last = None
            for cc in range(32):
                last = e.matmul(pb[6][0:1, 0:128], onescolb[:], y_sb[:, cc, :],
                                start=(cc == 0), stop=(cc == 31))
            return last
        Sq.add("pe", gsmm2, waits=(N + "gsq", "onescolb") + tuple(rPB2[6].wr(N + "gsmm")),
               signal=N + "gsmm")
        rY2.w = N + "gsmm"
        Sq.add("dve", lambda e: e.tensor_scalar_mul(statg[0:1, 0, 0:128],
                pb[6][0:1, 0:128], 1.0 / INTER),
               waits=(N + "gsmm",) + tuple(rST2.wr(N + "gvar")), signal=N + "gvar")
        rPB2[6].rd(N + "gvar")
        Sq.add("act", lambda e: e.activation(out=statg[0:1, 1, 0:128],
                in_=statg[0:1, 0, 0:128], func=AF.Sqrt, bias=epscol[0:1]),
               waits=(N + "gvar", "epscol"), signal=N + "gsqr")
        Sq.add("dve", lambda e: e.reciprocal(statg[0:1, 1, 0:128], statg[0:1, 1, 0:128]),
               waits=(N + "gsqr",), signal=N + "grs")
        Sq.add("pe", lambda e: e.matmul(pb[6][:, 0:128], ones1[:], statg[0:1, 1, 0:128],
                start=True, stop=True, skip_group_check=True),
               waits=(N + "grs",) + tuple(rPB2[6].wr(N + "gbc")), signal=N + "gbc")
        rST2.rd(N + "gbc")
        Sq.add("dve", lambda e, lc0=lc0: e.tensor_tensor(
                gnb[:, :, lc0:lc0+128], sgb[:],
                pb[6][:, 0:128].unsqueeze(1).broadcast_to([128, 32, 128]), ALU.mult),
               waits=(N + "gbc",) + (("ma3gst",) if l == 0 else ()) +
                     tuple(rGNB.wr(N + "norm")),
               signal=N + "norm")
        rSG2.rd(N + "norm"); rPB2[6].rd(N + "norm")
        if l % 4 == 3:
            ia2 = l // 4
            for oc in range(KCH):
                tg = f"n{l}op{oc}"
                pi = 4 + (oc % 2)
                wb = oc % 2
                Sq.add("qsync" if oc % 2 == 0 else "qgps",
                       lambda e, oc=oc, wb=wb: [e.dma_start(out=wot2[:, wb, :, :],
                        in_=woutb_dram[oc])],
                       waits=WOUT_READY + tuple(rWO2[wb].wr(tg + "ld")),
                       signal=tg + "ld", n_sig=1)
                def omm(e, pi=pi, wb=wb):
                    last = None
                    for kc in range(32):
                        last = e.matmul(pb[pi][:], wot2[:, wb, kc, :], gnb[:, kc, :],
                                        start=(kc == 0), stop=(kc == 31))
                    return last
                Sq.add("pe", omm, waits=(tg + "ld", N + "norm") +
                               tuple(rPB2[pi].wr(tg + "mm")), signal=tg + "mm")
                rWO2[wb].rd(tg + "mm")
                rGNB.rd(tg + "mm")
                Sq.add("dve", lambda e, pi=pi: e.tensor_copy(astg[:], pb[pi][:]),
                       waits=(tg + "mm",) + tuple(rSCR2.wr(tg + "ev")), signal=tg + "ev")
                rPB2[pi].rd(tg + "ev")
                Sq.add("qgps", lambda e, oc=oc, ia2=ia2: [e.dma_start(
                        out=attn_out.rearrange("(c p) t -> p c t", p=128)[:, oc, 512*ia2:512*(ia2+1)],
                        in_=astg[:])],
                       waits=(tg + "ev",), signal=tg + "st", n_sig=1)
                rSCR2.rd(tg + "st")

    ar.off = ar.base
    rep = sbuf(4, [128], BF16)
    shc = sbuf(128, [1], I32)
    xn = sbuf(128, [KCH, T], BF16)
    bytR = sbuf(128, [2, 2048], I32)
    kiF = sbuf(128, [2048], I32)
    qaF = sbuf(128, [2048], BF16)
    qbF = sbuf(128, [2048], BF16)
    scstB = sbuf(4, [2, 1024], BF16)
    wgB = sbuf(128, [2, KCH, 256], BF16)
    wuB = sbuf(128, [2, KCH, 256], BF16)
    interB = sbuf(128, [2, 4, T], BF16)
    wdR = sbuf(128, [2, 4, HID], BF16)
    dsF = sbuf(128, [4, 512], F32)
    qa3 = qaF.rearrange("p (a f) -> p a f", f=256)

    bankG = [PST[:, 0:512], PST[:, 512:1024]]
    bankU = [PST[:, 1024:1536], PST[:, 1536:2048]]
    bankD = [PST[:, 2048:2560], PST[:, 2560:3072]]
    bankS = [PST[:, 3072:3584], PST[:, 3584:4096]]

    NSL = NMLP // 256
    NGRP = NSL // 2

    Sq.barrier("MF")
    Sq.add("qsync", lambda e: e.dma_start(out=rep[:], in_=rep_in), signal="rep")
    Sq.add("qsync", lambda e: e.dma_start(out=shc[:], in_=shc_in), signal="shcl")
    Sq.add("qgps", lambda e: e.dma_start(out=xn[:], in_=xn_in.rearrange("(k p) t -> p k t", p=128)),
           signal="xn")

    pgr = pg_in.rearrange("(kc b) n -> b kc n", b=64)
    pur = pu_in.rearrange("(kc b) n -> b kc n", b=64)
    sgr = sg_in.rearrange("(kc b) n -> b kc n", b=4)
    sur = su_in.rearrange("(kc b) n -> b kc n", b=4)
    sdr = sd_in.rearrange("(nc b) h -> b nc h", b=4)

    CN = {"dg": 0, "sx": 0, "gm": 0, "um": 0, "im": 0, "ge": 0, "dm": 0, "cl": 0}
    dec_done = {}
    dwd_ev = {}
    dmz_ev = {}
    pe_sexp = []

    def emit_ext_val(ldev, first_wts=()):
        n = CN["dg"] - 1
        Sq.add("dve", lambda e: e.tensor_tensor(
                bytR[:, n % 2, :], bytR[:, n % 2, :],
                shc[:].broadcast_to([128, 2048]), ALU.logical_shift_right),
               waits=(ldev, "shcl") + tuple(first_wts))
        Sq.add("dve", lambda e: e.tensor_scalar(
                kiF[:], bytR[:, n % 2, :], 7, None, ALU.bitwise_and))
        Sq.add("dve", lambda e: e.tensor_scalar(qaF[:], kiF[:], -2.0, 2.0, ALU.add, ALU.mult))
        Sq.add("dve", lambda e: e.tensor_scalar(qbF[:], kiF[:], -4.0, 4.0, ALU.add, ALU.mult))
        Sq.add("dve", lambda e: e.tensor_tensor(qaF[:], qaF[:], qbF[:], ALU.max))
        Sq.add("dve", lambda e: e.tensor_tensor(qaF[:], qaF[:], kiF[:], ALU.max))
        Sq.add("dve", lambda e: e.tensor_scalar(
                kiF[:], bytR[:, n % 2, :], 8, None, ALU.bitwise_and),
               signal=f"fv{n}")
        Sq.add("dve", lambda e: e.scalar_tensor_tensor(qaF[:], kiF[:], -4.0, qaF[:],
                                                       ALU.add, ALU.mult))

    pe_sld = []

    def sexp_item(ld_fn, mm_waits_extra=()):
        m = CN["sx"]; CN["sx"] += 1
        swts = (f"sx{m-2}",) if m >= 2 else ()
        pe_sld.append((ld_fn(m % 2), swts, f"sl{m}"))
        pwts = [f"sl{m}"]
        if m == 0:
            pwts.append("rep")
        if m >= 1:
            pwts.append(f"fw{m-1}")
        def mm(e, m=m):
            e.matmul(bankS[0], rep[:], scstB[:, m % 2, 0:512],
                     start=True, stop=True, skip_group_check=True)
            return e.matmul(bankS[1], rep[:], scstB[:, m % 2, 512:1024],
                            start=True, stop=True, skip_group_check=True)
        pe_sexp.append((mm, tuple(pwts) + tuple(mm_waits_extra), f"sx{m}"))
        return m

    def emit_decode_gu(gs, mi, inject=None):
        psrc, ssrc, wt = (pgr, sgr, wgB) if mi == 0 else (pur, sur, wuB)
        buf = gs % 2
        for d in range(2):
            n = CN["dg"]; CN["dg"] += 1
            wts = (f"fv{n-2}",) if n >= 2 else ()
            bslot3 = bytR[:, n % 2, :].rearrange("p (kc f) -> p kc f", f=256)
            def ld(e, psrc=psrc, d=d, gs=gs, bslot3=bslot3):
                return [e.dma_start(out=bslot3[0:64],
                                    in_=psrc[0:64, 8*d:8*d+8, 256*gs:256*(gs+1)]),
                        e.dma_start(out=bslot3[64:128],
                                    in_=psrc[0:64, 8*d:8*d+8, 256*gs:256*(gs+1)])]
            Sq.add("qsync", ld, waits=wts, signal=f"fl{n}", n_sig=2)
            emit_ext_val(f"fl{n}")
            for gh in range(2):
                kc0 = 8 * d + 4 * gh
                def sld(e, ssrc=ssrc, kc0=kc0, gs=gs, slot=None):
                    return [e.dma_start(
                        out=scstB[:, slot, :].rearrange("p (a f) -> p a f", f=256),
                        in_=ssrc[0:4, kc0:kc0+4, 256*gs:256*(gs+1)])]
                m = sexp_item(lambda slot, sld=sld: (lambda e: sld(e, slot=slot)))
                wwts = [f"sx{m}"]
                if d == 0 and gh == 0 and gs >= 2:
                    wwts.append(f"lgm{gs-2}" if mi == 0 else f"lum{gs-2}")
                sig = None
                if d == 1 and gh == 1:
                    sig = f"dec{gs}m{mi}"
                    dec_done[(gs, mi)] = sig
                def wm(e, wt=wt, buf=buf, kc0=kc0, gh=gh, m=m):
                    return e.tensor_tensor(
                        wt[:, buf, kc0:kc0+4, :],
                        qa3[:, 4*gh:4*gh+4, :],
                        PST[:, 3072:4096].rearrange("p (a f) -> p a f", f=256), ALU.mult)
                Sq.add("dve", wm, waits=tuple(wwts), signal=sig or (f"fw{m}"))
                if sig:
                    Sq.events[f"fw{m}"] = Sq.events[sig]
                if inject and (d, gh) in inject:
                    inject[(d, gh)]()

    def emit_decode_down(s, jn, inject=None):
        G = s // 2
        gslot = G % 2
        if True:
            nch = 2 * s + jn
            j = 2 * (s % 2) + jn
            n = CN["dg"]; CN["dg"] += 1
            wts = (f"fv{n-2}",) if n >= 2 else ()
            def ld(e, nch=nch, n=n):
                return [e.dma_start(out=bytR[0:64, n % 2, :], in_=pd_in[64*nch:64*nch+64, :]),
                        e.dma_start(out=bytR[64:128, n % 2, :], in_=pd_in[64*nch:64*nch+64, :])]
            Sq.add("qsync", ld, waits=wts, signal=f"fl{n}", n_sig=2)
            fw = ()
            if jn == 0 and s % 2 == 0 and G >= 2:
                fw = (dmz_ev[G - 2],)
            emit_ext_val(f"fl{n}", first_wts=fw)
            for rh in range(2):
                def sld(e, nch=nch, rh=rh, slot=None):
                    return [e.dma_start(
                        out=scstB[:, slot, :].rearrange("p (o f) -> p o f", o=1),
                        in_=sdr[0:4, nch:nch+1, 1024*rh:1024*(rh+1)])]
                m = sexp_item(lambda slot, sld=sld: (lambda e: sld(e, slot=slot)))
                sig = None
                if jn == 1 and rh == 1:
                    sig = f"dwd{s}"
                    dwd_ev[s] = sig
                def wm(e, gslot=gslot, j=j, rh=rh, m=m):
                    return e.tensor_tensor(
                        wdR[:, gslot, j, 1024*rh:1024*(rh+1)],
                        qaF[:, 1024*rh:1024*(rh+1)], PST[:, 3072:4096], ALU.mult)
                Sq.add("dve", wm, waits=(f"sx{m}",), signal=sig or (f"fw{m}"))
                if sig:
                    Sq.events[f"fw{m}"] = Sq.events[sig]
                if inject and rh in inject:
                    inject[rh]()

    def emit_imul(s, i):
        idx = CN["im"]; CN["im"] += 1
        assert idx == 8 * s + i
        j2, tb = i // 4, i % 4
        gslot = (s // 2) % 2
        jg = 2 * (s % 2) + j2
        def im(e, gslot=gslot, jg=jg, tb=tb, idx=idx):
            return e.tensor_tensor(interB[:, gslot, jg, tb*512:(tb+1)*512],
                                   interB[:, gslot, jg, tb*512:(tb+1)*512],
                                   bankU[idx % 2], ALU.mult)
        Sq.add("dve", im, waits=(f"ge{idx}", f"um{idx}"), signal=f"im{idx}")

    def pop_sexp(k):
        for _ in range(k):
            if pe_sexp:
                while pe_sld and (CN["sx"] - len(pe_sexp) + 2 >
                                  CN["sx"] - len(pe_sld)):
                    lfn, lwts, lsig = pe_sld.pop(0)
                    Sq.add("qgps", lfn, waits=lwts, signal=lsig, n_sig=1)
                fn, wts, sig = pe_sexp.pop(0)
                Sq.add("pe", fn, waits=wts, signal=sig)

    def emit_dmm_cell(G, q, tb):
        gslot = G % 2
        for hj in range(4):
            hc = 4 * q + hj
            cl = CN["cl"]; CN["cl"] += 1
            half = cl % 4
            dmidx = CN["dm"]; CN["dm"] += 1
            def dmm(e, hc=hc, tb=tb, dmidx=dmidx, gslot=gslot):
                last = None
                for j in range(4):
                    last = e.matmul(bankD[dmidx % 2],
                                    wdR[:, gslot, j, hc*128:(hc+1)*128],
                                    interB[:, gslot, j, tb*512:(tb+1)*512],
                                    start=(j == 0), stop=(j == 3))
                return last
            wts = [f"dv{dmidx-2}"] if dmidx >= 2 else []
            if hj == 0:
                wts.append(f"im{8*(2*G+1)+4+tb}")
                if q == 0 and tb == 0:
                    wts.append(dwd_ev[2 * G + 1])
            Sq.add("pe", dmm, waits=tuple(wts), signal=f"dm{dmidx}")
            if hc == 15 and tb == 3:
                dmz_ev[G] = f"dm{dmidx}"
            awts = [f"dm{dmidx}"]
            if cl >= 4:
                awts.append(f"sq{cl-4}")
            Sq.add("act", lambda e, half=half, dmidx=dmidx: e.activation(
                    out=dsF[:, half, :], in_=bankD[dmidx % 2], func=AF.Copy),
                   waits=tuple(awts), signal=f"dv{dmidx}")
            def store(e, G=G, hc=hc, tb=tb, half=half):
                if G > 0:
                    return [e.dma_start(out=acc_r[:, hc:hc+1, tb*512:(tb+1)*512],
                                        in_=dsF[:, half, :].unsqueeze(1),
                                        accum_op=ALU.add)]
                return [e.dma_start(out=acc_r[:, hc:hc+1, tb*512:(tb+1)*512],
                                    in_=dsF[:, half, :].unsqueeze(1))]
            Sq.add("qgps", store, waits=(f"dv{dmidx}",), signal=f"sq{cl}", n_sig=1)

    emit_decode_gu(0, 0)
    emit_decode_gu(0, 1)
    pop_sexp(len(pe_sexp))

    for s in range(NSL):
        G = s // 2
        emit_decode_down(s, 0)
        emit_decode_down(s, 1, inject={
            0: lambda: emit_imul(s, 0)})
        if s + 1 < NSL:
            emit_decode_gu(s + 1, 0, inject={
                (0, 0): lambda: (emit_imul(s, 1), emit_imul(s, 2)),
                (1, 1): lambda: emit_imul(s, 3)})
            emit_decode_gu(s + 1, 1, inject={
                (0, 0): lambda: emit_imul(s, 4),
                (0, 1): lambda: emit_imul(s, 5),
                (1, 0): lambda: emit_imul(s, 6)})
            emit_imul(s, 7)
        else:
            for i in range(1, 8):
                emit_imul(s, i)
        Gd = G - 1 if s >= 2 else None
        qbase = 2 * (s % 2)
        for i in range(8):
            j2, tb = i // 4, i % 4
            gmidx = CN["gm"]; CN["gm"] += 1
            assert gmidx == 8 * s + i
            def gmm(e, s=s, j2=j2, tb=tb, gmidx=gmidx):
                last = None
                for k in range(KCH):
                    last = e.matmul(bankG[gmidx % 2], wgB[:, s % 2, k, j2*128:(j2+1)*128],
                                    xn[:, k, tb*512:(tb+1)*512],
                                    start=(k == 0), stop=(k == KCH - 1))
                return last
            wts = [f"ge{gmidx-2}"] if gmidx >= 2 else []
            if i == 0:
                wts.append(dec_done[(s, 0)])
                if s == 0:
                    wts.append("xn")
            Sq.add("pe", gmm, waits=tuple(wts), signal=f"gm{gmidx}")
            if i == 7:
                Sq.events[f"lgm{s}"] = Sq.events[f"gm{gmidx}"]
            awts = [f"gm{gmidx}"]
            if i == 0 and s % 2 == 0 and G >= 2:
                awts.append(dmz_ev[G - 2])
            gslot = G % 2
            jg = 2 * (s % 2) + j2
            Sq.add("act", lambda e, gslot=gslot, jg=jg, tb=tb, gmidx=gmidx: e.activation(
                    out=interB[:, gslot, jg, tb*512:(tb+1)*512], in_=bankG[gmidx % 2],
                    func=AF.Gelu_apprx_tanh),
                   waits=tuple(awts), signal=f"ge{gmidx}")
            umidx = CN["um"]; CN["um"] += 1
            assert umidx == gmidx
            def umm(e, s=s, j2=j2, tb=tb, umidx=umidx):
                last = None
                for k in range(KCH):
                    last = e.matmul(bankU[umidx % 2], wuB[:, s % 2, k, j2*128:(j2+1)*128],
                                    xn[:, k, tb*512:(tb+1)*512],
                                    start=(k == 0), stop=(k == KCH - 1))
                return last
            wts = [f"im{umidx-2}"] if umidx >= 2 else []
            if i == 0:
                wts.append(dec_done[(s, 1)])
            Sq.add("pe", umm, waits=tuple(wts), signal=f"um{umidx}")
            if i == 7:
                Sq.events[f"lum{s}"] = Sq.events[f"um{umidx}"]
            pop_sexp(2 if i % 2 == 0 else 1)
            if Gd is not None:
                emit_dmm_cell(Gd, qbase + i // 4, tb)
        pop_sexp(len(pe_sexp))

    for q in range(4):
        for tb in range(4):
            emit_dmm_cell(NGRP - 1, q, tb)

    Sq.barrier("FR")
    ar.off = ar.base
    unpif = sbuf(128, [128], F32)
    xtile = sbuf(128, [KCH, 512], F32)
    atile = sbuf(128, [KCH, 512], F32)
    ftile = sbuf(128, [KCH, 512], F32)
    ostage = sbuf(128, [KCH, 128], F32)
    rXTI = R(); rATI = R(); rFTI = R(); rOST = R()
    rPBr = [R() for i in range(4)]
    attn_r = attn_dram.rearrange("(k p) t -> p k t", p=128)
    xT_r = xT_in.rearrange("(k p) t -> p k t", p=128)

    Sq.add("qsync", lambda e: [e.dma_start(out=unpif, in_=unpi_in)], signal="unpi", n_sig=1)
    for ia in range(NA):
        RA = f"r{ia}"
        def rld(e, ia=ia):
            return [e.dma_start(out=xtile, in_=xT_r[:, :, 3+512*ia:3+512*(ia+1)]),
                    e.dma_start(out=atile, in_=attn_r[:, :, 512*ia:512*(ia+1)]),
                    e.dma_start(out=ftile, in_=acc_r[:, :, 512*ia:512*(ia+1)])]
        Sq.add("qsync", rld,
               waits=tuple(rXTI.wr(RA + "ld")) + tuple(rATI.wr(RA + "ld")) +
                     tuple(rFTI.wr(RA + "ld")),
               signal=RA + "ld", n_sig=3)
        Sq.add("dve", lambda e: e.tensor_tensor(xtile, xtile, atile, ALU.add),
               waits=(RA + "ld",) + tuple(rXTI.wr(RA + "s1")), signal=RA + "s1")
        rATI.rd(RA + "s1")
        Sq.add("dve", lambda e: e.tensor_tensor(xtile, xtile, ftile, ALU.add),
               waits=(RA + "s1",) + tuple(rXTI.wr(RA + "s2")), signal=RA + "s2")
        rFTI.rd(RA + "s2")
        for tt in range(4):
            RT = RA + f"t{tt}"
            for kg in range(4):
                def tmm(e, tt=tt, kg=kg):
                    last = None
                    for j in range(4):
                        kc = 4 * kg + j
                        last = e.transpose(pb[kg][:, j*128:(j+1)*128],
                                           xtile[:, kc, tt*128:(tt+1)*128], unpif)
                    return last
                Sq.add("pe", tmm, waits=(RA + "s2", "unpi") + tuple(rPBr[kg].wr(RT + f"tm{kg}")),
                       signal=RT + f"tm{kg}")
                rXTI.rd(RT + f"tm{kg}")
                Sq.add("dve", lambda e, kg=kg: e.tensor_copy(ostage[:, 4*kg:4*kg+4, :],
                        pb[kg].rearrange("p (a b) -> p a b", b=128)),
                       waits=(RT + f"tm{kg}",) +
                             (tuple(rOST.wr(RT + "ev0")) if kg == 0 else ()) +
                             tuple(rPBr[kg].rd(RT + f"ev{kg}")),
                       signal=RT + f"ev{kg}")
            def rst(e, ia=ia, tt=tt):
                return [e.dma_start(out=out_dram[512*ia+128*tt:512*ia+128*(tt+1), :],
                                    in_=ostage.rearrange("p a b -> p (a b)"))]
            Sq.add("qgps", rst, waits=tuple(RT + f"ev{kg}" for kg in range(4)),
                   signal=RT + "st", n_sig=1)
            rOST.rd(RT + "st")

    sems = {}
    for name in Sq.semnames:
        sems[name] = es.enter_context(nc.semaphore("sem_" + name))
    final_evs = [f"r{ia}t{tt}st" for ia in range(NA) for tt in range(4)]
    with es, nc.Block() as block:
        @block.sync
        def _(sync):
            Sq.emit("qsync", sync, sems)
        @block.gpsimd
        def _(gpsimd):
            Sq.emit("qgps", gpsimd, sems)
            agg = {}
            for evn in final_evs:
                sn, v = Sq.events[evn]
                agg[sn] = max(agg.get(sn, 0), v)
            for sn, v in agg.items():
                gpsimd.wait_ge(sems[sn], v)
        @block.vector
        def _(vector):
            Sq.emit("dve", vector, sems)
        @block.scalar
        def _(scalar):
            Sq.emit("act", scalar, sems)
        @block.tensor
        def _(tensor):
            Sq.emit("pe", tensor, sems)
    return nc


def kernel(**inputs):
    x = np.asarray(inputs["L0_attn_hidden_states"], dtype=np.float32)
    Wz = np.asarray(inputs["L0_attn_Wz"], dtype=np.float32)
    Wx = np.asarray(inputs["L0_attn_Wx"], dtype=np.float32)
    WB = np.asarray(inputs["L0_attn_WB"], dtype=np.float32)
    WC = np.asarray(inputs["L0_attn_WC"], dtype=np.float32)
    Wdt = np.asarray(inputs["L0_attn_Wdt"], dtype=np.float32)
    conv_w = np.asarray(inputs["L0_attn_conv_w"], dtype=np.float32)
    conv_b = np.asarray(inputs["L0_attn_conv_b"], dtype=np.float32)
    dt_bias = np.asarray(inputs["L0_attn_dt_bias"], dtype=np.float32)
    A_log = np.asarray(inputs["L0_attn_A_log"], dtype=np.float32)
    D = np.asarray(inputs["L0_attn_D"], dtype=np.float32)
    norm_w = np.asarray(inputs["L0_attn_norm_w"], dtype=np.float32)
    Wout = np.asarray(inputs["L0_attn_Wout"], dtype=np.float32)
    pg = np.asarray(inputs["L0_ffn_gate_W_packed"], dtype=np.int32)
    sg = np.asarray(inputs["L0_ffn_gate_scales"], dtype=np.float32)
    pu = np.asarray(inputs["L0_ffn_up_W_packed"], dtype=np.int32)
    su = np.asarray(inputs["L0_ffn_up_scales"], dtype=np.float32)
    pd = np.asarray(inputs["L0_ffn_down_W_packed"], dtype=np.int32)
    sd = np.asarray(inputs["L0_ffn_down_scales"], dtype=np.float32)

    pig = pi_full(HID)
    pid = pi_full(NMLP)
    wz = np.ascontiguousarray(Wz[pig])
    wx = np.ascontiguousarray(Wx[pig])
    wb = np.ascontiguousarray(WB[pig])
    wc = np.ascontiguousarray(WC[pig])
    wdt = np.ascontiguousarray(Wdt[pig])
    wout = np.ascontiguousarray(Wout[:, pig])
    convw_h = np.ascontiguousarray(conv_w.reshape(NCC, 128, 4).transpose(1, 0, 2).reshape(128, NCC * 4))
    convb_h = np.ascontiguousarray(conv_b.reshape(NCC, 128).T)
    dtb_h = np.ascontiguousarray(dt_bias[:, None])
    alog_h = np.ascontiguousarray(A_log[:, None])
    drep_h = np.ascontiguousarray(np.repeat(D, HD).reshape(32, 128).T)
    nw_h = np.ascontiguousarray(norm_w.reshape(32, 128).T)
    ident = np.eye(128, dtype=np.float32)
    import ml_dtypes
    mask = np.where(np.arange(128)[:, None] <= np.arange(128)[None, :], 0, -1e30).astype(ml_dtypes.bfloat16)
    unpi = np.zeros((128, 128), np.float32)
    unpi[np.argsort(pi128()), np.arange(128)] = 1.0
    repm = np.zeros((4, 128), np.float32)
    for p in range(128):
        repm[(p % 64) // 16, p] = -0.125
    repm = repm.astype(ml_dtypes.bfloat16)
    shc_h = np.zeros((128, 1), np.int32)
    shc_h[64:] = 4
    pgT = np.ascontiguousarray(pg.T)[:, pid].copy()
    puT = np.ascontiguousarray(pu.T)[:, pid].copy()
    pdT = np.ascontiguousarray(pd.T)[:, pig].copy()
    sgT = np.ascontiguousarray(sg.T)[:, pid].copy()
    suT = np.ascontiguousarray(su.T)[:, pid].copy()
    sdT = np.ascontiguousarray(sd.T)[:, pig].copy()

    shared = {
        "wz_in": wz, "wx_in": wx, "wb_in": wb, "wc_in": wc, "wdt_in": wdt,
        "wout_in": wout, "convw_in": convw_h, "convb_in": convb_h,
        "dtb_in": dtb_h, "alog_in": alog_h, "drep_in": drep_h, "nw_in": nw_h,
        "ident_in": ident, "mask_in": mask, "unpi_in": unpi, "rep_in": repm,
        "shc_in": shc_h,
        "pg_in": pgT, "pu_in": puT, "pd_in": pdT,
        "sg_in": sgT, "su_in": suT, "sd_in": sdT,
    }
    z3 = np.zeros((3, HID), np.float32)
    zprev = np.zeros((HID, T + 3), np.float32)
    in_maps = []
    for c in range(8):
        b, h = c // 2, c % 2
        xb = x[b]
        if h == 0:
            xT_c = np.ascontiguousarray(np.concatenate([z3, xb[0:T]], 0).T[pig])
            xp_c = zprev
            fl = np.zeros((128, 1), np.float32)
        else:
            xT_c = np.ascontiguousarray(np.concatenate([xb[T-3:T], xb[T:2*T]], 0).T[pig])
            xp_c = np.ascontiguousarray(np.concatenate([z3, xb[0:T]], 0).T[pig])
            fl = np.ones((128, 1), np.float32)
        sel = np.zeros((128, 8), np.float32)
        if h == 1:
            sel[:, (c - 1) % 8] = 1.0
        m = dict(shared)
        m.update({"xT_in": xT_c, "xprev_in": xp_c, "flag_in": fl, "selc_in": sel})
        in_maps.append(m)

    nc = build()
    res = run_bass_kernel_spmd(nc, in_maps, core_ids=list(range(8)))
    inv = np.argsort(pig)
    out = np.empty((4, 2 * T, HID), np.float32)
    for c in range(8):
        b, h = c // 2, c % 2
        out[b, h * T:(h + 1) * T, :] = res.results[c]["out"]
    return out
